# version 10
# baseline (speedup 1.0000x reference)
import contextlib
import os
import numpy as np
import ml_dtypes
import concourse.bass as bass
import concourse.mybir as mybir
from concourse.bass_utils import run_bass_kernel_spmd
from concourse.alu_op_type import AluOpType as RALU

F32 = mybir.dt.float32
BF16 = mybir.dt.bfloat16
AX = mybir.AxisListType.X
ALU = mybir.AluOpType
ACT = mybir.ActivationFunctionType
ts = bass.ts

B, N, D = 4, 4096, 2048
T = 2048
NT = T // 128
NTF = N // 128
KD = D // 128
NHL, HD = 8, 128
HL = NHL * HD
KHL = HL // 128

EPS_LN = 1e-5
EPS_RMS = 1e-8
EPS_G = 1e-8
EPS_X = 1e-8
EPS_K = 1e-6
EPS_OG = 1e-6
QB = 127.0
DBG_CHUNKS = int(os.environ.get("KDBG_CHUNKS", "0") or 0)
DBG_PHASE = os.environ.get("KDBG_PHASE", "")
DBG_SCANPART = int(os.environ.get("KDBG_SCANPART", "9") or 9)
DBG_PA = int(os.environ.get("KDBG_PA", "9") or 9)

ENGINES = ("sync", "gpsimd", "vector", "scalar", "tensor")


class Sem:
    def __init__(self, h, name):
        self.h = h
        self.name = name
        self.val = 0

    def inc(self, n=1):
        self.val += n
        return self.val


class Phase:

    def __init__(self, name):
        self.name = name
        self.ops = {e: [] for e in ENGINES}

    def op(self, eng, fn, sem=None, n=1):
        if sem is None:
            self.ops[eng].append(fn)
            return None
        v = sem.inc(n)
        self.ops[eng].append(lambda h, fn=fn, sem=sem, n=n: fn(h).then_inc(sem.h, n))
        return v

    def wait(self, eng, sem, val):
        if val:
            self.ops[eng].append(lambda h, s=sem, v=val: h.wait_ge(s.h, v))

    def drain(self, eng):
        self.ops[eng].append(lambda h: h.drain())

    def emit(self, blk):
        for e in ENGINES:
            lst = self.ops[e]
            if not lst:
                continue

            def body(h, lst=lst):
                for f in lst:
                    f(h)
            getattr(blk, e)(body)


class Dma:

    def __init__(self, sems, queue_eng):
        self.sems = sems
        self.uses = [0] * len(sems)
        self.i = 0
        self.eng = queue_eng

    def issue(self, ph, out_ap, in_ap, transpose=False, pre_wait=None, slow_ok=False):
        k = self.i % len(self.sems)
        self.i += 1
        s = self.sems[k]
        if self.uses[k] > 0:
            ph.wait(self.eng, s, 16 * self.uses[k])
        self.uses[k] += 1
        if pre_wait:
            for (ws, wv) in pre_wait:
                ph.wait(self.eng, ws, wv)
        v = s.inc(16)
        if transpose:
            ph.ops[self.eng].append(
                lambda h, o=out_ap, i_=in_ap, s=s: h.dma_start_transpose(o, i_).then_inc(s.h, 16))
        else:
            ph.ops[self.eng].append(
                lambda h, o=out_ap, i_=in_ap, s=s, sl=slow_ok: h.dma_start(
                    out=o, in_=i_, allow_slow_non_contiguous=sl).then_inc(s.h, 16))
        return (s, v)


def build_program():
    nc = bass.Bass("TRN2", target_bir_lowering=False,
                   detect_race_conditions=(os.environ.get("KDBG_RACE", "0") == "1"))

    x_in = nc.dram_tensor("x_in", [T, D], F32, kind="ExternalInput").ap()
    wq_t = nc.dram_tensor("wq_t", [D, HL], F32, kind="ExternalInput").ap()
    wk_t = nc.dram_tensor("wk_t", [D, HL], F32, kind="ExternalInput").ap()
    wv_t = nc.dram_tensor("wv_t", [D, HL], F32, kind="ExternalInput").ap()
    wbeta_t = nc.dram_tensor("wbeta_t", [D, NHL], F32, kind="ExternalInput").ap()
    wg1_t = nc.dram_tensor("wg1_t", [D, 128], F32, kind="ExternalInput").ap()
    wg2_t = nc.dram_tensor("wg2_t", [128, HL], F32, kind="ExternalInput").ap()
    wo_t = nc.dram_tensor("wo_t", [HL, D], F32, kind="ExternalInput").ap()
    gnorm_c = nc.dram_tensor("gnorm_c", [HL], F32, kind="ExternalInput").ap()
    wffn = nc.dram_tensor("wffn", [D, D], F32, kind="ExternalInput").ap()
    fbias = nc.dram_tensor("fbias", [D], F32, kind="ExternalInput").ap()
    eye_bfd = nc.dram_tensor("eye_bfd", [128, 128], BF16, kind="ExternalInput").ap()
    eye_fd = nc.dram_tensor("eye_fd", [128, 128], F32, kind="ExternalInput").ap()
    msl_d = nc.dram_tensor("msl_d", [128, 128], F32, kind="ExternalInput").ap()
    msu_d = nc.dram_tensor("msu_d", [128, 128], F32, kind="ExternalInput").ap()
    mtui_d = nc.dram_tensor("mtui_d", [128, 128], F32, kind="ExternalInput").ap()
    mbou_d = nc.dram_tensor("mbou_d", [128, 128], F32, kind="ExternalInput").ap()

    x_out = nc.dram_tensor("x_out", [T, D], F32, kind="ExternalOutput").ap()

    wffn_tern = nc.dram_tensor("wffn_tern", [D, D], BF16, kind="Internal").ap()
    ag_out = nc.dram_tensor("ag_out", [N, D], BF16, kind="Internal", addr_space="Shared").ap()
    qkv_scan = nc.dram_tensor("qkv_scan", [NTF, 128, NHL, 3, 128], BF16, kind="Internal").ap()
    gate_dram = nc.dram_tensor("gate_dram", [NTF, 128, HL], BF16, kind="Internal").ap()
    rs_sh = nc.dram_tensor("rs_sh", [2, N, D], F32, kind="Internal", addr_space="Shared").ap()
    flg = nc.dram_tensor("flg", [2, 128], mybir.dt.int32, kind="Internal", addr_space="Shared").ap()
    flagv = nc.dram_tensor("flagv", [2, 128], mybir.dt.int32, kind="ExternalInput").ap()

    st = contextlib.ExitStack()

    def mksem(name):
        return Sem(st.enter_context(nc.semaphore(name)), name)

    sT = mksem("sT")
    sV = mksem("sV")
    sS = mksem("sS")
    sGc = mksem("sGc")
    sC = mksem("sC")
    dmaS = Dma([mksem(f"dS{i}") for i in range(8)], "sync")
    dmaG = Dma([mksem(f"dG{i}") for i in range(8)], "gpsimd")
    ENG_SEMS = [sT, sV, sS, sGc]

    def sbuf(name, shape, dt=F32, scope=None):
        return (scope or st).enter_context(nc.sbuf_tensor(name, shape, dt))

    def psum(name, shape, dt=F32, scope=None):
        return (scope or st).enter_context(nc.psum_tensor(name, shape, dt))

    xr = sbuf("xr", [128, NT * D], BF16)
    stats = sbuf("stats", [128, 16 * 14], F32)
    bias_rep = sbuf("bias_rep", [128, D], BF16)
    sgam = sbuf("sgam", [128, KD], F32)
    eyeb = sbuf("eyeb", [128, 128], BF16)
    eyef = sbuf("eyef", [128, 128], F32)
    m_sl = sbuf("m_sl", [128, 128], F32)
    m_su = sbuf("m_su", [128, 128], F32)
    m_tui = sbuf("m_tui", [128, 128], F32)
    m_bou = sbuf("m_bou", [128, 128], F32)
    onesr = sbuf("onesr", [1, 128], BF16)
    fbias_sb = sbuf("fbias_sb", [1, D], BF16)
    gnorm_sb = sbuf("gnorm_sb", [128, KHL], F32)
    beta_tm = sbuf("beta_tm", [128, NTF * NHL], F32)
    flv_sb = st.enter_context(nc.sbuf_tensor("flv_sb", [2, 128], mybir.dt.int32))
    fl_sb = st.enter_context(nc.sbuf_tensor("fl_sb", [1, 128], mybir.dt.int32))

    (Q_SUMX, Q_SUMSQ, Q_MX, Q_MN, Q_M, Q_V, Q_SC, Q_KQ, Q_KB, Q_XS,
     Q_T0, Q_T1, Q_T2, Q_T3) = range(14)

    def stat(q, i=None):
        if i is None:
            return stats[:, q * 16: q * 16 + 16]
        return stats[:, q * 16 + i: q * 16 + i + 1]

    phases = []
    snaps = {}

    def snap(tag):
        snaps[tag] = ([(s, s.val) for s in ENG_SEMS + [sC]],
                      [(s, 16 * u) for s, u in zip(dmaS.sems + dmaG.sems, dmaS.uses + dmaG.uses)])

    def barrier(ph, tag):
        engs, dmas = snaps[tag]
        for e in ENGINES:
            for s, v in engs + dmas:
                if v:
                    ph.wait(e, s, v)

    blk = st.enter_context(nc.Block())

    PREG = {}

    def parity_setup(h, eng):
        if eng in PREG:
            return PREG[eng]
        pid = h.to_reg(h.partition_id())
        par = h.alloc_register()
        h.reg_alu(par, pid, 1, RALU.bitwise_and)
        PREG[eng] = par
        return par

    def op_pair(ph, eng, fn_even, fn_odd, sem=None, n=1):
        v = sem.inc(n) if sem is not None else None

        def closure(h, fe=fn_even, fo=fn_odd, eng=eng, sem=sem, n=n):
            par = parity_setup(h, eng)
            with h.If_eq(par, 0):
                ins = fe(h)
                if sem is not None:
                    ins.then_inc(sem.h, n)
            with h.Else():
                ins = fo(h)
                if sem is not None:
                    ins.then_inc(sem.h, n)
        ph.ops[eng].append(closure)
        return v

    poll_sems = [mksem(f"poll{i}") for i in range(2)]

    def emit_barrier(ph, k, pre_dma_sems):
        for (sm, val) in pre_dma_sems:
            ph.wait("gpsimd", sm, val)
        fw = poll_sems[k]

        def closure_all(h, k=k, fw=fw):
            par = parity_setup(h, "gpsimd")
            with h.If_eq(par, 0):
                h.dma_start(out=flg[0:1, :], in_=flv_sb[k:k + 1, :]).then_inc(fw.h, 16)
            with h.Else():
                h.dma_start(out=flg[1:2, :], in_=flv_sb[k:k + 1, :]).then_inc(fw.h, 16)
            h.wait_ge(fw.h, 16)
            with h.register(f"cnt{k}") as cnt, h.register(f"cc{k}") as cc, \
                 h.register(f"vv{k}") as vv, h.register(f"mgk{k}") as mgk:
                h.load(mgk, flv_sb[k:k + 1, 0:1])
                h.reg_mov(cnt, 16)
                h.reg_mov(cc, 1)
                with h.While(cc):
                    h.reg_add(cnt, cnt, 16)
                    with h.If_eq(par, 0):
                        h.dma_start(out=fl_sb[:], in_=flg[1:2, :]).then_inc(fw.h, 16)
                    with h.Else():
                        h.dma_start(out=fl_sb[:], in_=flg[0:1, :]).then_inc(fw.h, 16)
                    h.wait_ge(fw.h, cnt)
                    h.load(vv, fl_sb[0:1, 0:1])
                    h.reg_alu(cc, vv, mgk, RALU.subtract)
        ph.ops["gpsimd"].append(closure_all)
        return nop_inc(ph, "gpsimd", sGc)

    def nop_inc(ph, eng, sem):
        return ph.op(eng, lambda h: h.nop(), sem=sem)

    def rsqrt_chain(ph, src_ap, out_ap, t1, t2, t3, eps_add=None):
        if eps_add is not None:
            ph.op("vector", lambda h, a=src_ap, o=t1, e=eps_add: h.tensor_scalar(o, a, e, None, ALU.add))
            ph.drain("vector")
            base = t1
        else:
            base = src_ap
        vv = nop_inc(ph, "vector", sV)
        ph.wait("scalar", sV, vv)
        sv = ph.op("scalar", lambda h, a=base, o=t2: h.activation(o, a, ACT.Sqrt), sem=sS)
        ph.wait("vector", sS, sv)
        ph.op("vector", lambda h, o=t3, a=t2: h.reciprocal(o, a))
        ph.drain("vector")
        ph.op("vector", lambda h, o=t3, a=base, b=t3: h.tensor_mul(o, a, b))
        ph.drain("vector")
        ph.op("vector", lambda h, o=t3, a=t3, b=t2: h.tensor_add(o, a, b))
        ph.drain("vector")
        ph.op("vector", lambda h, o=t3, a=t3: h.tensor_scalar(o, a, 0.5, None, ALU.mult))
        ph.drain("vector")
        ph.op("vector", lambda h, o=out_ap, a=t3: h.reciprocal(o, a))
        ph.drain("vector")


    def dbg_finish():
        pD = Phase("pDbg")
        for se, u in zip(dmaS.sems, dmaS.uses):
            if u:
                pD.wait("sync", se, 16 * u)
        for se, u in zip(dmaG.sems, dmaG.uses):
            if u:
                pD.wait("gpsimd", se, 16 * u)
        pD.emit(blk)
        st.close()
        return nc

    p0s = contextlib.ExitStack()
    wkA = sbuf("wkA", [128, D], F32, p0s)
    wkB = sbuf("wkB", [128, D], F32, p0s)
    fA = sbuf("fA", [128, D], F32, p0s)
    fB = sbuf("fB", [128, D], F32, p0s)
    stg0 = sbuf("stg0", [128, D], BF16, p0s)
    stg1 = sbuf("stg1", [128, D], BF16, p0s)
    pbc = psum("pbc", [128, 512], F32, p0s)

    p0 = Phase("p0")
    cvals = []
    for sb_t, dr in ((eyeb, eye_bfd), (eyef, eye_fd), (m_sl, msl_d), (m_su, msu_d), (m_tui, mtui_d), (m_bou, mbou_d)):
        cvals.append(dmaS.issue(p0, sb_t[:], dr[:]))
    cvals.append(dmaS.issue(p0, gnorm_sb[:, 0:KHL],
                            gnorm_c.rearrange("(a p) -> p a", p=128), slow_ok=True))
    dmaG.issue(p0, xr.ap().rearrange("p (t d) -> p t d", t=NT),
               x_in.rearrange("(t p) d -> p t d", p=128))
    fbl = dmaG.issue(p0, fbias_sb[:], fbias[None, :])
    dmaS.issue(p0, flv_sb[:], flagv[:])

    v_ones = p0.op("vector", lambda h: h.memset(onesr[:], 1.0), sem=sV)

    p0.wait("tensor", fbl[0], fbl[1])
    p0.wait("tensor", sV, v_ones)
    for nb in range(4):
        if nb >= 1:
            p0.wait("tensor", sV, sV.val)
        tb = p0.op("tensor", lambda h, nb=nb: h.matmul(
            pbc[:, 0:512], onesr[:], fbias_sb[:, ts(nb, 512)], start=True, stop=True), sem=sT)
        p0.wait("vector", sT, tb)
        p0.op("vector", lambda h, nb=nb: h.tensor_scalar(
            bias_rep[:, ts(nb, 512)], pbc[:, 0:512], 0.5, None, ALU.mult), sem=sV)

    for (s, v) in cvals:
        p0.wait("vector", s, v)
    v_tern = []
    s_cast = []
    tern_store = []
    for i in range(KD):
        src = wkA if i % 2 == 0 else wkB
        dst = stg0 if i % 2 == 0 else stg1
        ld = dmaG.issue(p0, src[:], wffn[ts(i, 128), :],
                        pre_wait=([(sV, v_tern[i - 2])] if i >= 2 else None))
        p0.wait("vector", ld[0], ld[1])
        p0.op("vector", lambda h, s_=src, i=i: h.tensor_reduce(
            stat(Q_T0, i), s_[:], AX, ALU.add, apply_absolute_value=True))
        p0.drain("vector")
        p0.op("vector", lambda h, i=i: h.tensor_scalar(
            stat(Q_T1, i), stat(Q_T0, i), 1.0 / D, EPS_G, ALU.mult, ALU.add))
        p0.drain("vector")
        p0.op("vector", lambda h, i=i: h.reciprocal(stat(Q_T2, i), stat(Q_T1, i)))
        p0.op("vector", lambda h, i=i: h.tensor_scalar(
            sgam[:, i:i + 1], stat(Q_T0, i), 0.5 / D, None, ALU.mult))
        p0.drain("vector")
        if i >= 1:
            p0.wait("vector", sS, s_cast[i - 1])
        p0.op("vector", lambda h, s_=src, i=i: h.tensor_scalar(
            fA[:], s_[:], stat(Q_T2, i), 0.5, ALU.mult, ALU.is_gt))
        p0.op("vector", lambda h, s_=src, i=i: h.tensor_scalar(
            fB[:], s_[:], stat(Q_T2, i), -0.5, ALU.mult, ALU.is_lt))
        p0.drain("vector")
        v_tern.append(p0.op("vector", lambda h: h.tensor_sub(fA[:], fA[:], fB[:]), sem=sV))
        p0.wait("scalar", sV, v_tern[i])
        if i >= 2:
            p0.wait("scalar", tern_store[i - 2][0], tern_store[i - 2][1])
        s_cast.append(p0.op("scalar", lambda h, d=dst, i=i: h.activation(
            d[:], fA[:], ACT.Copy, scale=sgam[:, i:i + 1]), sem=sS))
        tern_store.append(dmaS.issue(p0, wffn_tern[ts(i, 128), :], dst[:],
                                     pre_wait=[(sS, s_cast[i])]))

    p0.emit(blk); p0s.close()
    snap("p0_end")
    if DBG_PHASE == "p0":
        return dbg_finish()

    def emit_ffn(tag, write_out, after_tag):
        ps_ = contextlib.ExitStack()
        tbuf = sbuf(f"tbuf_{tag}", [128, D], F32, ps_)
        fbuf = sbuf(f"fbuf_{tag}", [128, D], F32, ps_)
        cbuf = sbuf(f"cbuf_{tag}", [128, D], F32, ps_)
        aibuf = ps_.enter_context(nc.sbuf_tensor(f"ai_{tag}", [128, D], mybir.dt.int32))
        xqs = sbuf(f"xqs_{tag}", [128, D], BF16, ps_)
        xqT = sbuf(f"xqT_{tag}", [128, (NT // 2) * D], BF16, ps_)
        weffh = sbuf(f"weffh_{tag}", [128, KD * 1024], BF16, ps_)
        fstb = [sbuf(f"fst{i}_{tag}", [128, 512], F32, ps_) for i in range(2)]
        pM = [psum(f"pM{i}_{tag}", [128, 512], F32, ps_) for i in range(2)]
        pX = psum(f"pX_{tag}", [128, 512], BF16, ps_)

        ph = Phase(tag)
        barrier(ph, after_tag)

        for i in range(NT):
            xa = xr[:, ts(i, D)]
            ph.op("vector", lambda h, a=xa, i=i: h.tensor_reduce(stat(Q_SUMX, i), a, AX, ALU.add))
            ph.op("vector", lambda h, a=xa, i=i: h.tensor_reduce(stat(Q_MX, i), a, AX, ALU.max))
            ph.op("vector", lambda h, a=xa, i=i: h.tensor_reduce(
                stat(Q_MN, i), a, AX, ALU.min) if hasattr(h, "tensor_reduce") else None)
            ph.op("scalar", lambda h, a=xa, i=i: h.activation(
                fbuf[:], a, ACT.Square, accum_out=stat(Q_SUMSQ, i)))
            ph.drain("scalar")
        ph.drain("vector")
        v_st = nop_inc(ph, "vector", sV)
        s_st = nop_inc(ph, "scalar", sS)

        ph.wait("vector", sS, s_st)
        ph.op("vector", lambda h: h.tensor_scalar(stat(Q_M), stat(Q_SUMX), 1.0 / D, None, ALU.mult))
        ph.drain("vector")
        ph.op("vector", lambda h: h.tensor_mul(stat(Q_T0), stat(Q_M), stat(Q_M)))
        ph.op("vector", lambda h: h.tensor_scalar(stat(Q_T1), stat(Q_SUMSQ), 1.0 / D, None, ALU.mult))
        ph.drain("vector")
        ph.op("vector", lambda h: h.tensor_sub(stat(Q_V), stat(Q_T1), stat(Q_T0)))
        ph.drain("vector")
        ph.op("vector", lambda h: h.tensor_scalar(stat(Q_T0), stat(Q_V), 1.0 + EPS_RMS,
                                                  EPS_RMS * EPS_LN, ALU.mult, ALU.add))
        ph.drain("vector")
        rsqrt_chain(ph, stat(Q_T0), stat(Q_SC), stat(Q_T1), stat(Q_T2), stat(Q_T3))
        ph.op("vector", lambda h: h.tensor_sub(stat(Q_T0), stat(Q_MX), stat(Q_M)))
        ph.op("vector", lambda h: h.tensor_sub(stat(Q_T1), stat(Q_M), stat(Q_MN)))
        ph.drain("vector")
        ph.op("vector", lambda h: h.tensor_max(stat(Q_T0), stat(Q_T0), stat(Q_T1)))
        ph.drain("vector")
        ph.op("vector", lambda h: h.tensor_mul(stat(Q_T0), stat(Q_T0), stat(Q_SC)))
        ph.drain("vector")
        ph.op("vector", lambda h: h.tensor_scalar(stat(Q_XS), stat(Q_T0), 1.0 / QB, None, ALU.mult))
        ph.drain("vector")
        ph.op("vector", lambda h: h.tensor_scalar(stat(Q_T1), stat(Q_XS), EPS_X, None, ALU.add))
        ph.drain("vector")
        ph.op("vector", lambda h: h.reciprocal(stat(Q_T2), stat(Q_T1)))
        ph.drain("vector")
        ph.op("vector", lambda h: h.tensor_mul(stat(Q_KQ), stat(Q_SC), stat(Q_T2)))
        ph.drain("vector")
        ph.op("vector", lambda h: h.tensor_mul(stat(Q_KB), stat(Q_M), stat(Q_KQ)))
        ph.drain("vector")
        ph.op("vector", lambda h: h.tensor_scalar(stat(Q_KB), stat(Q_KB), -1.0, 256.0, ALU.mult, ALU.add))
        ph.drain("vector")
        v_m2 = nop_inc(ph, "vector", sV)
        ph.wait("scalar", sV, v_m2)
        ph.wait("gpsimd", sV, v_m2)

        ev_last = [None]
        mm_prev = [None]
        ep_hist = []
        st_hist = {0: None, 1: None}
        prev_vq = [None]
        prev_tl = [None]
        for th in range(2):
            gb = []
            s_q = []
            v_q = []
            for ii in range(NT // 2):
                i = th * (NT // 2) + ii
                xa = xr[:, ts(i, D)]
                if prev_vq[0] is not None:
                    ph.wait("scalar", sV, prev_vq[0])
                sq = ph.op("scalar", lambda h, a=xa, i=i: h.activation(
                    tbuf[:], a, ACT.Identity, scale=stat(Q_KQ, i), bias=stat(Q_KB, i)), sem=sS)
                s_q.append(sq)
                ph.wait("vector", sS, sq)
                ph.op("vector", lambda h: h.tensor_copy(aibuf[:], tbuf[:]), sem=sV)
                ph.drain("vector")
                ph.op("vector", lambda h: h.tensor_copy(fbuf[:], aibuf[:]), sem=sV)
                ph.drain("vector")
                ph.op("vector", lambda h: h.scalar_tensor_tensor(
                    cbuf[:], tbuf[:], -0.5, fbuf[:], ALU.add, ALU.is_ge), sem=sV)
                ph.drain("vector")
                if prev_tl[0] is not None:
                    ph.wait("vector", sT, prev_tl[0])
                vq = ph.op("vector", lambda h: h.scalar_tensor_tensor(
                    xqs[:], fbuf[:], -256.0, cbuf[:], ALU.add, ALU.add), sem=sV)
                v_q.append(vq)
                prev_vq[0] = vq
                ph.wait("tensor", sV, vq)
                for kb in range(4):
                    if ev_last[0] is not None:
                        ph.wait("tensor", sV, ev_last[0])
                    for k in range(4):
                        kt = kb * 4 + k
                        t_last = ph.op("tensor", lambda h, k=k, kt=kt: h.transpose(
                            pX[:, ts(k, 128)], xqs[:, ts(kt, 128)], eyeb[:]), sem=sT)
                    prev_tl[0] = t_last
                    ph.wait("vector", sT, t_last)
                    dst = xqT.ap().rearrange("p (k t f) -> p k t f", k=KD, t=NT // 2)[:, ts(kb, 4), ii, :]
                    ev_last[0] = ph.op("vector", lambda h, d=dst: h.tensor_copy(
                        d, pX.ap().rearrange("p (k f) -> p k f", k=4)), sem=sV)
            for ii in range(NT // 2):
                i = th * (NT // 2) + ii
                xa = xr[:, ts(i, D)]
                ph.wait("vector", sS, s_q[ii])
                gb.append(ph.op("vector", lambda h, a=xa: h.tensor_add(
                    a, a, bias_rep[:]), sem=sV))
                ph.drain("vector")

            for oh in range(2):
                wl = []
                for kt in range(KD):
                    pw = []
                    if th == 0 and oh == 0 and kt < len(tern_store):
                        pw.append((tern_store[kt][0], tern_store[kt][1]))
                    if mm_prev[0] is not None and kt == 0:
                        pw.append((sT, mm_prev[0]))
                    wl.append(dmaS.issue(
                        ph, weffh.ap().rearrange("p (k f) -> p k f", k=KD)[:, kt, :],
                        wffn_tern[oh * 1024:(oh + 1) * 1024, ts(kt, 128)],
                        transpose=True, pre_wait=pw or None))
                for (s, v) in wl:
                    ph.wait("tensor", s, v)
                ph.wait("tensor", sV, ev_last[0])
                for ii in range(NT // 2):
                    i = th * (NT // 2) + ii
                    for nb in range(2):
                        pidx = (2 * ii + nb) % 2
                        if len(ep_hist) >= 2:
                            ph.wait("tensor", sV, ep_hist[-2])
                        for kt in range(KD):
                            lhs = xqT.ap().rearrange("p (k t f) -> p k t f", k=KD, t=NT // 2)[:, kt, ii, :]
                            mmv = ph.op("tensor", lambda h, lhs=lhs, kt=kt, nb=nb, pidx=pidx: h.matmul(
                                pM[pidx][:], lhs,
                                weffh.ap().rearrange("p (k f) -> p k f", k=KD)[:, kt, ts(nb, 512)],
                                start=(kt == 0), stop=(kt == KD - 1)), sem=sT)
                        ph.wait("vector", sT, mmv)
                        col0 = i * D + oh * 1024 + nb * 512
                        dst = xr[:, col0: col0 + 512]
                        if write_out:
                            if st_hist[pidx] is not None:
                                ph.wait("vector", st_hist[pidx][0], st_hist[pidx][1])
                            ev = ph.op("vector", lambda h, d=dst, i=i, pidx=pidx: h.scalar_tensor_tensor(
                                fstb[pidx][:], pM[pidx][:], stat(Q_XS, i), d, ALU.mult, ALU.add), sem=sV)
                            st_hist[pidx] = dmaS.issue(
                                ph, x_out[ts(i, 128), oh * 1024 + nb * 512: oh * 1024 + (nb + 1) * 512],
                                fstb[pidx][:], pre_wait=[(sV, ev)])
                        else:
                            ev = ph.op("vector", lambda h, d=dst, i=i, pidx=pidx: h.scalar_tensor_tensor(
                                d, pM[pidx][:], stat(Q_XS, i), d, ALU.mult, ALU.add), sem=sV)
                        ep_hist.append(ev)
                mm_prev[0] = sT.val
        ph.emit(blk); ps_.close()

    emit_ffn("p1", False, "p0_end")
    snap("p1_end")
    if DBG_PHASE == "p1":
        return dbg_finish()

    p2s = contextlib.ExitStack()
    sq2 = sbuf("sq2", [128, D], F32, p2s)
    xn2 = [sbuf(f"xn2_{i}", [128, D], BF16, p2s) for i in range(2)]
    p2 = Phase("p2")
    barrier(p2, "p1_end")
    for i in range(NT):
        xa = xr[:, ts(i, D)]
        p2.op("vector", lambda h, a=xa, i=i: h.tensor_reduce(stat(Q_SUMX, i), a, AX, ALU.add))
        p2.op("scalar", lambda h, a=xa, i=i: h.activation(sq2[:], a, ACT.Square, accum_out=stat(Q_SUMSQ, i)))
        p2.drain("scalar")
    p2.drain("vector")
    v_st2 = nop_inc(p2, "vector", sV)
    s_st2 = nop_inc(p2, "scalar", sS)
    p2.wait("vector", sS, s_st2)
    p2.op("vector", lambda h: h.tensor_scalar(stat(Q_M), stat(Q_SUMX), 1.0 / D, None, ALU.mult))
    p2.drain("vector")
    p2.op("vector", lambda h: h.tensor_mul(stat(Q_T0), stat(Q_M), stat(Q_M)))
    p2.op("vector", lambda h: h.tensor_scalar(stat(Q_T1), stat(Q_SUMSQ), 1.0 / D, None, ALU.mult))
    p2.drain("vector")
    p2.op("vector", lambda h: h.tensor_sub(stat(Q_V), stat(Q_T1), stat(Q_T0)))
    p2.drain("vector")
    rsqrt_chain(p2, stat(Q_V), stat(Q_SC), stat(Q_T1), stat(Q_T2), stat(Q_T3), eps_add=EPS_LN)
    p2.op("vector", lambda h: h.tensor_mul(stat(Q_KB), stat(Q_M), stat(Q_SC)))
    p2.drain("vector")
    p2.op("vector", lambda h: h.tensor_scalar(stat(Q_KB), stat(Q_KB), -1.0, None, ALU.mult))
    p2.drain("vector")
    v_m3 = nop_inc(p2, "vector", sV)
    p2.wait("scalar", sV, v_m3)
    ag_st = []
    for i in range(NT):
        xa = xr[:, ts(i, D)]
        if i >= 2:
            p2.wait("scalar", ag_st[i - 2][0], ag_st[i - 2][1])
        sc = p2.op("scalar", lambda h, a=xa, i=i: h.activation(
            xn2[i % 2][:], a, ACT.Identity, scale=stat(Q_SC, i), bias=stat(Q_KB, i)), sem=sS)
        k = dmaS.i % len(dmaS.sems)
        dmaS.i += 1
        sring = dmaS.sems[k]
        if dmaS.uses[k] > 0:
            p2.wait("sync", sring, 16 * dmaS.uses[k])
        dmaS.uses[k] += 1
        p2.wait("sync", sS, sc)
        op_pair(p2, "sync",
                lambda h, i=i, b=xn2[i % 2]: h.dma_start(out=ag_out[ts(i, 128), :], in_=b[:]),
                lambda h, i=i, b=xn2[i % 2]: h.dma_start(out=ag_out[T + i * 128: T + (i + 1) * 128, :], in_=b[:]),
                sem=sring, n=16)
        ag_st.append((sring, sring.val))
    emit_barrier(p2, 0, ag_st)
    p2.emit(blk); p2s.close()
    snap("p2_end")
    if DBG_PHASE == "p2":
        return dbg_finish()

    p3s = contextlib.ExitStack()
    xnT = sbuf("xnT", [128, KD * 16 * 128], BF16, p3s)
    xg = [sbuf(f"xg{i}", [128, D], BF16, p3s) for i in range(2)]
    wst = [sbuf(f"wst{i}", [128, KD * 128], BF16, p3s) for i in range(2)]
    wg2s = sbuf("wg2s", [128, HL], BF16, p3s)
    g1T = sbuf("g1T", [128, N], BF16, p3s)
    beta_f = sbuf("beta_f", [NHL, N], F32, p3s)
    stq = [sbuf(f"stq{i}", [128, 512], BF16, p3s) for i in range(2)]
    pP = [psum(f"pP{i}", [128, 512], F32, p3s) for i in range(2)]
    pXt = psum("pXt", [128, 512], BF16, p3s)
    pBt = psum("pBt", [128, 128], F32, p3s)

    p3 = Phase("p3")
    barrier(p3, "p2_end")

    ldg2 = dmaG.issue(p3, wg2s[:], wg2_t[:])

    def xnT_blk(kt, n0, n1):
        return xnT.ap().rearrange("p (k f) -> p k f", k=KD)[:, kt, n0:n1]

    wsrc = [("q", wq_t, 0), ("k", wk_t, 1), ("v", wv_t, 2)]
    ev3 = [None]
    xg_tr_done = {0: None, 1: None}
    mm3_done = {}
    wst_free = {0: None, 1: None}
    pP_free = {0: None, 1: None}
    stq_free = {0: None, 1: None}

    for sh in range(2):
        for mt in range(16):
            gmt = sh * 16 + mt
            pw = []
            if mt == 0 and sh == 1:
                v = mm3_done.get((sh - 1, "last"))
                if v:
                    pw.append((sT, v))
            if xg_tr_done[mt % 2] is not None:
                pw.append((sT, xg_tr_done[mt % 2]))
            ld = dmaS.issue(p3, xg[mt % 2][:], ag_out[ts(gmt, 128), :], pre_wait=pw or None)
            p3.wait("tensor", ld[0], ld[1])
            for kb in range(4):
                if ev3[0] is not None:
                    p3.wait("tensor", sV, ev3[0])
                for k in range(4):
                    kt = kb * 4 + k
                    tv = p3.op("tensor", lambda h, k=k, kt=kt, mt=mt: h.transpose(
                        pXt[:, ts(k, 128)], xg[mt % 2][:, ts(kt, 128)], eyeb[:]), sem=sT)
                p3.wait("vector", sT, tv)
                dst = xnT.ap().rearrange("p (k t f) -> p k t f", k=KD, t=16)[:, ts(kb, 4), mt, :]
                ev3[0] = p3.op("vector", lambda h, d=dst: h.tensor_copy(
                    d, pXt.ap().rearrange("p (k f) -> p k f", k=4)), sem=sV)
            xg_tr_done[mt % 2] = tv

        for wi, (wname, wdr, widx) in enumerate(wsrc):
            for mh in range(NHL):
                mb = wi * NHL + mh
                pwv = wst_free[mb % 2]
                ld = dmaG.issue(p3, wst[mb % 2].ap().rearrange("p (k f) -> p k f", k=KD),
                                wdr.rearrange("(k p) f -> p k f", p=128)[:, :, ts(mh, 128)],
                                pre_wait=([(sT, pwv)] if pwv else None))
                p3.wait("tensor", ld[0], ld[1])
                if mb == 0:
                    p3.wait("tensor", sV, ev3[0])
                for nb in range(4):
                    par = nb % 2
                    if pP_free[par] is not None:
                        p3.wait("tensor", pP_free[par][0], pP_free[par][1])
                    for kt in range(KD):
                        mmv = p3.op("tensor", lambda h, kt=kt, nb=nb, mb=mb, par=par: h.matmul(
                            pP[par][:], wst[mb % 2][:, ts(kt, 128)],
                            xnT_blk(kt, nb * 512, (nb + 1) * 512),
                            start=(kt == 0), stop=(kt == KD - 1)), sem=sT)
                    eng, sem = ("scalar", sS) if wname in ("q", "k") else ("vector", sV)
                    p3.wait(eng, sT, mmv)
                    if stq_free[par] is not None:
                        p3.wait(eng, stq_free[par][0], stq_free[par][1])
                    if wname in ("q", "k"):
                        evv = p3.op(eng, lambda h, par=par: h.activation(
                            stq[par][:], pP[par][:], ACT.Silu), sem=sem)
                    else:
                        evv = p3.op(eng, lambda h, par=par: h.tensor_copy(stq[par][:], pP[par][:]), sem=sem)
                    pP_free[par] = (sem, evv)
                    c0 = sh * 16 + nb * 4
                    stv = dmaS.issue(p3, qkv_scan[c0:c0 + 4, :, mh, widx, :].rearrange("c p f -> p c f"),
                                     stq[par].ap().rearrange("p (c f) -> p c f", c=4),
                                     pre_wait=[(sem, evv)])
                    stq_free[par] = stv
                mm3_done[(sh, mb)] = sT.val
                wst_free[mb % 2] = sT.val
        mm3_done[(sh, "last")] = sT.val

        ldb = dmaG.issue(p3, wst[0].ap().rearrange("p (k f) -> p k f", k=KD)[:, :, 0:NHL],
                         wbeta_t.rearrange("(k p) f -> p k f", p=128),
                         pre_wait=[(sT, wst_free[0])])
        p3.wait("tensor", ldb[0], ldb[1])
        for nb in range(4):
            par = nb % 2
            if pP_free[par] is not None:
                p3.wait("tensor", pP_free[par][0], pP_free[par][1])
            for kt in range(KD):
                mmv = p3.op("tensor", lambda h, kt=kt, nb=nb, par=par: h.matmul(
                    pP[par][0:NHL, :], wst[0][:, kt * 128: kt * 128 + NHL],
                    xnT_blk(kt, nb * 512, (nb + 1) * 512),
                    start=(kt == 0), stop=(kt == KD - 1)), sem=sT)
            p3.wait("scalar", sT, mmv)
            scv = p3.op("scalar", lambda h, nb=nb, sh=sh, par=par: h.activation(
                beta_f[:, sh * T + nb * 512: sh * T + (nb + 1) * 512],
                pP[par][0:NHL, :], ACT.Sigmoid), sem=sS)
            pP_free[par] = (sS, scv)
        wst_free[0] = sT.val

        ldg1 = dmaG.issue(p3, wst[1].ap().rearrange("p (k f) -> p k f", k=KD),
                          wg1_t.rearrange("(k p) f -> p k f", p=128),
                          pre_wait=[(sT, wst_free[1])])
        p3.wait("tensor", ldg1[0], ldg1[1])
        for nb in range(4):
            par = nb % 2
            if pP_free[par] is not None:
                p3.wait("tensor", pP_free[par][0], pP_free[par][1])
            for kt in range(KD):
                mmv = p3.op("tensor", lambda h, kt=kt, nb=nb, par=par: h.matmul(
                    pP[par][:], wst[1][:, ts(kt, 128)],
                    xnT_blk(kt, nb * 512, (nb + 1) * 512),
                    start=(kt == 0), stop=(kt == KD - 1)), sem=sT)
            p3.wait("vector", sT, mmv)
            gv = p3.op("vector", lambda h, nb=nb, sh=sh, par=par: h.tensor_copy(
                g1T[:, sh * T + nb * 512: sh * T + (nb + 1) * 512], pP[par][:]), sem=sV)
            pP_free[par] = (sV, gv)
        wst_free[1] = sT.val

    p3.drain("scalar")
    s_bt = nop_inc(p3, "scalar", sS)
    p3.wait("vector", sS, s_bt)
    p3.op("vector", lambda h: h.tensor_scalar(beta_f[:], beta_f[:], 2.0, None, ALU.mult))
    p3.drain("vector")
    v_bt = nop_inc(p3, "vector", sV)
    p3.wait("tensor", sV, v_bt)
    for c in range(NTF):
        tv = p3.op("tensor", lambda h, c=c: h.transpose(
            pBt[:, 0:NHL], beta_f[:, ts(c, 128)], eyef[0:NHL, 0:NHL]), sem=sT)
        p3.wait("vector", sT, tv)
        bv = p3.op("vector", lambda h, c=c: h.tensor_copy(
            beta_tm[:, c * NHL:(c + 1) * NHL], pBt[:, 0:NHL]), sem=sV)
        p3.wait("tensor", sV, bv)

    p3.wait("tensor", ldg2[0], ldg2[1])
    gate_ev = {0: None, 1: None}
    for c in range(NTF):
        for nb in range(2):
            par = (2 * c + nb) % 2
            if gate_ev[par] is not None:
                p3.wait("tensor", gate_ev[par][0], gate_ev[par][1])
            mmv = p3.op("tensor", lambda h, c=c, nb=nb, par=par: h.matmul(
                pP[par][:], g1T[:, ts(c, 128)], wg2s[:, ts(nb, 512)],
                start=True, stop=True), sem=sT)
            p3.wait("scalar", sT, mmv)
            if stq_free[par] is not None:
                p3.wait("scalar", stq_free[par][0], stq_free[par][1])
            scv = p3.op("scalar", lambda h, par=par: h.activation(
                stq[par][:], pP[par][:], ACT.Sigmoid), sem=sS)
            stv = dmaS.issue(p3, gate_dram[c, :, ts(nb, 512)], stq[par][:],
                             pre_wait=[(sS, scv)])
            stq_free[par] = stv
            gate_ev[par] = (sS, scv)
    p3.emit(blk); p3s.close()
    snap("p3_end")
    if DBG_PHASE == "p3":
        return dbg_finish()

    p4s = contextlib.ExitStack()
    woT = sbuf("woT", [128, KHL * D], BF16, p4s)
    wof = [sbuf(f"wof{i}", [128, D], F32, p4s) for i in range(2)]
    S32 = sbuf("S32", [128, NHL * 128], F32, p4s)
    S_bf = sbuf("S_bf", [128, NHL * 128], BF16, p4s)
    qkvb = [sbuf(f"qkvb{i}", [128, NHL * 3 * 128], BF16, p4s) for i in range(2)]
    gtb = [sbuf(f"gtb{i}", [128, HL], BF16, p4s) for i in range(2)]
    scr = sbuf("scr", [128, 42 * 128], BF16, p4s)
    scw = sbuf("scw", [128, NHL * 2 * 128], BF16, p4s)
    gsb = sbuf("gsb", [128, NHL * 128], BF16, p4s)
    scf = sbuf("scf", [128, 128], F32, p4s)
    scf2 = sbuf("scf2", [128, 128], F32, p4s)
    ogb = sbuf("ogb", [128, HL], F32, p4s)
    ogsq = sbuf("ogsq", [128, HL], F32, p4s)
    ogn_s = sbuf("ogn_s", [128, HL], BF16, p4s)
    ognT = [sbuf(f"ognT{i}", [128, NHL * 128], BF16, p4s) for i in range(2)]
    sc8 = sbuf("sc8", [128, 8 * 8], F32, p4s)
    fst4 = [sbuf(f"fst4{i}", [128, 512], F32, p4s) for i in range(2)]

    bankA = [psum(f"pbA{i}", [128, 512], F32, p4s) for i in range(2)]
    bankB = [psum(f"pbB{i}", [128, 512], F32, p4s) for i in range(2)]
    pT4l = [psum(f"pT4{i}", [128, 256], BF16, p4s) for i in range(2)]
    pCmb = [bankA[0][:, 0:384], bankA[1][:, 0:384]]
    pWo = [psum(f"pWo{i}", [128, 512], F32, p4s) for i in range(2)]

    SLOT_NAMES = ["Nd", "Dbn", "P0", "P1", "P2", "P3", "Bd0", "Bd1", "Bd2", "Bd3", "Bd4",
                  "Bo", "LT", "ta", "tb", "v0", "Uf", "Khat", "R", "TG2", "GNs"]

    def slot(par, name):
        i = par * len(SLOT_NAMES) + SLOT_NAMES.index(name)
        return scr[:, i * 128:(i + 1) * 128]

    def s8(q, j=None):
        if j is None:
            return sc8[:, q * 8:(q + 1) * 8]
        return sc8[:, q * 8 + j: q * 8 + j + 1]

    p4 = Phase("p4")
    barrier(p4, "p3_end")

    s_wo = []
    for r in range(KHL):
        ld = dmaS.issue(p4, wof[r % 2][:], wo_t[ts(r, 128), :],
                        pre_wait=([(sS, s_wo[r - 2])] if r >= 2 else None))
        p4.wait("scalar", ld[0], ld[1])
        s_wo.append(p4.op("scalar", lambda h, r=r: h.activation(
            woT[:, ts(r, D)], wof[r % 2][:], ACT.Copy, scale=gnorm_sb[:, r:r + 1]), sem=sS))
    p4.op("vector", lambda h: h.memset(S32[:], 0.0), sem=sV)
    v_z2 = p4.op("vector", lambda h: h.memset(S_bf[:], 0.0), sem=sV)

    rs_stores = []
    last_sbf_v = {bh: (sV, v_z2) for bh in range(NHL)}
    prev_pa_ev = None
    qkv_free = []
    gate_free = []
    wo_done = {0: None, 1: None}
    pWo_free = {0: None, 1: None}
    fst4_free = {0: None, 1: None}
    ognT_ev = [None]
    mix_guard = {0: (sV, v_z2), 1: (sV, v_z2)}
    pt4_guard = {0: None, 1: None}
    u_guard = {0: None, 1: None}

    for c in range(DBG_CHUNKS if DBG_CHUNKS else NTF):
        pwq = [(sT, qkv_free[c - 2])] if c >= 2 else None
        ldq = dmaG.issue(p4, qkvb[c % 2][:],
                         qkv_scan[c].rearrange("p b t f -> p (b t f)"), pre_wait=pwq)
        pwg = [(sV, gate_free[c - 2])] if c >= 2 else None
        ldg = dmaG.issue(p4, gtb[c % 2][:], gate_dram[c, :, :], pre_wait=pwg)

        def kqv(bh, w, c=c):
            return qkvb[c % 2][:, (bh * 3 + w) * 128: (bh * 3 + w + 1) * 128]

        pa_ev = []
        p4.wait("tensor", ldq[0], ldq[1])
        for bh in (range(NHL) if DBG_SCANPART >= 1 else []):
            par = bh % 2
            p4.wait("tensor", last_sbf_v[bh][0], last_sbf_v[bh][1])
            p4.wait("tensor", mix_guard[par][0], mix_guard[par][1])
            guard = pa_ev[-2] if len(pa_ev) >= 2 else (prev_pa_ev[bh - 2] if prev_pa_ev else None)
            if guard:
                p4.wait("tensor", sV, guard[0])
                p4.wait("tensor", sS, guard[1])
            KT = kqv(bh, 1)
            QT = kqv(bh, 0)
            p4.op("tensor", lambda h, KT=KT, par=par: h.matmul(
                pCmb[par][:, 0:128], KT, KT, start=True, stop=True), sem=sT)
            p4.op("tensor", lambda h, KT=KT, par=par, bh=bh: h.matmul(
                pCmb[par][:, 128:256], KT, S_bf[:, ts(bh, 128)], start=True, stop=True), sem=sT)
            tA = p4.op("tensor", lambda h, KT=KT, QT=QT, par=par: h.matmul(
                pCmb[par][:, 256:384], KT, QT, start=True, stop=True), sem=sT)
            p4.wait("vector", sT, tA)
            if DBG_PA >= 2:
                p4.op("vector", lambda h, bh=bh, par=par: h.tensor_copy(
                    gsb[:, ts(bh, 128)], pCmb[par][:, 0:128]), sem=sV)
                p4.op("vector", lambda h, par=par: h.tensor_mul(scf[:], pCmb[par][:, 0:128], eyef[:]), sem=sV)
                p4.drain("vector")
                vA = p4.op("vector", lambda h, bh=bh: h.tensor_reduce(
                    s8(0, bh), scf[:], AX, ALU.add), sem=sV)
                p4.drain("vector")
            else:
                vA = nop_inc(p4, "vector", sV)
            p4.wait("scalar", sT, tA)
            p4.wait("scalar", sV, vA)
            if DBG_PA >= 3:
                p4.op("scalar", lambda h, bh=bh, par=par: h.copy(
                    scw[:, ts(2 * bh, 128)], pCmb[par][:, 128:256]), sem=sS)
                sA = p4.op("scalar", lambda h, bh=bh, par=par: h.copy(
                    scw[:, ts(2 * bh + 1, 128)], pCmb[par][:, 256:384]), sem=sS)
            else:
                sA = nop_inc(p4, "scalar", sS)
            pa_ev.append((vA, sA))
        prev_pa_ev = pa_ev

        if DBG_SCANPART < 2:
            qkv_free.append(sT.val)
            gate_free.append(sV.val)
            continue
        p4.drain("vector")
        rsqrt_chain(p4, s8(0), s8(1), s8(4), s8(5), s8(6), eps_add=EPS_K)
        bsl = beta_tm[:, c * NHL:(c + 1) * NHL]
        p4.op("vector", lambda h, b=bsl: h.tensor_mul(s8(2), s8(1), b))
        p4.drain("vector")
        p4.op("vector", lambda h: h.tensor_scalar(s8(3), s8(2), -1.0, None, ALU.mult))
        p4.drain("vector")
        nop_inc(p4, "vector", sV)

        def pass_b_gen(bh):
            par = bh % 2
            n_ap = s8(1, bh)
            bn_ap = s8(2, bh)
            nbn_ap = s8(3, bh)
            bcol = beta_tm[:, c * NHL + bh: c * NHL + bh + 1]
            KT = kqv(bh, 1)
            QT = kqv(bh, 0)
            VT = kqv(bh, 2)
            scfl = scf if par == 0 else scf2
            mxGN = bankB[par][:, 256:384]
            mxB0 = bankB[par][:, 384:512]
            mxO = mxGN
            mxDS = mxB0
            sqP = bankB[par][:, 0:128]
            sqB = bankB[par][:, 128:256]
            apP = bankA[par][:, 384:512]
            t4V = pT4l[par][:, 0:128]
            t4K = pT4l[par][:, 128:256]

            p4.op("vector", lambda h, par=par, n=n_ap: h.tensor_scalar(
                slot(par, "Nd"), eyef[:], n, None, ALU.mult), sem=sV)
            v_nd = p4.op("vector", lambda h, par=par, b=bn_ap: h.tensor_scalar(
                slot(par, "Dbn"), eyef[:], b, None, ALU.mult), sem=sV)
            p4.wait("tensor", mix_guard[par][0], mix_guard[par][1])
            p4.wait("tensor", sV, v_nd)
            t_gn = p4.op("tensor", lambda h, bh=bh, par=par, o=mxGN: h.matmul(
                o, gsb[:, ts(bh, 128)], slot(par, "Nd"), start=True, stop=True), sem=sT)
            yield
            p4.wait("vector", sT, t_gn)
            v_gns = p4.op("vector", lambda h, par=par, i_=mxGN: h.tensor_copy(slot(par, "GNs"), i_), sem=sV)
            v_p0 = p4.op("vector", lambda h, par=par, nb=nbn_ap, i_=mxGN: h.scalar_tensor_tensor(
                slot(par, "P0"), i_, nb, m_sl[:], ALU.mult, ALU.mult), sem=sV)
            yield
            p4.wait("tensor", sV, v_p0)
            t_b0 = p4.op("tensor", lambda h, par=par, o=mxB0: h.matmul(
                o, slot(par, "GNs"), slot(par, "Dbn"), start=True, stop=True), sem=sT)
            yield
            p4.wait("vector", sT, t_b0)
            p4.op("vector", lambda h, par=par, i_=mxB0: h.scalar_tensor_tensor(
                slot(par, "Bd0"), i_, -1.0, m_su[:], ALU.mult, ALU.mult), sem=sV)
            v_b0 = p4.op("vector", lambda h, par=par, i_=mxB0: h.scalar_tensor_tensor(
                slot(par, "Bo"), i_, -1.0, m_bou[:], ALU.mult, ALU.mult), sem=sV)
            yield
            if pt4_guard[par] is not None:
                p4.wait("tensor", sV, pt4_guard[par])
            p4.op("tensor", lambda h, a=VT, o=t4V: h.transpose(o, a, eyeb[:]), sem=sT)
            t_tr2 = p4.op("tensor", lambda h, a=KT, o=t4K: h.transpose(o, a, eyeb[:]), sem=sT)
            yield
            p4.wait("vector", sT, t_tr2)
            p4.op("vector", lambda h, b=bcol, i_=t4V, o=scfl: h.tensor_scalar(
                o[:], i_, b, None, ALU.mult), sem=sV)
            v_kh = p4.op("vector", lambda h, par=par, n=n_ap, i_=t4K: h.tensor_scalar(
                slot(par, "Khat"), i_, n, None, ALU.mult), sem=sV)
            pt4_guard[par] = v_kh
            p4.drain("vector")
            v_r = p4.op("vector", lambda h, par=par, bh=bh, nb=nbn_ap, o=scfl: h.scalar_tensor_tensor(
                slot(par, "R"), scw[:, ts(2 * bh, 128)], nb, o[:], ALU.mult, ALU.add), sem=sV)
            v_tg2 = p4.op("vector", lambda h, par=par, bh=bh, n=n_ap: h.scalar_tensor_tensor(
                slot(par, "TG2"), scw[:, ts(2 * bh + 1, 128)], n, m_tui[:], ALU.mult, ALU.mult), sem=sV)
            yield
            p4.wait("tensor", sV, v_b0)
            s_sq = None
            for j in range(1, 5):
                if s_sq is not None:
                    p4.wait("tensor", sS, s_sq)
                if j <= 3:
                    p4.op("tensor", lambda h, par=par, j=j, o=sqP: h.matmul(
                        o, slot(par, f"Bd{j-1}"), slot(par, f"P{j-1}"),
                        start=True, stop=True), sem=sT)
                t_sq = p4.op("tensor", lambda h, par=par, j=j, o=sqB: h.matmul(
                    o, slot(par, f"P{j-1}"), slot(par, f"Bd{j-1}"),
                    start=True, stop=True), sem=sT)
                p4.wait("scalar", sT, t_sq)
                if j <= 3:
                    p4.op("scalar", lambda h, par=par, j=j, i_=sqP: h.copy(
                        slot(par, f"P{j}"), i_), sem=sS)
                s_sq = p4.op("scalar", lambda h, par=par, j=j, i_=sqB: h.copy(
                    slot(par, f"Bd{j}"), i_), sem=sS)
                yield
            p4.wait("vector", sS, s_sq)
            v_lt = p4.op("vector", lambda h, par=par: h.tensor_add(
                slot(par, "LT"), eyef[:], slot(par, "Bd4")), sem=sV)
            for j in range(3, -1, -1):
                p4.wait("tensor", sV, v_lt)
                t_lt = p4.op("tensor", lambda h, par=par, j=j, o=apP: h.matmul(
                    o, slot(par, f"P{j}"), slot(par, "LT"), start=True, stop=True), sem=sT)
                p4.wait("vector", sT, t_lt)
                v_lt = p4.op("vector", lambda h, par=par, i_=apP: h.scalar_tensor_tensor(
                    slot(par, "LT"), i_, 1.0, slot(par, "LT"), ALU.bypass, ALU.add), sem=sV)
                yield
            p4.wait("tensor", sV, v_lt)
            p4.wait("tensor", sV, v_r)
            t_v0 = p4.op("tensor", lambda h, par=par, o=apP: h.matmul(
                o, slot(par, "LT"), slot(par, "R"), start=True, stop=True), sem=sT)
            p4.wait("scalar", sT, t_v0)
            s_v0 = p4.op("scalar", lambda h, par=par, i_=apP: h.copy(slot(par, "v0"), i_), sem=sS)
            yield
            hcur = "v0"
            ev_h = (sS, s_v0)
            for stage in range(3):
                p4.wait("tensor", ev_h[0], ev_h[1])
                t_z = p4.op("tensor", lambda h, par=par, hcur=hcur, o=apP: h.matmul(
                    o, slot(par, "Bo"), slot(par, hcur), start=True, stop=True), sem=sT)
                p4.wait("scalar", sT, t_z)
                s_z = p4.op("scalar", lambda h, par=par, i_=apP: h.copy(slot(par, "tb"), i_), sem=sS)
                yield
                p4.wait("tensor", sS, s_z)
                t_w = p4.op("tensor", lambda h, par=par, o=apP: h.matmul(
                    o, slot(par, "LT"), slot(par, "tb"), start=True, stop=True), sem=sT)
                dstn = "ta" if stage < 2 else "Uf"
                if dstn == "Uf" and u_guard[par] is not None:
                    p4.wait("vector", sT, u_guard[par])
                p4.wait("vector", sT, t_w)
                v_h = p4.op("vector", lambda h, par=par, dstn=dstn, i_=apP: h.scalar_tensor_tensor(
                    slot(par, dstn), i_, 1.0, slot(par, "v0"), ALU.bypass, ALU.add), sem=sV)
                ev_h = (sV, v_h)
                hcur = dstn
                yield
            U = hcur
            p4.wait("tensor", ev_h[0], ev_h[1])
            p4.wait("tensor", sV, v_tg2)
            p4.op("tensor", lambda h, QT=QT, bh=bh, o=mxO: h.matmul(
                o, QT, S_bf[:, ts(bh, 128)], start=True, stop=False), sem=sT)
            t_o = p4.op("tensor", lambda h, par=par, U=U, o=mxO: h.matmul(
                o, slot(par, "TG2"), slot(par, U), start=False, stop=True), sem=sT)
            t_ds = p4.op("tensor", lambda h, par=par, U=U, bh=bh, o=mxDS: h.matmul(
                o, slot(par, "Khat"), slot(par, U), start=True, stop=True), sem=sT)
            u_guard[par] = t_ds
            p4.wait("vector", sT, t_o)
            p4.wait("vector", ldg[0], ldg[1])
            p4.op("vector", lambda h, bh=bh, c=c, i_=mxO: h.scalar_tensor_tensor(
                ogb[:, ts(bh, 128)], i_, 1.0, gtb[c % 2][:, ts(bh, 128)],
                ALU.bypass, ALU.mult), sem=sV)
            p4.wait("vector", sT, t_ds)
            v_s32 = p4.op("vector", lambda h, bh=bh, i_=mxDS: h.tensor_add(
                S32[:, ts(bh, 128)], S32[:, ts(bh, 128)], i_), sem=sV)
            p4.wait("scalar", sV, v_s32)
            last_sbf_v[bh] = (sS, p4.op("scalar", lambda h, bh=bh: h.copy(
                S_bf[:, ts(bh, 128)], S32[:, ts(bh, 128)]), sem=sS))
            mix_guard[par] = last_sbf_v[bh]

        if DBG_SCANPART >= 3:
            for bh0 in range(0, NHL, 2):
                gens = [pass_b_gen(bh0), pass_b_gen(bh0 + 1)]
                live = [True, True]
                while any(live):
                    for gi in range(2):
                        if live[gi]:
                            try:
                                next(gens[gi])
                            except StopIteration:
                                live[gi] = False
        qkv_free.append(sT.val)

        if DBG_SCANPART < 6:
            gate_free.append(sV.val)
            continue
        p4.drain("vector")
        p4.op("vector", lambda h: h.tensor_mul(ogsq[:], ogb[:], ogb[:]), sem=sV)
        p4.drain("vector")
        p4.op("vector", lambda h: h.tensor_reduce(
            s8(4).rearrange("p (a o) -> p a o", o=1),
            ogsq.ap().rearrange("p (a f) -> p a f", a=NHL), AX, ALU.add), sem=sV)
        p4.drain("vector")
        p4.op("vector", lambda h: h.tensor_scalar(s8(4), s8(4), 1.0 / HD, None, ALU.mult))
        p4.drain("vector")
        rsqrt_chain(p4, s8(4), s8(5), s8(0), s8(6), s8(7), eps_add=EPS_OG)
        iv = s8(5)
        inv_b = bass.AP(iv.tensor, iv.offset, [iv.ap[0], list(iv.ap[1])[:1] + [NHL] if False else [1, NHL], [0, 128]])
        v_ogn = p4.op("vector", lambda h, ib=inv_b: h.tensor_mul(
            ogn_s.ap().rearrange("p (a f) -> p a f", a=NHL),
            ogb.ap().rearrange("p (a f) -> p a f", a=NHL), ib), sem=sV)
        gate_free.append(v_ogn)
        p4.wait("tensor", sV, v_ogn)
        for _par in (0, 1):
            if pt4_guard[_par] is not None:
                p4.wait("tensor", sV, pt4_guard[_par])
        if wo_done[c % 2] is not None:
            p4.wait("vector", sT, wo_done[c % 2])
        for hb2 in range(4):
            if ognT_ev[0] is not None:
                p4.wait("tensor", sV, ognT_ev[0])
            p4.op("tensor", lambda h, hb2=hb2: h.transpose(
                pT4l[0][:, 0:128], ogn_s[:, ts(2 * hb2, 128)], eyeb[:]), sem=sT)
            tv2 = p4.op("tensor", lambda h, hb2=hb2: h.transpose(
                pT4l[1][:, 0:128], ogn_s[:, ts(2 * hb2 + 1, 128)], eyeb[:]), sem=sT)
            p4.wait("vector", sT, tv2)
            p4.op("vector", lambda h, hb2=hb2, c=c: h.tensor_copy(
                ognT[c % 2][:, hb2 * 256: hb2 * 256 + 128], pT4l[0][:, 0:128]), sem=sV)
            ognT_ev[0] = p4.op("vector", lambda h, hb2=hb2, c=c: h.tensor_copy(
                ognT[c % 2][:, hb2 * 256 + 128:(hb2 + 1) * 256], pT4l[1][:, 0:128]), sem=sV)
        p4.wait("tensor", sV, ognT_ev[0])
        pt4_guard[0] = ognT_ev[0]
        pt4_guard[1] = ognT_ev[0]
        for nb in range(4):
            par = nb % 2
            if pWo_free[par] is not None:
                p4.wait("tensor", pWo_free[par][0], pWo_free[par][1])
            for kt in range(KHL):
                t_wo = p4.op("tensor", lambda h, nb=nb, kt=kt, c=c, par=par: h.matmul(
                    pWo[par][:], ognT[c % 2][:, ts(kt, 128)],
                    woT[:, kt * D + nb * 512: kt * D + (nb + 1) * 512],
                    start=(kt == 0), stop=(kt == KHL - 1)), sem=sT)
            eng, sem = ("scalar", sS) if par == 0 else ("vector", sV)
            p4.wait(eng, sT, t_wo)
            if fst4_free[par] is not None:
                p4.wait(eng, fst4_free[par][0], fst4_free[par][1])
            if eng == "scalar":
                e_wo = p4.op(eng, lambda h, par=par: h.copy(fst4[par][:], pWo[par][:]), sem=sem)
            else:
                e_wo = p4.op(eng, lambda h, par=par: h.tensor_copy(fst4[par][:], pWo[par][:]), sem=sem)
            pWo_free[par] = (sem, e_wo)
            k_ = dmaS.i % len(dmaS.sems)
            dmaS.i += 1
            sring = dmaS.sems[k_]
            if dmaS.uses[k_] > 0:
                p4.wait("sync", sring, 16 * dmaS.uses[k_])
            dmaS.uses[k_] += 1
            p4.wait("sync", sem, e_wo)
            op_pair(p4, "sync",
                    lambda h, c=c, nb=nb, par=par: h.dma_start(
                        out=rs_sh[0, ts(c, 128), ts(nb, 512)], in_=fst4[par][:]),
                    lambda h, c=c, nb=nb, par=par: h.dma_start(
                        out=rs_sh[1, ts(c, 128), ts(nb, 512)], in_=fst4[par][:]),
                    sem=sring, n=16)
            stv = (sring, sring.val)
            fst4_free[par] = stv
            rs_stores.append(stv)
        wo_done[c % 2] = sT.val

    if not DBG_CHUNKS:
        emit_barrier(p4, 1, [(s, 16 * u) for s, u in zip(dmaS.sems, dmaS.uses) if u])
    p4.emit(blk); p4s.close()
    snap("p4_end")

    if DBG_CHUNKS:
        pD = Phase("pD")
        for se, u in zip(dmaS.sems, dmaS.uses):
            if u:
                pD.wait("sync", se, 16 * u)
        for se, u in zip(dmaG.sems, dmaG.uses):
            if u:
                pD.wait("gpsimd", se, 16 * u)
        pD.emit(blk)
        st.close()
        return nc

    p5s = contextlib.ExitStack()
    rsb = [sbuf(f"rsb{i}", [128, D], F32, p5s) for i in range(4)]
    p5 = Phase("p5")
    barrier(p5, "p4_end")
    v_add = []
    for i in range(NT):
        for sl_ in range(2):
            bi = (2 * i + sl_) % 4
            k_ = dmaS.i % len(dmaS.sems)
            dmaS.i += 1
            sring = dmaS.sems[k_]
            if dmaS.uses[k_] > 0:
                p5.wait("sync", sring, 16 * dmaS.uses[k_])
            dmaS.uses[k_] += 1
            idx = 2 * i + sl_
            if idx >= 4:
                p5.wait("sync", sV, v_add[idx - 4])
            op_pair(p5, "sync",
                    lambda h, i=i, sl_=sl_, bi=bi: h.dma_start(
                        out=rsb[bi][:], in_=rs_sh[sl_, ts(i, 128), :]),
                    lambda h, i=i, sl_=sl_, bi=bi: h.dma_start(
                        out=rsb[bi][:], in_=rs_sh[sl_, T + i * 128: T + (i + 1) * 128, :]),
                    sem=sring, n=16)
            p5.wait("vector", sring, sring.val)
        xa = xr[:, ts(i, D)]
        v_add.append(p5.op("vector", lambda h, a=xa, bi0=(2 * i) % 4: h.tensor_add(
            a, a, rsb[bi0][:]), sem=sV))
        p5.drain("vector")
        v_add.append(p5.op("vector", lambda h, a=xa, bi1=(2 * i + 1) % 4: h.tensor_add(
            a, a, rsb[bi1][:]), sem=sV))
        p5.drain("vector")
    p5.emit(blk); p5s.close()
    snap("p5_end")

    emit_ffn("p6", True, "p5_end")
    snap("p6_end")

    pFin = Phase("pF")
    for s, u in zip(dmaS.sems, dmaS.uses):
        if u:
            pFin.wait("sync", s, 16 * u)
    for s, u in zip(dmaG.sems, dmaG.uses):
        if u:
            pFin.wait("gpsimd", s, 16 * u)
    pFin.emit(blk)

    st.close()
    return nc


_NC_CACHE = None


def kernel(**inputs):
    global _NC_CACHE
    x = np.ascontiguousarray(np.asarray(inputs["L0_attn_x"], dtype=np.float32))
    Wq = np.asarray(inputs["L0_attn_Wq"], dtype=np.float32)
    Wk = np.asarray(inputs["L0_attn_Wk"], dtype=np.float32)
    Wv = np.asarray(inputs["L0_attn_Wv"], dtype=np.float32)
    Wbeta = np.asarray(inputs["L0_attn_Wbeta"], dtype=np.float32)
    Wg1 = np.asarray(inputs["L0_attn_Wg1"], dtype=np.float32)
    Wg2 = np.asarray(inputs["L0_attn_Wg2"], dtype=np.float32)
    g_norm = np.asarray(inputs["L0_attn_g_norm"], dtype=np.float32)
    Wo = np.asarray(inputs["L0_attn_Wo"], dtype=np.float32)
    Wffn = np.asarray(inputs["L0_ffn_weight"], dtype=np.float32)
    fb = np.asarray(inputs["L0_ffn_bias"], dtype=np.float32)

    wq_T = np.ascontiguousarray(Wq.T)
    wk_T = np.ascontiguousarray(Wk.T)
    wv_T = np.ascontiguousarray(Wv.T)
    wb_T = np.ascontiguousarray(Wbeta.T)
    wg1_T = np.ascontiguousarray(Wg1.T)
    wg2_T = np.ascontiguousarray(Wg2.T)
    wo_T = np.ascontiguousarray(Wo.T)

    eye_bf = np.eye(128, dtype=ml_dtypes.bfloat16)
    eye_f = np.eye(128, dtype=np.float32)
    idx = np.arange(128)
    same_blk = (idx[:, None] // 32) == (idx[None, :] // 32)
    msl = ((idx[:, None] > idx[None, :]) & same_blk).astype(np.float32)
    msu = ((idx[:, None] < idx[None, :]) & same_blk).astype(np.float32)
    mbou = ((idx[:, None] < idx[None, :]) & ~same_blk).astype(np.float32)
    mtui = (idx[:, None] <= idx[None, :]).astype(np.float32)

    if _NC_CACHE is None:
        _NC_CACHE = build_program()
    nc = _NC_CACHE

    nonce = int(np.random.randint(1, 2**30))
    flagv = np.empty((2, 128), dtype=np.int32)
    flagv[0, :] = nonce + 1
    flagv[1, :] = nonce + 2

    in_maps = []
    for c in range(8):
        b, h = c // 2, c % 2
        hs = slice(h * HL, (h + 1) * HL)
        in_maps.append({
            "x_in": np.ascontiguousarray(x[b, h * T:(h + 1) * T, :]),
            "wq_t": np.ascontiguousarray(wq_T[:, hs]),
            "wk_t": np.ascontiguousarray(wk_T[:, hs]),
            "wv_t": np.ascontiguousarray(wv_T[:, hs]),
            "wbeta_t": np.ascontiguousarray(wb_T[:, h * NHL:(h + 1) * NHL]),
            "wg1_t": wg1_T,
            "wg2_t": np.ascontiguousarray(wg2_T[:, hs]),
            "wo_t": np.ascontiguousarray(wo_T[hs, :]),
            "gnorm_c": np.ascontiguousarray(g_norm[hs]),
            "wffn": Wffn,
            "fbias": fb,
            "eye_bfd": eye_bf,
            "eye_fd": eye_f,
            "msl_d": msl,
            "msu_d": msu,
            "mtui_d": mtui,
            "mbou_d": mbou,
            "flagv": flagv,
        })

    res = run_bass_kernel_spmd(nc, in_maps, core_ids=list(range(8)))
    out = np.empty((B, N, D), dtype=np.float32)
    for c in range(8):
        b, h = c // 2, c % 2
        out[b, h * T:(h + 1) * T, :] = res.results[c]["x_out"]
    return out


# version 12
# speedup vs baseline: 1.0740x; 1.0740x over previous
import contextlib
import os
import numpy as np
import ml_dtypes
import concourse.bass as bass
import concourse.mybir as mybir
from concourse.bass_utils import run_bass_kernel_spmd
from concourse.alu_op_type import AluOpType as RALU

F32 = mybir.dt.float32
BF16 = mybir.dt.bfloat16
AX = mybir.AxisListType.X
ALU = mybir.AluOpType
ACT = mybir.ActivationFunctionType
ts = bass.ts

B, N, D = 4, 4096, 2048
T = 2048
NT = T // 128
NTF = N // 128
KD = D // 128
NHL, HD = 8, 128
HL = NHL * HD
KHL = HL // 128

EPS_LN = 1e-5
EPS_RMS = 1e-8
EPS_G = 1e-8
EPS_X = 1e-8
EPS_K = 1e-6
EPS_OG = 1e-6
QB = 127.0
USE_DRAIN = os.environ.get("KDBG_DRAIN", "1") == "1"
DBG_CHUNKS = int(os.environ.get("KDBG_CHUNKS", "0") or 0)
DBG_PHASE = os.environ.get("KDBG_PHASE", "")
DBG_SCANPART = int(os.environ.get("KDBG_SCANPART", "9") or 9)
DBG_PA = int(os.environ.get("KDBG_PA", "9") or 9)

ENGINES = ("sync", "gpsimd", "vector", "scalar", "tensor")


class Sem:
    def __init__(self, h, name):
        self.h = h
        self.name = name
        self.val = 0

    def inc(self, n=1):
        self.val += n
        return self.val


class Phase:

    def __init__(self, name):
        self.name = name
        self.ops = {e: [] for e in ENGINES}

    def op(self, eng, fn, sem=None, n=1):
        if sem is None:
            self.ops[eng].append(fn)
            return None
        v = sem.inc(n)
        self.ops[eng].append(lambda h, fn=fn, sem=sem, n=n: fn(h).then_inc(sem.h, n))
        return v

    def wait(self, eng, sem, val):
        if val:
            self.ops[eng].append(lambda h, s=sem, v=val: h.wait_ge(s.h, v))

    def drain(self, eng):
        if eng == "vector" and not USE_DRAIN:
            return
        self.ops[eng].append(lambda h: h.drain())

    def emit(self, blk):
        for e in ENGINES:
            lst = self.ops[e]
            if not lst:
                continue

            def body(h, lst=lst):
                for f in lst:
                    f(h)
            getattr(blk, e)(body)


class Dma:

    def __init__(self, sems, queue_eng):
        self.sems = sems
        self.uses = [0] * len(sems)
        self.i = 0
        self.eng = queue_eng

    def issue(self, ph, out_ap, in_ap, transpose=False, pre_wait=None, slow_ok=False):
        k = self.i % len(self.sems)
        self.i += 1
        s = self.sems[k]
        if self.uses[k] > 0:
            ph.wait(self.eng, s, 16 * self.uses[k])
        self.uses[k] += 1
        if pre_wait:
            for (ws, wv) in pre_wait:
                ph.wait(self.eng, ws, wv)
        v = s.inc(16)
        if transpose:
            ph.ops[self.eng].append(
                lambda h, o=out_ap, i_=in_ap, s=s: h.dma_start_transpose(o, i_).then_inc(s.h, 16))
        else:
            ph.ops[self.eng].append(
                lambda h, o=out_ap, i_=in_ap, s=s, sl=slow_ok: h.dma_start(
                    out=o, in_=i_, allow_slow_non_contiguous=sl).then_inc(s.h, 16))
        return (s, v)


def build_program():
    nc = bass.Bass("TRN2", target_bir_lowering=False,
                   detect_race_conditions=(os.environ.get("KDBG_RACE", "0") == "1"))

    x_in = nc.dram_tensor("x_in", [T, D], F32, kind="ExternalInput").ap()
    wq_t = nc.dram_tensor("wq_t", [D, HL], F32, kind="ExternalInput").ap()
    wk_t = nc.dram_tensor("wk_t", [D, HL], F32, kind="ExternalInput").ap()
    wv_t = nc.dram_tensor("wv_t", [D, HL], F32, kind="ExternalInput").ap()
    wbeta_t = nc.dram_tensor("wbeta_t", [D, NHL], F32, kind="ExternalInput").ap()
    wg1_t = nc.dram_tensor("wg1_t", [D, 128], F32, kind="ExternalInput").ap()
    wg2_t = nc.dram_tensor("wg2_t", [128, HL], F32, kind="ExternalInput").ap()
    wo_t = nc.dram_tensor("wo_t", [HL, D], F32, kind="ExternalInput").ap()
    gnorm_c = nc.dram_tensor("gnorm_c", [HL], F32, kind="ExternalInput").ap()
    wffn = nc.dram_tensor("wffn", [D, D], F32, kind="ExternalInput").ap()
    fbias = nc.dram_tensor("fbias", [D], F32, kind="ExternalInput").ap()
    eye_bfd = nc.dram_tensor("eye_bfd", [128, 128], BF16, kind="ExternalInput").ap()
    eye_fd = nc.dram_tensor("eye_fd", [128, 128], F32, kind="ExternalInput").ap()
    msl_d = nc.dram_tensor("msl_d", [128, 128], F32, kind="ExternalInput").ap()
    msu_d = nc.dram_tensor("msu_d", [128, 128], F32, kind="ExternalInput").ap()
    mtui_d = nc.dram_tensor("mtui_d", [128, 128], F32, kind="ExternalInput").ap()
    mbou_d = nc.dram_tensor("mbou_d", [128, 128], F32, kind="ExternalInput").ap()

    x_out = nc.dram_tensor("x_out", [T, D], F32, kind="ExternalOutput").ap()

    wffn_tern = nc.dram_tensor("wffn_tern", [D, D], BF16, kind="Internal").ap()
    ag_out = nc.dram_tensor("ag_out", [N, D], BF16, kind="Internal", addr_space="Shared").ap()
    qkv_scan = nc.dram_tensor("qkv_scan", [NTF, 128, NHL, 3, 128], BF16, kind="Internal").ap()
    gate_dram = nc.dram_tensor("gate_dram", [NTF, 128, HL], BF16, kind="Internal").ap()
    rs_sh = nc.dram_tensor("rs_sh", [2, N, D], F32, kind="Internal", addr_space="Shared").ap()
    flg = nc.dram_tensor("flg", [2, 128], mybir.dt.int32, kind="Internal", addr_space="Shared").ap()
    flagv = nc.dram_tensor("flagv", [2, 128], mybir.dt.int32, kind="ExternalInput").ap()

    st = contextlib.ExitStack()

    def mksem(name):
        return Sem(st.enter_context(nc.semaphore(name)), name)

    sT = mksem("sT")
    sV = mksem("sV")
    sS = mksem("sS")
    sGc = mksem("sGc")
    sC = mksem("sC")
    dmaS = Dma([mksem(f"dS{i}") for i in range(8)], "sync")
    dmaG = Dma([mksem(f"dG{i}") for i in range(8)], "gpsimd")
    ENG_SEMS = [sT, sV, sS, sGc]

    def sbuf(name, shape, dt=F32, scope=None):
        return (scope or st).enter_context(nc.sbuf_tensor(name, shape, dt))

    def psum(name, shape, dt=F32, scope=None):
        return (scope or st).enter_context(nc.psum_tensor(name, shape, dt))

    xr = sbuf("xr", [128, NT * D], BF16)
    stats = sbuf("stats", [128, 16 * 14], F32)
    bias_rep = sbuf("bias_rep", [128, D], BF16)
    sgam = sbuf("sgam", [128, KD], F32)
    eyeb = sbuf("eyeb", [128, 128], BF16)
    eyef = sbuf("eyef", [128, 128], F32)
    m_sl = sbuf("m_sl", [128, 128], F32)
    m_su = sbuf("m_su", [128, 128], F32)
    m_tui = sbuf("m_tui", [128, 128], F32)
    m_bou = sbuf("m_bou", [128, 128], F32)
    onesr = sbuf("onesr", [1, 128], BF16)
    fbias_sb = sbuf("fbias_sb", [1, D], BF16)
    gnorm_sb = sbuf("gnorm_sb", [128, KHL], F32)
    beta_tm = sbuf("beta_tm", [128, NTF * NHL], F32)
    flv_sb = st.enter_context(nc.sbuf_tensor("flv_sb", [2, 128], mybir.dt.int32))
    fl_sb = st.enter_context(nc.sbuf_tensor("fl_sb", [1, 128], mybir.dt.int32))

    (Q_SUMX, Q_SUMSQ, Q_MX, Q_MN, Q_M, Q_V, Q_SC, Q_KQ, Q_KB, Q_XS,
     Q_T0, Q_T1, Q_T2, Q_T3) = range(14)

    def stat(q, i=None):
        if i is None:
            return stats[:, q * 16: q * 16 + 16]
        return stats[:, q * 16 + i: q * 16 + i + 1]

    phases = []
    snaps = {}

    def snap(tag):
        snaps[tag] = ([(s, s.val) for s in ENG_SEMS + [sC]],
                      [(s, 16 * u) for s, u in zip(dmaS.sems + dmaG.sems, dmaS.uses + dmaG.uses)])

    def barrier(ph, tag):
        engs, dmas = snaps[tag]
        for e in ENGINES:
            for s, v in engs + dmas:
                if v:
                    ph.wait(e, s, v)

    blk = st.enter_context(nc.Block())

    PREG = {}

    def parity_setup(h, eng):
        if eng in PREG:
            return PREG[eng]
        pid = h.to_reg(h.partition_id())
        par = h.alloc_register()
        h.reg_alu(par, pid, 1, RALU.bitwise_and)
        PREG[eng] = par
        return par

    def op_pair(ph, eng, fn_even, fn_odd, sem=None, n=1):
        v = sem.inc(n) if sem is not None else None

        def closure(h, fe=fn_even, fo=fn_odd, eng=eng, sem=sem, n=n):
            par = parity_setup(h, eng)
            with h.If_eq(par, 0):
                ins = fe(h)
                if sem is not None:
                    ins.then_inc(sem.h, n)
            with h.Else():
                ins = fo(h)
                if sem is not None:
                    ins.then_inc(sem.h, n)
        ph.ops[eng].append(closure)
        return v

    poll_sems = [mksem(f"poll{i}") for i in range(2)]

    def emit_barrier(ph, k, pre_dma_sems):
        for (sm, val) in pre_dma_sems:
            ph.wait("gpsimd", sm, val)
        fw = poll_sems[k]

        def closure_all(h, k=k, fw=fw):
            par = parity_setup(h, "gpsimd")
            with h.If_eq(par, 0):
                h.dma_start(out=flg[0:1, :], in_=flv_sb[k:k + 1, :]).then_inc(fw.h, 16)
            with h.Else():
                h.dma_start(out=flg[1:2, :], in_=flv_sb[k:k + 1, :]).then_inc(fw.h, 16)
            h.wait_ge(fw.h, 16)
            with h.register(f"cnt{k}") as cnt, h.register(f"cc{k}") as cc, \
                 h.register(f"vv{k}") as vv, h.register(f"mgk{k}") as mgk:
                h.load(mgk, flv_sb[k:k + 1, 0:1])
                h.reg_mov(cnt, 16)
                h.reg_mov(cc, 1)
                with h.While(cc):
                    h.reg_add(cnt, cnt, 16)
                    with h.If_eq(par, 0):
                        h.dma_start(out=fl_sb[:], in_=flg[1:2, :]).then_inc(fw.h, 16)
                    with h.Else():
                        h.dma_start(out=fl_sb[:], in_=flg[0:1, :]).then_inc(fw.h, 16)
                    h.wait_ge(fw.h, cnt)
                    h.load(vv, fl_sb[0:1, 0:1])
                    h.reg_alu(cc, vv, mgk, RALU.subtract)
        ph.ops["gpsimd"].append(closure_all)
        return nop_inc(ph, "gpsimd", sGc)

    def nop_inc(ph, eng, sem):
        return ph.op(eng, lambda h: h.nop(), sem=sem)

    def rsqrt_chain(ph, src_ap, out_ap, t1, t2, t3, eps_add=None):
        if eps_add is not None:
            ph.op("vector", lambda h, a=src_ap, o=t1, e=eps_add: h.tensor_scalar(o, a, e, None, ALU.add))
            ph.drain("vector")
            base = t1
        else:
            base = src_ap
        vv = nop_inc(ph, "vector", sV)
        ph.wait("scalar", sV, vv)
        sv = ph.op("scalar", lambda h, a=base, o=t2: h.activation(o, a, ACT.Sqrt), sem=sS)
        ph.wait("vector", sS, sv)
        ph.op("vector", lambda h, o=t3, a=t2: h.reciprocal(o, a))
        ph.drain("vector")
        ph.op("vector", lambda h, o=t3, a=base, b=t3: h.tensor_mul(o, a, b))
        ph.drain("vector")
        ph.op("vector", lambda h, o=t3, a=t3, b=t2: h.tensor_add(o, a, b))
        ph.drain("vector")
        ph.op("vector", lambda h, o=t3, a=t3: h.tensor_scalar(o, a, 0.5, None, ALU.mult))
        ph.drain("vector")
        ph.op("vector", lambda h, o=out_ap, a=t3: h.reciprocal(o, a))
        ph.drain("vector")


    def dbg_finish():
        pD = Phase("pDbg")
        for se, u in zip(dmaS.sems, dmaS.uses):
            if u:
                pD.wait("sync", se, 16 * u)
        for se, u in zip(dmaG.sems, dmaG.uses):
            if u:
                pD.wait("gpsimd", se, 16 * u)
        pD.emit(blk)
        st.close()
        return nc

    p0s = contextlib.ExitStack()
    wkA = sbuf("wkA", [128, D], F32, p0s)
    wkB = sbuf("wkB", [128, D], F32, p0s)
    fA = sbuf("fA", [128, D], F32, p0s)
    fB = sbuf("fB", [128, D], F32, p0s)
    stg0 = sbuf("stg0", [128, D], BF16, p0s)
    stg1 = sbuf("stg1", [128, D], BF16, p0s)
    pbc = psum("pbc", [128, 512], F32, p0s)

    p0 = Phase("p0")
    cvals = []
    for sb_t, dr in ((eyeb, eye_bfd), (eyef, eye_fd), (m_sl, msl_d), (m_su, msu_d), (m_tui, mtui_d), (m_bou, mbou_d)):
        cvals.append(dmaS.issue(p0, sb_t[:], dr[:]))
    cvals.append(dmaS.issue(p0, gnorm_sb[:, 0:KHL],
                            gnorm_c.rearrange("(a p) -> p a", p=128), slow_ok=True))
    dmaG.issue(p0, xr.ap().rearrange("p (t d) -> p t d", t=NT),
               x_in.rearrange("(t p) d -> p t d", p=128))
    fbl = dmaG.issue(p0, fbias_sb[:], fbias[None, :])
    dmaS.issue(p0, flv_sb[:], flagv[:])

    v_ones = p0.op("vector", lambda h: h.memset(onesr[:], 1.0), sem=sV)

    p0.wait("tensor", fbl[0], fbl[1])
    p0.wait("tensor", sV, v_ones)
    for nb in range(4):
        if nb >= 1:
            p0.wait("tensor", sV, sV.val)
        tb = p0.op("tensor", lambda h, nb=nb: h.matmul(
            pbc[:, 0:512], onesr[:], fbias_sb[:, ts(nb, 512)], start=True, stop=True), sem=sT)
        p0.wait("vector", sT, tb)
        p0.op("vector", lambda h, nb=nb: h.tensor_scalar(
            bias_rep[:, ts(nb, 512)], pbc[:, 0:512], 0.5, None, ALU.mult), sem=sV)

    for (s, v) in cvals:
        p0.wait("vector", s, v)
    v_tern = []
    s_cast = []
    tern_store = []
    for i in range(KD):
        src = wkA if i % 2 == 0 else wkB
        dst = stg0 if i % 2 == 0 else stg1
        ld = dmaG.issue(p0, src[:], wffn[ts(i, 128), :],
                        pre_wait=([(sV, v_tern[i - 2])] if i >= 2 else None))
        p0.wait("vector", ld[0], ld[1])
        p0.op("vector", lambda h, s_=src, i=i: h.tensor_reduce(
            stat(Q_T0, i), s_[:], AX, ALU.add, apply_absolute_value=True))
        p0.drain("vector")
        p0.op("vector", lambda h, i=i: h.tensor_scalar(
            stat(Q_T1, i), stat(Q_T0, i), 1.0 / D, EPS_G, ALU.mult, ALU.add))
        p0.drain("vector")
        p0.op("vector", lambda h, i=i: h.reciprocal(stat(Q_T2, i), stat(Q_T1, i)))
        p0.op("vector", lambda h, i=i: h.tensor_scalar(
            sgam[:, i:i + 1], stat(Q_T0, i), 0.5 / D, None, ALU.mult))
        p0.drain("vector")
        if i >= 1:
            p0.wait("vector", sS, s_cast[i - 1])
        p0.op("vector", lambda h, s_=src, i=i: h.tensor_scalar(
            fA[:], s_[:], stat(Q_T2, i), 0.5, ALU.mult, ALU.is_gt))
        p0.op("vector", lambda h, s_=src, i=i: h.tensor_scalar(
            fB[:], s_[:], stat(Q_T2, i), -0.5, ALU.mult, ALU.is_lt))
        p0.drain("vector")
        v_tern.append(p0.op("vector", lambda h: h.tensor_sub(fA[:], fA[:], fB[:]), sem=sV))
        p0.wait("scalar", sV, v_tern[i])
        if i >= 2:
            p0.wait("scalar", tern_store[i - 2][0], tern_store[i - 2][1])
        s_cast.append(p0.op("scalar", lambda h, d=dst, i=i: h.activation(
            d[:], fA[:], ACT.Copy, scale=sgam[:, i:i + 1]), sem=sS))
        tern_store.append(dmaS.issue(p0, wffn_tern[ts(i, 128), :], dst[:],
                                     pre_wait=[(sS, s_cast[i])]))

    p0.emit(blk); p0s.close()
    snap("p0_end")
    if DBG_PHASE == "p0":
        return dbg_finish()

    def emit_ffn(tag, write_out, after_tag):
        ps_ = contextlib.ExitStack()
        tbuf = sbuf(f"tbuf_{tag}", [128, D], F32, ps_)
        fbuf = sbuf(f"fbuf_{tag}", [128, D], F32, ps_)
        aibuf = ps_.enter_context(nc.sbuf_tensor(f"ai_{tag}", [128, D], mybir.dt.int32))
        xqs = sbuf(f"xqs_{tag}", [128, D], BF16, ps_)
        xqT = sbuf(f"xqT_{tag}", [128, (NT // 2) * D], BF16, ps_)
        weffh = [sbuf(f"weffh{i}_{tag}", [128, KD * 1024], BF16, ps_) for i in range(2)]
        fstb = [sbuf(f"fst{i}_{tag}", [128, 512], F32, ps_) for i in range(2)]
        pM = [psum(f"pM{i}_{tag}", [128, 512], F32, ps_) for i in range(2)]
        pX = psum(f"pX_{tag}", [128, 512], BF16, ps_)

        ph = Phase(tag)
        barrier(ph, after_tag)

        for i in range(NT):
            xa = xr[:, ts(i, D)]
            ph.op("vector", lambda h, a=xa, i=i: h.tensor_reduce(stat(Q_SUMX, i), a, AX, ALU.add))
            ph.op("vector", lambda h, a=xa, i=i: h.tensor_reduce(stat(Q_MX, i), a, AX, ALU.max))
            ph.op("vector", lambda h, a=xa, i=i: h.tensor_reduce(
                stat(Q_MN, i), a, AX, ALU.min) if hasattr(h, "tensor_reduce") else None)
            ph.op("scalar", lambda h, a=xa, i=i: h.activation(
                fbuf[:], a, ACT.Square, accum_out=stat(Q_SUMSQ, i)))
            ph.drain("scalar")
        ph.drain("vector")
        v_st = nop_inc(ph, "vector", sV)
        s_st = nop_inc(ph, "scalar", sS)

        ph.wait("vector", sS, s_st)
        ph.op("vector", lambda h: h.tensor_scalar(stat(Q_M), stat(Q_SUMX), 1.0 / D, None, ALU.mult))
        ph.drain("vector")
        ph.op("vector", lambda h: h.tensor_mul(stat(Q_T0), stat(Q_M), stat(Q_M)))
        ph.op("vector", lambda h: h.tensor_scalar(stat(Q_T1), stat(Q_SUMSQ), 1.0 / D, None, ALU.mult))
        ph.drain("vector")
        ph.op("vector", lambda h: h.tensor_sub(stat(Q_V), stat(Q_T1), stat(Q_T0)))
        ph.drain("vector")
        ph.op("vector", lambda h: h.tensor_scalar(stat(Q_T0), stat(Q_V), 1.0 + EPS_RMS,
                                                  EPS_RMS * EPS_LN, ALU.mult, ALU.add))
        ph.drain("vector")
        rsqrt_chain(ph, stat(Q_T0), stat(Q_SC), stat(Q_T1), stat(Q_T2), stat(Q_T3))
        ph.op("vector", lambda h: h.tensor_sub(stat(Q_T0), stat(Q_MX), stat(Q_M)))
        ph.op("vector", lambda h: h.tensor_sub(stat(Q_T1), stat(Q_M), stat(Q_MN)))
        ph.drain("vector")
        ph.op("vector", lambda h: h.tensor_max(stat(Q_T0), stat(Q_T0), stat(Q_T1)))
        ph.drain("vector")
        ph.op("vector", lambda h: h.tensor_mul(stat(Q_T0), stat(Q_T0), stat(Q_SC)))
        ph.drain("vector")
        ph.op("vector", lambda h: h.tensor_scalar(stat(Q_XS), stat(Q_T0), 1.0 / QB, None, ALU.mult))
        ph.drain("vector")
        ph.op("vector", lambda h: h.tensor_scalar(stat(Q_T1), stat(Q_XS), EPS_X, None, ALU.add))
        ph.drain("vector")
        ph.op("vector", lambda h: h.reciprocal(stat(Q_T2), stat(Q_T1)))
        ph.drain("vector")
        ph.op("vector", lambda h: h.tensor_mul(stat(Q_KQ), stat(Q_SC), stat(Q_T2)))
        ph.drain("vector")
        ph.op("vector", lambda h: h.tensor_mul(stat(Q_KB), stat(Q_M), stat(Q_KQ)))
        ph.drain("vector")
        ph.op("vector", lambda h: h.tensor_scalar(stat(Q_KB), stat(Q_KB), -1.0, 256.0, ALU.mult, ALU.add))
        ph.drain("vector")
        v_m2 = nop_inc(ph, "vector", sV)
        ph.wait("scalar", sV, v_m2)
        ph.wait("gpsimd", sV, v_m2)

        ev_last = [None]
        mm_by_buf = {0: None, 1: None}
        ohstep = [0]
        ep_hist = []
        st_hist = {0: None, 1: None}
        prev_vq = [None]
        prev_tl = [None]
        for th in range(2):
            gb = []
            s_q = []
            v_q = []
            for ii in range(NT // 2):
                i = th * (NT // 2) + ii
                xa = xr[:, ts(i, D)]
                if prev_vq[0] is not None:
                    ph.wait("scalar", sV, prev_vq[0])
                sq = ph.op("scalar", lambda h, a=xa, i=i: h.activation(
                    tbuf[:], a, ACT.Identity, scale=stat(Q_KQ, i), bias=stat(Q_KB, i)), sem=sS)
                s_q.append(sq)
                ph.wait("vector", sS, sq)
                ph.op("vector", lambda h: h.tensor_copy(aibuf[:], tbuf[:]), sem=sV)
                ph.drain("vector")
                ph.op("vector", lambda h: h.tensor_copy(fbuf[:], aibuf[:]), sem=sV)
                ph.drain("vector")
                ph.op("vector", lambda h: h.scalar_tensor_tensor(
                    tbuf[:], tbuf[:], -0.5, fbuf[:], ALU.add, ALU.is_ge), sem=sV)
                ph.drain("vector")
                if prev_tl[0] is not None:
                    ph.wait("vector", sT, prev_tl[0])
                vq = ph.op("vector", lambda h: h.scalar_tensor_tensor(
                    xqs[:], fbuf[:], -256.0, tbuf[:], ALU.add, ALU.add), sem=sV)
                v_q.append(vq)
                prev_vq[0] = vq
                ph.wait("tensor", sV, vq)
                for kb in range(4):
                    if ev_last[0] is not None:
                        ph.wait("tensor", sV, ev_last[0])
                    for k in range(4):
                        kt = kb * 4 + k
                        t_last = ph.op("tensor", lambda h, k=k, kt=kt: h.transpose(
                            pX[:, ts(k, 128)], xqs[:, ts(kt, 128)], eyeb[:]), sem=sT)
                    prev_tl[0] = t_last
                    ph.wait("vector", sT, t_last)
                    dst = xqT.ap().rearrange("p (k t f) -> p k t f", k=KD, t=NT // 2)[:, ts(kb, 4), ii, :]
                    ev_last[0] = ph.op("vector", lambda h, d=dst: h.tensor_copy(
                        d, pX.ap().rearrange("p (k f) -> p k f", k=4)), sem=sV)
            for ii in range(NT // 2):
                i = th * (NT // 2) + ii
                xa = xr[:, ts(i, D)]
                ph.wait("vector", sS, s_q[ii])
                gb.append(ph.op("vector", lambda h, a=xa: h.tensor_add(
                    a, a, bias_rep[:]), sem=sV))
                ph.drain("vector")

            for oh in range(2):
                buf = ohstep[0] % 2
                ohstep[0] += 1
                wl = []
                for kt in range(KD):
                    pw = []
                    if th == 0 and oh == 0 and kt < len(tern_store):
                        pw.append((tern_store[kt][0], tern_store[kt][1]))
                    if mm_by_buf[buf] is not None and kt == 0:
                        pw.append((sT, mm_by_buf[buf]))
                    wl.append(dmaS.issue(
                        ph, weffh[buf].ap().rearrange("p (k f) -> p k f", k=KD)[:, kt, :],
                        wffn_tern[oh * 1024:(oh + 1) * 1024, ts(kt, 128)],
                        transpose=True, pre_wait=pw or None))
                for (s, v) in wl:
                    ph.wait("tensor", s, v)
                ph.wait("tensor", sV, ev_last[0])
                for ii in range(NT // 2):
                    i = th * (NT // 2) + ii
                    for nb in range(2):
                        pidx = (2 * ii + nb) % 2
                        if len(ep_hist) >= 2:
                            ph.wait("tensor", sV, ep_hist[-2])
                        for kt in range(KD):
                            lhs = xqT.ap().rearrange("p (k t f) -> p k t f", k=KD, t=NT // 2)[:, kt, ii, :]
                            mmv = ph.op("tensor", lambda h, lhs=lhs, kt=kt, nb=nb, pidx=pidx, buf=buf: h.matmul(
                                pM[pidx][:], lhs,
                                weffh[buf].ap().rearrange("p (k f) -> p k f", k=KD)[:, kt, ts(nb, 512)],
                                start=(kt == 0), stop=(kt == KD - 1)), sem=sT)
                        ph.wait("vector", sT, mmv)
                        col0 = i * D + oh * 1024 + nb * 512
                        dst = xr[:, col0: col0 + 512]
                        if write_out:
                            if st_hist[pidx] is not None:
                                ph.wait("vector", st_hist[pidx][0], st_hist[pidx][1])
                            ev = ph.op("vector", lambda h, d=dst, i=i, pidx=pidx: h.scalar_tensor_tensor(
                                fstb[pidx][:], pM[pidx][:], stat(Q_XS, i), d, ALU.mult, ALU.add), sem=sV)
                            st_hist[pidx] = dmaS.issue(
                                ph, x_out[ts(i, 128), oh * 1024 + nb * 512: oh * 1024 + (nb + 1) * 512],
                                fstb[pidx][:], pre_wait=[(sV, ev)])
                        else:
                            ev = ph.op("vector", lambda h, d=dst, i=i, pidx=pidx: h.scalar_tensor_tensor(
                                d, pM[pidx][:], stat(Q_XS, i), d, ALU.mult, ALU.add), sem=sV)
                        ep_hist.append(ev)
                mm_by_buf[buf] = sT.val
        ph.emit(blk); ps_.close()

    emit_ffn("p1", False, "p0_end")
    snap("p1_end")
    if DBG_PHASE == "p1":
        return dbg_finish()

    p2s = contextlib.ExitStack()
    sq2 = sbuf("sq2", [128, D], F32, p2s)
    xn2 = [sbuf(f"xn2_{i}", [128, D], BF16, p2s) for i in range(2)]
    p2 = Phase("p2")
    barrier(p2, "p1_end")
    for i in range(NT):
        xa = xr[:, ts(i, D)]
        p2.op("vector", lambda h, a=xa, i=i: h.tensor_reduce(stat(Q_SUMX, i), a, AX, ALU.add))
        p2.op("scalar", lambda h, a=xa, i=i: h.activation(sq2[:], a, ACT.Square, accum_out=stat(Q_SUMSQ, i)))
        p2.drain("scalar")
    p2.drain("vector")
    v_st2 = nop_inc(p2, "vector", sV)
    s_st2 = nop_inc(p2, "scalar", sS)
    p2.wait("vector", sS, s_st2)
    p2.op("vector", lambda h: h.tensor_scalar(stat(Q_M), stat(Q_SUMX), 1.0 / D, None, ALU.mult))
    p2.drain("vector")
    p2.op("vector", lambda h: h.tensor_mul(stat(Q_T0), stat(Q_M), stat(Q_M)))
    p2.op("vector", lambda h: h.tensor_scalar(stat(Q_T1), stat(Q_SUMSQ), 1.0 / D, None, ALU.mult))
    p2.drain("vector")
    p2.op("vector", lambda h: h.tensor_sub(stat(Q_V), stat(Q_T1), stat(Q_T0)))
    p2.drain("vector")
    rsqrt_chain(p2, stat(Q_V), stat(Q_SC), stat(Q_T1), stat(Q_T2), stat(Q_T3), eps_add=EPS_LN)
    p2.op("vector", lambda h: h.tensor_mul(stat(Q_KB), stat(Q_M), stat(Q_SC)))
    p2.drain("vector")
    p2.op("vector", lambda h: h.tensor_scalar(stat(Q_KB), stat(Q_KB), -1.0, None, ALU.mult))
    p2.drain("vector")
    v_m3 = nop_inc(p2, "vector", sV)
    p2.wait("scalar", sV, v_m3)
    ag_st = []
    for i in range(NT):
        xa = xr[:, ts(i, D)]
        if i >= 2:
            p2.wait("scalar", ag_st[i - 2][0], ag_st[i - 2][1])
        sc = p2.op("scalar", lambda h, a=xa, i=i: h.activation(
            xn2[i % 2][:], a, ACT.Identity, scale=stat(Q_SC, i), bias=stat(Q_KB, i)), sem=sS)
        k = dmaS.i % len(dmaS.sems)
        dmaS.i += 1
        sring = dmaS.sems[k]
        if dmaS.uses[k] > 0:
            p2.wait("sync", sring, 16 * dmaS.uses[k])
        dmaS.uses[k] += 1
        p2.wait("sync", sS, sc)
        op_pair(p2, "sync",
                lambda h, i=i, b=xn2[i % 2]: h.dma_start(out=ag_out[ts(i, 128), :], in_=b[:]),
                lambda h, i=i, b=xn2[i % 2]: h.dma_start(out=ag_out[T + i * 128: T + (i + 1) * 128, :], in_=b[:]),
                sem=sring, n=16)
        ag_st.append((sring, sring.val))
    emit_barrier(p2, 0, ag_st)
    p2.emit(blk); p2s.close()
    snap("p2_end")
    if DBG_PHASE == "p2":
        return dbg_finish()

    p3s = contextlib.ExitStack()
    xnT = sbuf("xnT", [128, KD * 16 * 128], BF16, p3s)
    xg = [sbuf(f"xg{i}", [128, D], BF16, p3s) for i in range(2)]
    wst = [sbuf(f"wst{i}", [128, KD * 128], BF16, p3s) for i in range(2)]
    wg2s = sbuf("wg2s", [128, HL], BF16, p3s)
    g1T = sbuf("g1T", [128, N], BF16, p3s)
    beta_f = sbuf("beta_f", [NHL, N], F32, p3s)
    stq = [sbuf(f"stq{i}", [128, 512], BF16, p3s) for i in range(2)]
    pP = [psum(f"pP{i}", [128, 512], F32, p3s) for i in range(2)]
    pXt = psum("pXt", [128, 512], BF16, p3s)
    pBt = psum("pBt", [128, 128], F32, p3s)

    p3 = Phase("p3")
    barrier(p3, "p2_end")

    ldg2 = dmaG.issue(p3, wg2s[:], wg2_t[:])

    def xnT_blk(kt, n0, n1):
        return xnT.ap().rearrange("p (k f) -> p k f", k=KD)[:, kt, n0:n1]

    wsrc = [("q", wq_t, 0), ("k", wk_t, 1), ("v", wv_t, 2)]
    ev3 = [None]
    xg_tr_done = {0: None, 1: None}
    mm3_done = {}
    wst_free = {0: None, 1: None}
    pP_free = {0: None, 1: None}
    stq_free = {0: None, 1: None}

    for sh in range(2):
        for mt in range(16):
            gmt = sh * 16 + mt
            pw = []
            if mt == 0 and sh == 1:
                v = mm3_done.get((sh - 1, "last"))
                if v:
                    pw.append((sT, v))
            if xg_tr_done[mt % 2] is not None:
                pw.append((sT, xg_tr_done[mt % 2]))
            ld = dmaS.issue(p3, xg[mt % 2][:], ag_out[ts(gmt, 128), :], pre_wait=pw or None)
            p3.wait("tensor", ld[0], ld[1])
            for kb in range(4):
                if ev3[0] is not None:
                    p3.wait("tensor", sV, ev3[0])
                for k in range(4):
                    kt = kb * 4 + k
                    tv = p3.op("tensor", lambda h, k=k, kt=kt, mt=mt: h.transpose(
                        pXt[:, ts(k, 128)], xg[mt % 2][:, ts(kt, 128)], eyeb[:]), sem=sT)
                p3.wait("vector", sT, tv)
                dst = xnT.ap().rearrange("p (k t f) -> p k t f", k=KD, t=16)[:, ts(kb, 4), mt, :]
                ev3[0] = p3.op("vector", lambda h, d=dst: h.tensor_copy(
                    d, pXt.ap().rearrange("p (k f) -> p k f", k=4)), sem=sV)
            xg_tr_done[mt % 2] = tv

        for wi, (wname, wdr, widx) in enumerate(wsrc):
            for mh in range(NHL):
                mb = wi * NHL + mh
                pwv = wst_free[mb % 2]
                ld = dmaG.issue(p3, wst[mb % 2].ap().rearrange("p (k f) -> p k f", k=KD),
                                wdr.rearrange("(k p) f -> p k f", p=128)[:, :, ts(mh, 128)],
                                pre_wait=([(sT, pwv)] if pwv else None))
                p3.wait("tensor", ld[0], ld[1])
                if mb == 0:
                    p3.wait("tensor", sV, ev3[0])
                for nb in range(4):
                    par = nb % 2
                    if pP_free[par] is not None:
                        p3.wait("tensor", pP_free[par][0], pP_free[par][1])
                    for kt in range(KD):
                        mmv = p3.op("tensor", lambda h, kt=kt, nb=nb, mb=mb, par=par: h.matmul(
                            pP[par][:], wst[mb % 2][:, ts(kt, 128)],
                            xnT_blk(kt, nb * 512, (nb + 1) * 512),
                            start=(kt == 0), stop=(kt == KD - 1)), sem=sT)
                    eng, sem = ("scalar", sS) if wname in ("q", "k") else ("vector", sV)
                    p3.wait(eng, sT, mmv)
                    if stq_free[par] is not None:
                        p3.wait(eng, stq_free[par][0], stq_free[par][1])
                    if wname in ("q", "k"):
                        evv = p3.op(eng, lambda h, par=par: h.activation(
                            stq[par][:], pP[par][:], ACT.Silu), sem=sem)
                    else:
                        evv = p3.op(eng, lambda h, par=par: h.tensor_copy(stq[par][:], pP[par][:]), sem=sem)
                    pP_free[par] = (sem, evv)
                    c0 = sh * 16 + nb * 4
                    stv = dmaS.issue(p3, qkv_scan[c0:c0 + 4, :, mh, widx, :].rearrange("c p f -> p c f"),
                                     stq[par].ap().rearrange("p (c f) -> p c f", c=4),
                                     pre_wait=[(sem, evv)])
                    stq_free[par] = stv
                mm3_done[(sh, mb)] = sT.val
                wst_free[mb % 2] = sT.val
        mm3_done[(sh, "last")] = sT.val

        ldb = dmaG.issue(p3, wst[0].ap().rearrange("p (k f) -> p k f", k=KD)[:, :, 0:NHL],
                         wbeta_t.rearrange("(k p) f -> p k f", p=128),
                         pre_wait=[(sT, wst_free[0])])
        p3.wait("tensor", ldb[0], ldb[1])
        for nb in range(4):
            par = nb % 2
            if pP_free[par] is not None:
                p3.wait("tensor", pP_free[par][0], pP_free[par][1])
            for kt in range(KD):
                mmv = p3.op("tensor", lambda h, kt=kt, nb=nb, par=par: h.matmul(
                    pP[par][0:NHL, :], wst[0][:, kt * 128: kt * 128 + NHL],
                    xnT_blk(kt, nb * 512, (nb + 1) * 512),
                    start=(kt == 0), stop=(kt == KD - 1)), sem=sT)
            p3.wait("scalar", sT, mmv)
            scv = p3.op("scalar", lambda h, nb=nb, sh=sh, par=par: h.activation(
                beta_f[:, sh * T + nb * 512: sh * T + (nb + 1) * 512],
                pP[par][0:NHL, :], ACT.Sigmoid), sem=sS)
            pP_free[par] = (sS, scv)
        wst_free[0] = sT.val

        ldg1 = dmaG.issue(p3, wst[1].ap().rearrange("p (k f) -> p k f", k=KD),
                          wg1_t.rearrange("(k p) f -> p k f", p=128),
                          pre_wait=[(sT, wst_free[1])])
        p3.wait("tensor", ldg1[0], ldg1[1])
        for nb in range(4):
            par = nb % 2
            if pP_free[par] is not None:
                p3.wait("tensor", pP_free[par][0], pP_free[par][1])
            for kt in range(KD):
                mmv = p3.op("tensor", lambda h, kt=kt, nb=nb, par=par: h.matmul(
                    pP[par][:], wst[1][:, ts(kt, 128)],
                    xnT_blk(kt, nb * 512, (nb + 1) * 512),
                    start=(kt == 0), stop=(kt == KD - 1)), sem=sT)
            p3.wait("vector", sT, mmv)
            gv = p3.op("vector", lambda h, nb=nb, sh=sh, par=par: h.tensor_copy(
                g1T[:, sh * T + nb * 512: sh * T + (nb + 1) * 512], pP[par][:]), sem=sV)
            pP_free[par] = (sV, gv)
        wst_free[1] = sT.val

    p3.drain("scalar")
    s_bt = nop_inc(p3, "scalar", sS)
    p3.wait("vector", sS, s_bt)
    p3.op("vector", lambda h: h.tensor_scalar(beta_f[:], beta_f[:], 2.0, None, ALU.mult))
    p3.drain("vector")
    v_bt = nop_inc(p3, "vector", sV)
    p3.wait("tensor", sV, v_bt)
    for c in range(NTF):
        tv = p3.op("tensor", lambda h, c=c: h.transpose(
            pBt[:, 0:NHL], beta_f[:, ts(c, 128)], eyef[0:NHL, 0:NHL]), sem=sT)
        p3.wait("vector", sT, tv)
        bv = p3.op("vector", lambda h, c=c: h.tensor_copy(
            beta_tm[:, c * NHL:(c + 1) * NHL], pBt[:, 0:NHL]), sem=sV)
        p3.wait("tensor", sV, bv)

    p3.wait("tensor", ldg2[0], ldg2[1])
    gate_ev = {0: None, 1: None}
    for c in range(NTF):
        for nb in range(2):
            par = (2 * c + nb) % 2
            if gate_ev[par] is not None:
                p3.wait("tensor", gate_ev[par][0], gate_ev[par][1])
            mmv = p3.op("tensor", lambda h, c=c, nb=nb, par=par: h.matmul(
                pP[par][:], g1T[:, ts(c, 128)], wg2s[:, ts(nb, 512)],
                start=True, stop=True), sem=sT)
            p3.wait("scalar", sT, mmv)
            if stq_free[par] is not None:
                p3.wait("scalar", stq_free[par][0], stq_free[par][1])
            scv = p3.op("scalar", lambda h, par=par: h.activation(
                stq[par][:], pP[par][:], ACT.Sigmoid), sem=sS)
            stv = dmaS.issue(p3, gate_dram[c, :, ts(nb, 512)], stq[par][:],
                             pre_wait=[(sS, scv)])
            stq_free[par] = stv
            gate_ev[par] = (sS, scv)
    p3.emit(blk); p3s.close()
    snap("p3_end")
    if DBG_PHASE == "p3":
        return dbg_finish()

    p4s = contextlib.ExitStack()
    woT = sbuf("woT", [128, KHL * D], BF16, p4s)
    wof = [sbuf(f"wof{i}", [128, D], F32, p4s) for i in range(2)]
    S32 = sbuf("S32", [128, NHL * 128], F32, p4s)
    S_bf = sbuf("S_bf", [128, NHL * 128], BF16, p4s)
    qkvb = [sbuf(f"qkvb{i}", [128, NHL * 3 * 128], BF16, p4s) for i in range(2)]
    gtb = [sbuf(f"gtb{i}", [128, HL], BF16, p4s) for i in range(2)]
    scr = sbuf("scr", [128, 42 * 128], BF16, p4s)
    scw = sbuf("scw", [128, NHL * 2 * 128], BF16, p4s)
    gsb = sbuf("gsb", [128, NHL * 128], BF16, p4s)
    scf = sbuf("scf", [128, 128], F32, p4s)
    scf2 = sbuf("scf2", [128, 128], F32, p4s)
    ogb = sbuf("ogb", [128, HL], F32, p4s)
    ogsq = sbuf("ogsq", [128, HL], F32, p4s)
    ogn_s = sbuf("ogn_s", [128, HL], BF16, p4s)
    ognT = [sbuf(f"ognT{i}", [128, NHL * 128], BF16, p4s) for i in range(2)]
    sc8 = sbuf("sc8", [128, 8 * 8], F32, p4s)
    fst4 = [sbuf(f"fst4{i}", [128, 512], F32, p4s) for i in range(2)]

    bankA = [psum(f"pbA{i}", [128, 512], F32, p4s) for i in range(2)]
    bankB = [psum(f"pbB{i}", [128, 512], F32, p4s) for i in range(2)]
    pT4l = [psum(f"pT4{i}", [128, 256], BF16, p4s) for i in range(2)]
    pCmb = [bankA[0][:, 0:384], bankA[1][:, 0:384]]
    pWo = [psum(f"pWo{i}", [128, 512], F32, p4s) for i in range(2)]

    SLOT_NAMES = ["Nd", "Dbn", "P0", "P1", "P2", "P3", "Bd0", "Bd1", "Bd2", "Bd3", "Bd4",
                  "Bo", "LT", "ta", "tb", "v0", "Uf", "Khat", "R", "TG2", "GNs"]

    def slot(par, name):
        i = par * len(SLOT_NAMES) + SLOT_NAMES.index(name)
        return scr[:, i * 128:(i + 1) * 128]

    def s8(q, j=None):
        if j is None:
            return sc8[:, q * 8:(q + 1) * 8]
        return sc8[:, q * 8 + j: q * 8 + j + 1]

    p4 = Phase("p4")
    barrier(p4, "p3_end")

    s_wo = []
    for r in range(KHL):
        ld = dmaS.issue(p4, wof[r % 2][:], wo_t[ts(r, 128), :],
                        pre_wait=([(sS, s_wo[r - 2])] if r >= 2 else None))
        p4.wait("scalar", ld[0], ld[1])
        s_wo.append(p4.op("scalar", lambda h, r=r: h.activation(
            woT[:, ts(r, D)], wof[r % 2][:], ACT.Copy, scale=gnorm_sb[:, r:r + 1]), sem=sS))
    p4.op("vector", lambda h: h.memset(S32[:], 0.0), sem=sV)
    v_z2 = p4.op("vector", lambda h: h.memset(S_bf[:], 0.0), sem=sV)

    rs_stores = []
    last_sbf_v = {bh: (sV, v_z2) for bh in range(NHL)}
    prev_pa_ev = None
    qkv_free = []
    gate_free = []
    wo_done = {0: None, 1: None}
    pWo_free = {0: None, 1: None}
    fst4_free = {0: None, 1: None}
    ognT_ev = [None]
    mix_guard = {0: (sV, v_z2), 1: (sV, v_z2)}
    pt4_guard = {0: None, 1: None}
    u_guard = {0: None, 1: None}

    for c in range(DBG_CHUNKS if DBG_CHUNKS else NTF):
        pwq = [(sT, qkv_free[c - 2])] if c >= 2 else None
        ldq = dmaG.issue(p4, qkvb[c % 2][:],
                         qkv_scan[c].rearrange("p b t f -> p (b t f)"), pre_wait=pwq)
        pwg = [(sV, gate_free[c - 2])] if c >= 2 else None
        ldg = dmaG.issue(p4, gtb[c % 2][:], gate_dram[c, :, :], pre_wait=pwg)

        def kqv(bh, w, c=c):
            return qkvb[c % 2][:, (bh * 3 + w) * 128: (bh * 3 + w + 1) * 128]

        pa_ev = []
        p4.wait("tensor", ldq[0], ldq[1])
        for bh in (range(NHL) if DBG_SCANPART >= 1 else []):
            par = bh % 2
            p4.wait("tensor", last_sbf_v[bh][0], last_sbf_v[bh][1])
            p4.wait("tensor", mix_guard[par][0], mix_guard[par][1])
            guard = pa_ev[-2] if len(pa_ev) >= 2 else (prev_pa_ev[bh - 2] if prev_pa_ev else None)
            if guard:
                p4.wait("tensor", sV, guard[0])
                p4.wait("tensor", sS, guard[1])
            KT = kqv(bh, 1)
            QT = kqv(bh, 0)
            p4.op("tensor", lambda h, KT=KT, par=par: h.matmul(
                pCmb[par][:, 0:128], KT, KT, start=True, stop=True), sem=sT)
            p4.op("tensor", lambda h, KT=KT, par=par, bh=bh: h.matmul(
                pCmb[par][:, 128:256], KT, S_bf[:, ts(bh, 128)], start=True, stop=True), sem=sT)
            tA = p4.op("tensor", lambda h, KT=KT, QT=QT, par=par: h.matmul(
                pCmb[par][:, 256:384], KT, QT, start=True, stop=True), sem=sT)
            p4.wait("vector", sT, tA)
            if DBG_PA >= 2:
                p4.op("vector", lambda h, bh=bh, par=par: h.tensor_copy(
                    gsb[:, ts(bh, 128)], pCmb[par][:, 0:128]), sem=sV)
                p4.op("vector", lambda h, par=par: h.tensor_mul(scf[:], pCmb[par][:, 0:128], eyef[:]), sem=sV)
                p4.drain("vector")
                vA = p4.op("vector", lambda h, bh=bh: h.tensor_reduce(
                    s8(0, bh), scf[:], AX, ALU.add), sem=sV)
                p4.drain("vector")
            else:
                vA = nop_inc(p4, "vector", sV)
            p4.wait("scalar", sT, tA)
            p4.wait("scalar", sV, vA)
            if DBG_PA >= 3:
                p4.op("scalar", lambda h, bh=bh, par=par: h.copy(
                    scw[:, ts(2 * bh, 128)], pCmb[par][:, 128:256]), sem=sS)
                sA = p4.op("scalar", lambda h, bh=bh, par=par: h.copy(
                    scw[:, ts(2 * bh + 1, 128)], pCmb[par][:, 256:384]), sem=sS)
            else:
                sA = nop_inc(p4, "scalar", sS)
            pa_ev.append((vA, sA))
        prev_pa_ev = pa_ev

        if DBG_SCANPART < 2:
            qkv_free.append(sT.val)
            gate_free.append(sV.val)
            continue
        p4.drain("vector")
        rsqrt_chain(p4, s8(0), s8(1), s8(4), s8(5), s8(6), eps_add=EPS_K)
        bsl = beta_tm[:, c * NHL:(c + 1) * NHL]
        p4.op("vector", lambda h, b=bsl: h.tensor_mul(s8(2), s8(1), b))
        p4.drain("vector")
        p4.op("vector", lambda h: h.tensor_scalar(s8(3), s8(2), -1.0, None, ALU.mult))
        p4.drain("vector")
        nop_inc(p4, "vector", sV)

        def pass_b_gen(bh):
            par = bh % 2
            n_ap = s8(1, bh)
            bn_ap = s8(2, bh)
            nbn_ap = s8(3, bh)
            bcol = beta_tm[:, c * NHL + bh: c * NHL + bh + 1]
            KT = kqv(bh, 1)
            QT = kqv(bh, 0)
            VT = kqv(bh, 2)
            scfl = scf if par == 0 else scf2
            mxGN = bankB[par][:, 256:384]
            mxB0 = bankB[par][:, 384:512]
            mxO = mxGN
            mxDS = mxB0
            sqP = bankB[par][:, 0:128]
            sqB = bankB[par][:, 128:256]
            apP = bankA[par][:, 384:512]
            t4V = pT4l[par][:, 0:128]
            t4K = pT4l[par][:, 128:256]

            p4.op("vector", lambda h, par=par, n=n_ap: h.tensor_scalar(
                slot(par, "Nd"), eyef[:], n, None, ALU.mult), sem=sV)
            v_nd = p4.op("vector", lambda h, par=par, b=bn_ap: h.tensor_scalar(
                slot(par, "Dbn"), eyef[:], b, None, ALU.mult), sem=sV)
            p4.wait("tensor", mix_guard[par][0], mix_guard[par][1])
            p4.wait("tensor", sV, v_nd)
            t_gn = p4.op("tensor", lambda h, bh=bh, par=par, o=mxGN: h.matmul(
                o, gsb[:, ts(bh, 128)], slot(par, "Nd"), start=True, stop=True), sem=sT)
            yield
            p4.wait("vector", sT, t_gn)
            v_gns = p4.op("vector", lambda h, par=par, i_=mxGN: h.tensor_copy(slot(par, "GNs"), i_), sem=sV)
            v_p0 = p4.op("vector", lambda h, par=par, nb=nbn_ap, i_=mxGN: h.scalar_tensor_tensor(
                slot(par, "P0"), i_, nb, m_sl[:], ALU.mult, ALU.mult), sem=sV)
            yield
            p4.wait("tensor", sV, v_p0)
            t_b0 = p4.op("tensor", lambda h, par=par, o=mxB0: h.matmul(
                o, slot(par, "GNs"), slot(par, "Dbn"), start=True, stop=True), sem=sT)
            yield
            p4.wait("vector", sT, t_b0)
            p4.op("vector", lambda h, par=par, i_=mxB0: h.scalar_tensor_tensor(
                slot(par, "Bd0"), i_, -1.0, m_su[:], ALU.mult, ALU.mult), sem=sV)
            v_b0 = p4.op("vector", lambda h, par=par, i_=mxB0: h.scalar_tensor_tensor(
                slot(par, "Bo"), i_, -1.0, m_bou[:], ALU.mult, ALU.mult), sem=sV)
            yield
            if pt4_guard[par] is not None:
                p4.wait("tensor", sV, pt4_guard[par])
            p4.op("tensor", lambda h, a=VT, o=t4V: h.transpose(o, a, eyeb[:]), sem=sT)
            t_tr2 = p4.op("tensor", lambda h, a=KT, o=t4K: h.transpose(o, a, eyeb[:]), sem=sT)
            yield
            p4.wait("vector", sT, t_tr2)
            p4.op("vector", lambda h, b=bcol, i_=t4V, o=scfl: h.tensor_scalar(
                o[:], i_, b, None, ALU.mult), sem=sV)
            v_kh = p4.op("vector", lambda h, par=par, n=n_ap, i_=t4K: h.tensor_scalar(
                slot(par, "Khat"), i_, n, None, ALU.mult), sem=sV)
            pt4_guard[par] = v_kh
            p4.drain("vector")
            v_r = p4.op("vector", lambda h, par=par, bh=bh, nb=nbn_ap, o=scfl: h.scalar_tensor_tensor(
                slot(par, "R"), scw[:, ts(2 * bh, 128)], nb, o[:], ALU.mult, ALU.add), sem=sV)
            v_tg2 = p4.op("vector", lambda h, par=par, bh=bh, n=n_ap: h.scalar_tensor_tensor(
                slot(par, "TG2"), scw[:, ts(2 * bh + 1, 128)], n, m_tui[:], ALU.mult, ALU.mult), sem=sV)
            yield
            p4.wait("tensor", sV, v_b0)
            s_sq = None
            for j in range(1, 5):
                if s_sq is not None:
                    p4.wait("tensor", sS, s_sq)
                if j <= 3:
                    p4.op("tensor", lambda h, par=par, j=j, o=sqP: h.matmul(
                        o, slot(par, f"Bd{j-1}"), slot(par, f"P{j-1}"),
                        start=True, stop=True), sem=sT)
                t_sq = p4.op("tensor", lambda h, par=par, j=j, o=sqB: h.matmul(
                    o, slot(par, f"P{j-1}"), slot(par, f"Bd{j-1}"),
                    start=True, stop=True), sem=sT)
                p4.wait("scalar", sT, t_sq)
                if j <= 3:
                    p4.op("scalar", lambda h, par=par, j=j, i_=sqP: h.copy(
                        slot(par, f"P{j}"), i_), sem=sS)
                s_sq = p4.op("scalar", lambda h, par=par, j=j, i_=sqB: h.copy(
                    slot(par, f"Bd{j}"), i_), sem=sS)
                yield
            p4.wait("vector", sS, s_sq)
            v_lt = p4.op("vector", lambda h, par=par: h.tensor_add(
                slot(par, "LT"), eyef[:], slot(par, "Bd4")), sem=sV)
            for j in range(3, -1, -1):
                p4.wait("tensor", sV, v_lt)
                t_lt = p4.op("tensor", lambda h, par=par, j=j, o=apP: h.matmul(
                    o, slot(par, f"P{j}"), slot(par, "LT"), start=True, stop=True), sem=sT)
                p4.wait("vector", sT, t_lt)
                v_lt = p4.op("vector", lambda h, par=par, i_=apP: h.scalar_tensor_tensor(
                    slot(par, "LT"), i_, 1.0, slot(par, "LT"), ALU.bypass, ALU.add), sem=sV)
                yield
            p4.wait("tensor", sV, v_lt)
            p4.wait("tensor", sV, v_r)
            t_v0 = p4.op("tensor", lambda h, par=par, o=apP: h.matmul(
                o, slot(par, "LT"), slot(par, "R"), start=True, stop=True), sem=sT)
            p4.wait("scalar", sT, t_v0)
            s_v0 = p4.op("scalar", lambda h, par=par, i_=apP: h.copy(slot(par, "v0"), i_), sem=sS)
            yield
            hcur = "v0"
            ev_h = (sS, s_v0)
            for stage in range(3):
                p4.wait("tensor", ev_h[0], ev_h[1])
                t_z = p4.op("tensor", lambda h, par=par, hcur=hcur, o=apP: h.matmul(
                    o, slot(par, "Bo"), slot(par, hcur), start=True, stop=True), sem=sT)
                p4.wait("scalar", sT, t_z)
                s_z = p4.op("scalar", lambda h, par=par, i_=apP: h.copy(slot(par, "tb"), i_), sem=sS)
                yield
                p4.wait("tensor", sS, s_z)
                t_w = p4.op("tensor", lambda h, par=par, o=apP: h.matmul(
                    o, slot(par, "LT"), slot(par, "tb"), start=True, stop=True), sem=sT)
                dstn = "ta" if stage < 2 else "Uf"
                if dstn == "Uf" and u_guard[par] is not None:
                    p4.wait("vector", sT, u_guard[par])
                p4.wait("vector", sT, t_w)
                v_h = p4.op("vector", lambda h, par=par, dstn=dstn, i_=apP: h.scalar_tensor_tensor(
                    slot(par, dstn), i_, 1.0, slot(par, "v0"), ALU.bypass, ALU.add), sem=sV)
                ev_h = (sV, v_h)
                hcur = dstn
                yield
            U = hcur
            p4.wait("tensor", ev_h[0], ev_h[1])
            p4.wait("tensor", sV, v_tg2)
            p4.op("tensor", lambda h, QT=QT, bh=bh, o=mxO: h.matmul(
                o, QT, S_bf[:, ts(bh, 128)], start=True, stop=False), sem=sT)
            t_o = p4.op("tensor", lambda h, par=par, U=U, o=mxO: h.matmul(
                o, slot(par, "TG2"), slot(par, U), start=False, stop=True), sem=sT)
            t_ds = p4.op("tensor", lambda h, par=par, U=U, bh=bh, o=mxDS: h.matmul(
                o, slot(par, "Khat"), slot(par, U), start=True, stop=True), sem=sT)
            u_guard[par] = t_ds
            p4.wait("vector", sT, t_o)
            p4.wait("vector", ldg[0], ldg[1])
            p4.op("vector", lambda h, bh=bh, c=c, i_=mxO: h.scalar_tensor_tensor(
                ogb[:, ts(bh, 128)], i_, 1.0, gtb[c % 2][:, ts(bh, 128)],
                ALU.bypass, ALU.mult), sem=sV)
            p4.wait("vector", sT, t_ds)
            v_s32 = p4.op("vector", lambda h, bh=bh, i_=mxDS: h.tensor_add(
                S32[:, ts(bh, 128)], S32[:, ts(bh, 128)], i_), sem=sV)
            p4.wait("scalar", sV, v_s32)
            last_sbf_v[bh] = (sS, p4.op("scalar", lambda h, bh=bh: h.copy(
                S_bf[:, ts(bh, 128)], S32[:, ts(bh, 128)]), sem=sS))
            mix_guard[par] = last_sbf_v[bh]

        if DBG_SCANPART >= 3:
            for bh0 in range(0, NHL, 2):
                gens = [pass_b_gen(bh0), pass_b_gen(bh0 + 1)]
                live = [True, True]
                while any(live):
                    for gi in range(2):
                        if live[gi]:
                            try:
                                next(gens[gi])
                            except StopIteration:
                                live[gi] = False
        qkv_free.append(sT.val)

        if DBG_SCANPART < 6:
            gate_free.append(sV.val)
            continue
        p4.drain("vector")
        p4.op("vector", lambda h: h.tensor_mul(ogsq[:], ogb[:], ogb[:]), sem=sV)
        p4.drain("vector")
        p4.op("vector", lambda h: h.tensor_reduce(
            s8(4).rearrange("p (a o) -> p a o", o=1),
            ogsq.ap().rearrange("p (a f) -> p a f", a=NHL), AX, ALU.add), sem=sV)
        p4.drain("vector")
        p4.op("vector", lambda h: h.tensor_scalar(s8(4), s8(4), 1.0 / HD, None, ALU.mult))
        p4.drain("vector")
        rsqrt_chain(p4, s8(4), s8(5), s8(0), s8(6), s8(7), eps_add=EPS_OG)
        iv = s8(5)
        inv_b = bass.AP(iv.tensor, iv.offset, [iv.ap[0], list(iv.ap[1])[:1] + [NHL] if False else [1, NHL], [0, 128]])
        v_ogn = p4.op("vector", lambda h, ib=inv_b: h.tensor_mul(
            ogn_s.ap().rearrange("p (a f) -> p a f", a=NHL),
            ogb.ap().rearrange("p (a f) -> p a f", a=NHL), ib), sem=sV)
        gate_free.append(v_ogn)
        p4.wait("tensor", sV, v_ogn)
        for _par in (0, 1):
            if pt4_guard[_par] is not None:
                p4.wait("tensor", sV, pt4_guard[_par])
        if wo_done[c % 2] is not None:
            p4.wait("vector", sT, wo_done[c % 2])
        for hb2 in range(4):
            if ognT_ev[0] is not None:
                p4.wait("tensor", sV, ognT_ev[0])
            p4.op("tensor", lambda h, hb2=hb2: h.transpose(
                pT4l[0][:, 0:128], ogn_s[:, ts(2 * hb2, 128)], eyeb[:]), sem=sT)
            tv2 = p4.op("tensor", lambda h, hb2=hb2: h.transpose(
                pT4l[1][:, 0:128], ogn_s[:, ts(2 * hb2 + 1, 128)], eyeb[:]), sem=sT)
            p4.wait("vector", sT, tv2)
            p4.op("vector", lambda h, hb2=hb2, c=c: h.tensor_copy(
                ognT[c % 2][:, hb2 * 256: hb2 * 256 + 128], pT4l[0][:, 0:128]), sem=sV)
            ognT_ev[0] = p4.op("vector", lambda h, hb2=hb2, c=c: h.tensor_copy(
                ognT[c % 2][:, hb2 * 256 + 128:(hb2 + 1) * 256], pT4l[1][:, 0:128]), sem=sV)
        p4.wait("tensor", sV, ognT_ev[0])
        pt4_guard[0] = ognT_ev[0]
        pt4_guard[1] = ognT_ev[0]
        for nb in range(4):
            par = nb % 2
            if pWo_free[par] is not None:
                p4.wait("tensor", pWo_free[par][0], pWo_free[par][1])
            for kt in range(KHL):
                t_wo = p4.op("tensor", lambda h, nb=nb, kt=kt, c=c, par=par: h.matmul(
                    pWo[par][:], ognT[c % 2][:, ts(kt, 128)],
                    woT[:, kt * D + nb * 512: kt * D + (nb + 1) * 512],
                    start=(kt == 0), stop=(kt == KHL - 1)), sem=sT)
            eng, sem = ("scalar", sS) if par == 0 else ("vector", sV)
            p4.wait(eng, sT, t_wo)
            if fst4_free[par] is not None:
                p4.wait(eng, fst4_free[par][0], fst4_free[par][1])
            if eng == "scalar":
                e_wo = p4.op(eng, lambda h, par=par: h.copy(fst4[par][:], pWo[par][:]), sem=sem)
            else:
                e_wo = p4.op(eng, lambda h, par=par: h.tensor_copy(fst4[par][:], pWo[par][:]), sem=sem)
            pWo_free[par] = (sem, e_wo)
            k_ = dmaS.i % len(dmaS.sems)
            dmaS.i += 1
            sring = dmaS.sems[k_]
            if dmaS.uses[k_] > 0:
                p4.wait("sync", sring, 16 * dmaS.uses[k_])
            dmaS.uses[k_] += 1
            p4.wait("sync", sem, e_wo)
            op_pair(p4, "sync",
                    lambda h, c=c, nb=nb, par=par: h.dma_start(
                        out=rs_sh[0, ts(c, 128), ts(nb, 512)], in_=fst4[par][:]),
                    lambda h, c=c, nb=nb, par=par: h.dma_start(
                        out=rs_sh[1, ts(c, 128), ts(nb, 512)], in_=fst4[par][:]),
                    sem=sring, n=16)
            stv = (sring, sring.val)
            fst4_free[par] = stv
            rs_stores.append(stv)
        wo_done[c % 2] = sT.val

    if not DBG_CHUNKS:
        emit_barrier(p4, 1, [(s, 16 * u) for s, u in zip(dmaS.sems, dmaS.uses) if u])
    p4.emit(blk); p4s.close()
    snap("p4_end")

    if DBG_CHUNKS:
        pD = Phase("pD")
        for se, u in zip(dmaS.sems, dmaS.uses):
            if u:
                pD.wait("sync", se, 16 * u)
        for se, u in zip(dmaG.sems, dmaG.uses):
            if u:
                pD.wait("gpsimd", se, 16 * u)
        pD.emit(blk)
        st.close()
        return nc

    p5s = contextlib.ExitStack()
    rsb = [sbuf(f"rsb{i}", [128, D], F32, p5s) for i in range(4)]
    p5 = Phase("p5")
    barrier(p5, "p4_end")
    v_add = []
    for i in range(NT):
        for sl_ in range(2):
            bi = (2 * i + sl_) % 4
            k_ = dmaS.i % len(dmaS.sems)
            dmaS.i += 1
            sring = dmaS.sems[k_]
            if dmaS.uses[k_] > 0:
                p5.wait("sync", sring, 16 * dmaS.uses[k_])
            dmaS.uses[k_] += 1
            idx = 2 * i + sl_
            if idx >= 4:
                p5.wait("sync", sV, v_add[idx - 4])
            op_pair(p5, "sync",
                    lambda h, i=i, sl_=sl_, bi=bi: h.dma_start(
                        out=rsb[bi][:], in_=rs_sh[sl_, ts(i, 128), :]),
                    lambda h, i=i, sl_=sl_, bi=bi: h.dma_start(
                        out=rsb[bi][:], in_=rs_sh[sl_, T + i * 128: T + (i + 1) * 128, :]),
                    sem=sring, n=16)
            p5.wait("vector", sring, sring.val)
        xa = xr[:, ts(i, D)]
        v_add.append(p5.op("vector", lambda h, a=xa, bi0=(2 * i) % 4: h.tensor_add(
            a, a, rsb[bi0][:]), sem=sV))
        p5.drain("vector")
        v_add.append(p5.op("vector", lambda h, a=xa, bi1=(2 * i + 1) % 4: h.tensor_add(
            a, a, rsb[bi1][:]), sem=sV))
        p5.drain("vector")
    p5.emit(blk); p5s.close()
    snap("p5_end")

    emit_ffn("p6", True, "p5_end")
    snap("p6_end")

    pFin = Phase("pF")
    for s, u in zip(dmaS.sems, dmaS.uses):
        if u:
            pFin.wait("sync", s, 16 * u)
    for s, u in zip(dmaG.sems, dmaG.uses):
        if u:
            pFin.wait("gpsimd", s, 16 * u)
    pFin.emit(blk)

    st.close()
    return nc


_NC_CACHE = None


def kernel(**inputs):
    global _NC_CACHE
    x = np.ascontiguousarray(np.asarray(inputs["L0_attn_x"], dtype=np.float32))
    Wq = np.asarray(inputs["L0_attn_Wq"], dtype=np.float32)
    Wk = np.asarray(inputs["L0_attn_Wk"], dtype=np.float32)
    Wv = np.asarray(inputs["L0_attn_Wv"], dtype=np.float32)
    Wbeta = np.asarray(inputs["L0_attn_Wbeta"], dtype=np.float32)
    Wg1 = np.asarray(inputs["L0_attn_Wg1"], dtype=np.float32)
    Wg2 = np.asarray(inputs["L0_attn_Wg2"], dtype=np.float32)
    g_norm = np.asarray(inputs["L0_attn_g_norm"], dtype=np.float32)
    Wo = np.asarray(inputs["L0_attn_Wo"], dtype=np.float32)
    Wffn = np.asarray(inputs["L0_ffn_weight"], dtype=np.float32)
    fb = np.asarray(inputs["L0_ffn_bias"], dtype=np.float32)

    wq_T = np.ascontiguousarray(Wq.T)
    wk_T = np.ascontiguousarray(Wk.T)
    wv_T = np.ascontiguousarray(Wv.T)
    wb_T = np.ascontiguousarray(Wbeta.T)
    wg1_T = np.ascontiguousarray(Wg1.T)
    wg2_T = np.ascontiguousarray(Wg2.T)
    wo_T = np.ascontiguousarray(Wo.T)

    eye_bf = np.eye(128, dtype=ml_dtypes.bfloat16)
    eye_f = np.eye(128, dtype=np.float32)
    idx = np.arange(128)
    same_blk = (idx[:, None] // 32) == (idx[None, :] // 32)
    msl = ((idx[:, None] > idx[None, :]) & same_blk).astype(np.float32)
    msu = ((idx[:, None] < idx[None, :]) & same_blk).astype(np.float32)
    mbou = ((idx[:, None] < idx[None, :]) & ~same_blk).astype(np.float32)
    mtui = (idx[:, None] <= idx[None, :]).astype(np.float32)

    if _NC_CACHE is None:
        _NC_CACHE = build_program()
    nc = _NC_CACHE

    nonce = int(np.random.randint(1, 2**30))
    flagv = np.empty((2, 128), dtype=np.int32)
    flagv[0, :] = nonce + 1
    flagv[1, :] = nonce + 2

    in_maps = []
    for c in range(8):
        b, h = c // 2, c % 2
        hs = slice(h * HL, (h + 1) * HL)
        in_maps.append({
            "x_in": np.ascontiguousarray(x[b, h * T:(h + 1) * T, :]),
            "wq_t": np.ascontiguousarray(wq_T[:, hs]),
            "wk_t": np.ascontiguousarray(wk_T[:, hs]),
            "wv_t": np.ascontiguousarray(wv_T[:, hs]),
            "wbeta_t": np.ascontiguousarray(wb_T[:, h * NHL:(h + 1) * NHL]),
            "wg1_t": wg1_T,
            "wg2_t": np.ascontiguousarray(wg2_T[:, hs]),
            "wo_t": np.ascontiguousarray(wo_T[hs, :]),
            "gnorm_c": np.ascontiguousarray(g_norm[hs]),
            "wffn": Wffn,
            "fbias": fb,
            "eye_bfd": eye_bf,
            "eye_fd": eye_f,
            "msl_d": msl,
            "msu_d": msu,
            "mtui_d": mtui,
            "mbou_d": mbou,
            "flagv": flagv,
        })

    res = run_bass_kernel_spmd(nc, in_maps, core_ids=list(range(8)))
    out = np.empty((B, N, D), dtype=np.float32)
    for c in range(8):
        b, h = c // 2, c % 2
        out[b, h * T:(h + 1) * T, :] = res.results[c]["x_out"]
    return out


# version 13 (final)
# speedup vs baseline: 1.7767x; 1.6543x over previous
import contextlib
import os
import numpy as np
import ml_dtypes
import concourse.bass as bass
import concourse.mybir as mybir
from concourse.bass_utils import run_bass_kernel_spmd
from concourse.alu_op_type import AluOpType as RALU

F32 = mybir.dt.float32
BF16 = mybir.dt.bfloat16
AX = mybir.AxisListType.X
ALU = mybir.AluOpType
ACT = mybir.ActivationFunctionType
ts = bass.ts

B, N, D = 4, 4096, 2048
T = 2048
NT = T // 128
NTF = N // 128
KD = D // 128
NHL, HD = 8, 128
HL = NHL * HD
KHL = HL // 128

EPS_LN = 1e-5
EPS_RMS = 1e-8
EPS_G = 1e-8
EPS_X = 1e-8
EPS_K = 1e-6
EPS_OG = 1e-6
QB = 127.0
USE_DRAIN = os.environ.get("KDBG_DRAIN", "1") == "1"
DBG_CHUNKS = int(os.environ.get("KDBG_CHUNKS", "0") or 0)
DBG_PHASE = os.environ.get("KDBG_PHASE", "")
DBG_SCANPART = int(os.environ.get("KDBG_SCANPART", "9") or 9)
DBG_PA = int(os.environ.get("KDBG_PA", "9") or 9)

ENGINES = ("sync", "gpsimd", "vector", "scalar", "tensor")


class Sem:
    def __init__(self, h, name):
        self.h = h
        self.name = name
        self.val = 0

    def inc(self, n=1):
        self.val += n
        return self.val


class Phase:

    def __init__(self, name):
        self.name = name
        self.ops = {e: [] for e in ENGINES}

    def op(self, eng, fn, sem=None, n=1):
        if sem is None:
            self.ops[eng].append(fn)
            return None
        v = sem.inc(n)
        self.ops[eng].append(lambda h, fn=fn, sem=sem, n=n: fn(h).then_inc(sem.h, n))
        return v

    def wait(self, eng, sem, val):
        if val:
            self.ops[eng].append(lambda h, s=sem, v=val: h.wait_ge(s.h, v))

    def drain(self, eng):
        if eng == "vector" and not USE_DRAIN:
            return
        self.ops[eng].append(lambda h: h.drain())

    def emit(self, blk):
        for e in ENGINES:
            lst = self.ops[e]
            if not lst:
                continue

            def body(h, lst=lst):
                for f in lst:
                    f(h)
            getattr(blk, e)(body)


class Dma:

    def __init__(self, sems, queue_eng):
        self.sems = sems
        self.uses = [0] * len(sems)
        self.i = 0
        self.eng = queue_eng

    def issue(self, ph, out_ap, in_ap, transpose=False, pre_wait=None, slow_ok=False):
        k = self.i % len(self.sems)
        self.i += 1
        s = self.sems[k]
        if self.uses[k] > 0:
            ph.wait(self.eng, s, 16 * self.uses[k])
        self.uses[k] += 1
        if pre_wait:
            for (ws, wv) in pre_wait:
                ph.wait(self.eng, ws, wv)
        v = s.inc(16)
        if transpose:
            ph.ops[self.eng].append(
                lambda h, o=out_ap, i_=in_ap, s=s: h.dma_start_transpose(o, i_).then_inc(s.h, 16))
        else:
            ph.ops[self.eng].append(
                lambda h, o=out_ap, i_=in_ap, s=s, sl=slow_ok: h.dma_start(
                    out=o, in_=i_, allow_slow_non_contiguous=sl).then_inc(s.h, 16))
        return (s, v)


def build_program():
    nc = bass.Bass("TRN2", target_bir_lowering=False,
                   detect_race_conditions=(os.environ.get("KDBG_RACE", "0") == "1"))

    x_in = nc.dram_tensor("x_in", [T, D], F32, kind="ExternalInput").ap()
    wq_t = nc.dram_tensor("wq_t", [D, HL], F32, kind="ExternalInput").ap()
    wk_t = nc.dram_tensor("wk_t", [D, HL], F32, kind="ExternalInput").ap()
    wv_t = nc.dram_tensor("wv_t", [D, HL], F32, kind="ExternalInput").ap()
    wbeta_t = nc.dram_tensor("wbeta_t", [D, NHL], F32, kind="ExternalInput").ap()
    wg1_t = nc.dram_tensor("wg1_t", [D, 128], F32, kind="ExternalInput").ap()
    wg2_t = nc.dram_tensor("wg2_t", [128, HL], F32, kind="ExternalInput").ap()
    wo_t = nc.dram_tensor("wo_t", [HL, D], F32, kind="ExternalInput").ap()
    gnorm_c = nc.dram_tensor("gnorm_c", [HL], F32, kind="ExternalInput").ap()
    wffn = nc.dram_tensor("wffn", [D, D], F32, kind="ExternalInput").ap()
    fbias = nc.dram_tensor("fbias", [D], F32, kind="ExternalInput").ap()
    eye_bfd = nc.dram_tensor("eye_bfd", [128, 128], BF16, kind="ExternalInput").ap()
    eye_fd = nc.dram_tensor("eye_fd", [128, 128], F32, kind="ExternalInput").ap()
    msl_d = nc.dram_tensor("msl_d", [128, 128], F32, kind="ExternalInput").ap()
    msu_d = nc.dram_tensor("msu_d", [128, 128], F32, kind="ExternalInput").ap()
    mtui_d = nc.dram_tensor("mtui_d", [128, 128], F32, kind="ExternalInput").ap()
    mbou_d = nc.dram_tensor("mbou_d", [128, 128], F32, kind="ExternalInput").ap()

    x_out = nc.dram_tensor("x_out", [T, D], F32, kind="ExternalOutput").ap()

    wffn_tern = nc.dram_tensor("wffn_tern", [D, D], BF16, kind="Internal").ap()
    ag_out = nc.dram_tensor("ag_out", [N, D], BF16, kind="Internal", addr_space="Shared").ap()
    qkv_scan = nc.dram_tensor("qkv_scan", [NTF, 128, NHL, 3, 128], BF16, kind="Internal").ap()
    gate_dram = nc.dram_tensor("gate_dram", [NTF, 128, HL], BF16, kind="Internal").ap()
    rs_sh = nc.dram_tensor("rs_sh", [2, N, D], F32, kind="Internal", addr_space="Shared").ap()
    flg = nc.dram_tensor("flg", [2, 128], mybir.dt.int32, kind="Internal", addr_space="Shared").ap()
    flagv = nc.dram_tensor("flagv", [2, 128], mybir.dt.int32, kind="ExternalInput").ap()

    st = contextlib.ExitStack()

    def mksem(name):
        return Sem(st.enter_context(nc.semaphore(name)), name)

    sT = mksem("sT")
    sV = mksem("sV")
    sS = mksem("sS")
    sGc = mksem("sGc")
    sC = mksem("sC")
    dmaS = Dma([mksem(f"dS{i}") for i in range(8)], "sync")
    dmaG = Dma([mksem(f"dG{i}") for i in range(8)], "gpsimd")
    ENG_SEMS = [sT, sV, sS, sGc]

    def sbuf(name, shape, dt=F32, scope=None):
        return (scope or st).enter_context(nc.sbuf_tensor(name, shape, dt))

    def psum(name, shape, dt=F32, scope=None):
        return (scope or st).enter_context(nc.psum_tensor(name, shape, dt))

    xr = sbuf("xr", [128, NT * D], BF16)
    stats = sbuf("stats", [128, 16 * 14], F32)
    bias_rep = sbuf("bias_rep", [128, D], BF16)
    sgam = sbuf("sgam", [128, KD], F32)
    eyeb = sbuf("eyeb", [128, 128], BF16)
    eyef = sbuf("eyef", [128, 128], F32)
    m_sl = sbuf("m_sl", [128, 128], F32)
    m_su = sbuf("m_su", [128, 128], F32)
    m_tui = sbuf("m_tui", [128, 128], F32)
    m_bou = sbuf("m_bou", [128, 128], F32)
    onesr = sbuf("onesr", [1, 128], BF16)
    fbias_sb = sbuf("fbias_sb", [1, D], BF16)
    gnorm_sb = sbuf("gnorm_sb", [128, KHL], F32)
    beta_tm = sbuf("beta_tm", [128, NTF * NHL], F32)
    flv_sb = st.enter_context(nc.sbuf_tensor("flv_sb", [2, 128], mybir.dt.int32))
    fl_sb = st.enter_context(nc.sbuf_tensor("fl_sb", [1, 128], mybir.dt.int32))

    (Q_SUMX, Q_SUMSQ, Q_MX, Q_MN, Q_M, Q_V, Q_SC, Q_KQ, Q_KB, Q_XS,
     Q_T0, Q_T1, Q_T2, Q_T3) = range(14)

    def stat(q, i=None):
        if i is None:
            return stats[:, q * 16: q * 16 + 16]
        return stats[:, q * 16 + i: q * 16 + i + 1]

    phases = []
    snaps = {}

    def snap(tag):
        snaps[tag] = ([(s, s.val) for s in ENG_SEMS + [sC]],
                      [(s, 16 * u) for s, u in zip(dmaS.sems + dmaG.sems, dmaS.uses + dmaG.uses)])

    def barrier(ph, tag):
        engs, dmas = snaps[tag]
        for e in ENGINES:
            for s, v in engs + dmas:
                if v:
                    ph.wait(e, s, v)

    blk = st.enter_context(nc.Block())

    PREG = {}

    def parity_setup(h, eng):
        if eng in PREG:
            return PREG[eng]
        pid = h.to_reg(h.partition_id())
        par = h.alloc_register()
        h.reg_alu(par, pid, 1, RALU.bitwise_and)
        PREG[eng] = par
        return par

    def op_pair(ph, eng, fn_even, fn_odd, sem=None, n=1):
        v = sem.inc(n) if sem is not None else None

        def closure(h, fe=fn_even, fo=fn_odd, eng=eng, sem=sem, n=n):
            par = parity_setup(h, eng)
            with h.If_eq(par, 0):
                ins = fe(h)
                if sem is not None:
                    ins.then_inc(sem.h, n)
            with h.Else():
                ins = fo(h)
                if sem is not None:
                    ins.then_inc(sem.h, n)
        ph.ops[eng].append(closure)
        return v

    poll_sems = [mksem(f"poll{i}") for i in range(2)]

    def emit_barrier(ph, k, pre_dma_sems):
        for (sm, val) in pre_dma_sems:
            ph.wait("gpsimd", sm, val)
        fw = poll_sems[k]

        def closure_all(h, k=k, fw=fw):
            par = parity_setup(h, "gpsimd")
            with h.If_eq(par, 0):
                h.dma_start(out=flg[0:1, :], in_=flv_sb[k:k + 1, :]).then_inc(fw.h, 16)
            with h.Else():
                h.dma_start(out=flg[1:2, :], in_=flv_sb[k:k + 1, :]).then_inc(fw.h, 16)
            h.wait_ge(fw.h, 16)
            with h.register(f"cnt{k}") as cnt, h.register(f"cc{k}") as cc, \
                 h.register(f"vv{k}") as vv, h.register(f"mgk{k}") as mgk:
                h.load(mgk, flv_sb[k:k + 1, 0:1])
                h.reg_mov(cnt, 16)
                h.reg_mov(cc, 1)
                with h.While(cc):
                    h.reg_add(cnt, cnt, 16)
                    with h.If_eq(par, 0):
                        h.dma_start(out=fl_sb[:], in_=flg[1:2, :]).then_inc(fw.h, 16)
                    with h.Else():
                        h.dma_start(out=fl_sb[:], in_=flg[0:1, :]).then_inc(fw.h, 16)
                    h.wait_ge(fw.h, cnt)
                    h.load(vv, fl_sb[0:1, 0:1])
                    h.reg_alu(cc, vv, mgk, RALU.subtract)
        ph.ops["gpsimd"].append(closure_all)
        return nop_inc(ph, "gpsimd", sGc)

    def nop_inc(ph, eng, sem):
        return ph.op(eng, lambda h: h.nop(), sem=sem)

    def rsqrt_chain(ph, src_ap, out_ap, t1, t2, t3, eps_add=None):
        if eps_add is not None:
            ph.op("vector", lambda h, a=src_ap, o=t1, e=eps_add: h.tensor_scalar(o, a, e, None, ALU.add))
            ph.drain("vector")
            base = t1
        else:
            base = src_ap
        vv = nop_inc(ph, "vector", sV)
        ph.wait("scalar", sV, vv)
        sv = ph.op("scalar", lambda h, a=base, o=t2: h.activation(o, a, ACT.Sqrt), sem=sS)
        ph.wait("vector", sS, sv)
        ph.op("vector", lambda h, o=t3, a=t2: h.reciprocal(o, a))
        ph.drain("vector")
        ph.op("vector", lambda h, o=t3, a=base, b=t3: h.tensor_mul(o, a, b))
        ph.drain("vector")
        ph.op("vector", lambda h, o=t3, a=t3, b=t2: h.tensor_add(o, a, b))
        ph.drain("vector")
        ph.op("vector", lambda h, o=t3, a=t3: h.tensor_scalar(o, a, 0.5, None, ALU.mult))
        ph.drain("vector")
        ph.op("vector", lambda h, o=out_ap, a=t3: h.reciprocal(o, a))
        ph.drain("vector")


    def dbg_finish():
        pD = Phase("pDbg")
        for se, u in zip(dmaS.sems, dmaS.uses):
            if u:
                pD.wait("sync", se, 16 * u)
        for se, u in zip(dmaG.sems, dmaG.uses):
            if u:
                pD.wait("gpsimd", se, 16 * u)
        pD.emit(blk)
        st.close()
        return nc

    p0s = contextlib.ExitStack()
    wkA = sbuf("wkA", [128, D], F32, p0s)
    wkB = sbuf("wkB", [128, D], F32, p0s)
    fA = sbuf("fA", [128, D], F32, p0s)
    fB = sbuf("fB", [128, D], F32, p0s)
    stg0 = sbuf("stg0", [128, D], BF16, p0s)
    stg1 = sbuf("stg1", [128, D], BF16, p0s)
    pbc = psum("pbc", [128, 512], F32, p0s)

    p0 = Phase("p0")
    cvals = []
    for sb_t, dr in ((eyeb, eye_bfd), (eyef, eye_fd), (m_sl, msl_d), (m_su, msu_d), (m_tui, mtui_d), (m_bou, mbou_d)):
        cvals.append(dmaS.issue(p0, sb_t[:], dr[:]))
    cvals.append(dmaS.issue(p0, gnorm_sb[:, 0:KHL],
                            gnorm_c.rearrange("(a p) -> p a", p=128), slow_ok=True))
    dmaG.issue(p0, xr.ap().rearrange("p (t d) -> p t d", t=NT),
               x_in.rearrange("(t p) d -> p t d", p=128))
    fbl = dmaG.issue(p0, fbias_sb[:], fbias[None, :])
    dmaS.issue(p0, flv_sb[:], flagv[:])

    v_ones = p0.op("vector", lambda h: h.memset(onesr[:], 1.0), sem=sV)

    p0.wait("tensor", fbl[0], fbl[1])
    p0.wait("tensor", sV, v_ones)
    for nb in range(4):
        if nb >= 1:
            p0.wait("tensor", sV, sV.val)
        tb = p0.op("tensor", lambda h, nb=nb: h.matmul(
            pbc[:, 0:512], onesr[:], fbias_sb[:, ts(nb, 512)], start=True, stop=True), sem=sT)
        p0.wait("vector", sT, tb)
        p0.op("vector", lambda h, nb=nb: h.tensor_scalar(
            bias_rep[:, ts(nb, 512)], pbc[:, 0:512], 0.5, None, ALU.mult), sem=sV)

    for (s, v) in cvals:
        p0.wait("vector", s, v)
    v_tern = []
    s_cast = []
    tern_store = []
    for i in range(KD):
        src = wkA if i % 2 == 0 else wkB
        dst = stg0 if i % 2 == 0 else stg1
        ld = dmaG.issue(p0, src[:], wffn[ts(i, 128), :],
                        pre_wait=([(sV, v_tern[i - 2])] if i >= 2 else None))
        p0.wait("vector", ld[0], ld[1])
        p0.op("vector", lambda h, s_=src, i=i: h.tensor_reduce(
            stat(Q_T0, i), s_[:], AX, ALU.add, apply_absolute_value=True))
        p0.drain("vector")
        p0.op("vector", lambda h, i=i: h.tensor_scalar(
            stat(Q_T1, i), stat(Q_T0, i), 1.0 / D, EPS_G, ALU.mult, ALU.add))
        p0.drain("vector")
        p0.op("vector", lambda h, i=i: h.reciprocal(stat(Q_T2, i), stat(Q_T1, i)))
        p0.op("vector", lambda h, i=i: h.tensor_scalar(
            sgam[:, i:i + 1], stat(Q_T0, i), 0.5 / D, None, ALU.mult))
        p0.drain("vector")
        if i >= 1:
            p0.wait("vector", sS, s_cast[i - 1])
        p0.op("vector", lambda h, s_=src, i=i: h.tensor_scalar(
            fA[:], s_[:], stat(Q_T2, i), 0.5, ALU.mult, ALU.is_gt))
        p0.op("vector", lambda h, s_=src, i=i: h.tensor_scalar(
            fB[:], s_[:], stat(Q_T2, i), -0.5, ALU.mult, ALU.is_lt))
        p0.drain("vector")
        v_tern.append(p0.op("vector", lambda h: h.tensor_sub(fA[:], fA[:], fB[:]), sem=sV))
        p0.wait("scalar", sV, v_tern[i])
        if i >= 2:
            p0.wait("scalar", tern_store[i - 2][0], tern_store[i - 2][1])
        s_cast.append(p0.op("scalar", lambda h, d=dst, i=i: h.activation(
            d[:], fA[:], ACT.Copy, scale=sgam[:, i:i + 1]), sem=sS))
        tern_store.append(dmaS.issue(p0, wffn_tern[ts(i, 128), :], dst[:],
                                     pre_wait=[(sS, s_cast[i])]))

    p0.emit(blk); p0s.close()
    snap("p0_end")
    if DBG_PHASE == "p0":
        return dbg_finish()

    def emit_ffn(tag, write_out, after_tag):
        ps_ = contextlib.ExitStack()
        tbuf = sbuf(f"tbuf_{tag}", [128, D], F32, ps_)
        fbuf = sbuf(f"fbuf_{tag}", [128, D], F32, ps_)
        aibuf = ps_.enter_context(nc.sbuf_tensor(f"ai_{tag}", [128, D], mybir.dt.int32))
        xqs = sbuf(f"xqs_{tag}", [128, D], BF16, ps_)
        xqT = sbuf(f"xqT_{tag}", [128, (NT // 2) * D], BF16, ps_)
        weffh = [sbuf(f"weffh{i}_{tag}", [128, KD * 1024], BF16, ps_) for i in range(2)]
        fstb = [sbuf(f"fst{i}_{tag}", [128, 512], F32, ps_) for i in range(2)]
        pM = [psum(f"pM{i}_{tag}", [128, 512], F32, ps_) for i in range(2)]
        pX = psum(f"pX_{tag}", [128, 512], BF16, ps_)

        ph = Phase(tag)
        barrier(ph, after_tag)

        for i in range(NT):
            xa = xr[:, ts(i, D)]
            ph.op("vector", lambda h, a=xa, i=i: h.tensor_reduce(stat(Q_SUMX, i), a, AX, ALU.add))
            ph.op("vector", lambda h, a=xa, i=i: h.tensor_reduce(stat(Q_MX, i), a, AX, ALU.max))
            ph.op("vector", lambda h, a=xa, i=i: h.tensor_reduce(
                stat(Q_MN, i), a, AX, ALU.min) if hasattr(h, "tensor_reduce") else None)
            ph.op("scalar", lambda h, a=xa, i=i: h.activation(
                fbuf[:], a, ACT.Square, accum_out=stat(Q_SUMSQ, i)))
            ph.drain("scalar")
        ph.drain("vector")
        v_st = nop_inc(ph, "vector", sV)
        s_st = nop_inc(ph, "scalar", sS)

        ph.wait("vector", sS, s_st)
        ph.op("vector", lambda h: h.tensor_scalar(stat(Q_M), stat(Q_SUMX), 1.0 / D, None, ALU.mult))
        ph.drain("vector")
        ph.op("vector", lambda h: h.tensor_mul(stat(Q_T0), stat(Q_M), stat(Q_M)))
        ph.op("vector", lambda h: h.tensor_scalar(stat(Q_T1), stat(Q_SUMSQ), 1.0 / D, None, ALU.mult))
        ph.drain("vector")
        ph.op("vector", lambda h: h.tensor_sub(stat(Q_V), stat(Q_T1), stat(Q_T0)))
        ph.drain("vector")
        ph.op("vector", lambda h: h.tensor_scalar(stat(Q_T0), stat(Q_V), 1.0 + EPS_RMS,
                                                  EPS_RMS * EPS_LN, ALU.mult, ALU.add))
        ph.drain("vector")
        rsqrt_chain(ph, stat(Q_T0), stat(Q_SC), stat(Q_T1), stat(Q_T2), stat(Q_T3))
        ph.op("vector", lambda h: h.tensor_sub(stat(Q_T0), stat(Q_MX), stat(Q_M)))
        ph.op("vector", lambda h: h.tensor_sub(stat(Q_T1), stat(Q_M), stat(Q_MN)))
        ph.drain("vector")
        ph.op("vector", lambda h: h.tensor_max(stat(Q_T0), stat(Q_T0), stat(Q_T1)))
        ph.drain("vector")
        ph.op("vector", lambda h: h.tensor_mul(stat(Q_T0), stat(Q_T0), stat(Q_SC)))
        ph.drain("vector")
        ph.op("vector", lambda h: h.tensor_scalar(stat(Q_XS), stat(Q_T0), 1.0 / QB, None, ALU.mult))
        ph.drain("vector")
        ph.op("vector", lambda h: h.tensor_scalar(stat(Q_T1), stat(Q_XS), EPS_X, None, ALU.add))
        ph.drain("vector")
        ph.op("vector", lambda h: h.reciprocal(stat(Q_T2), stat(Q_T1)))
        ph.drain("vector")
        ph.op("vector", lambda h: h.tensor_mul(stat(Q_KQ), stat(Q_SC), stat(Q_T2)))
        ph.drain("vector")
        ph.op("vector", lambda h: h.tensor_mul(stat(Q_KB), stat(Q_M), stat(Q_KQ)))
        ph.drain("vector")
        ph.op("vector", lambda h: h.tensor_scalar(stat(Q_KB), stat(Q_KB), -1.0, 256.0, ALU.mult, ALU.add))
        ph.drain("vector")
        v_m2 = nop_inc(ph, "vector", sV)
        ph.wait("scalar", sV, v_m2)
        ph.wait("gpsimd", sV, v_m2)

        ev_last = [None]
        mm_by_buf = {0: None, 1: None}
        ohstep = [0]
        ep_hist = []
        st_hist = {0: None, 1: None}
        prev_vq = [None]
        prev_tl = [None]
        for th in range(2):
            gb = []
            s_q = []
            v_q = []
            for ii in range(NT // 2):
                i = th * (NT // 2) + ii
                xa = xr[:, ts(i, D)]
                if prev_vq[0] is not None:
                    ph.wait("scalar", sV, prev_vq[0])
                sq = ph.op("scalar", lambda h, a=xa, i=i: h.activation(
                    tbuf[:], a, ACT.Identity, scale=stat(Q_KQ, i), bias=stat(Q_KB, i)), sem=sS)
                s_q.append(sq)
                ph.wait("vector", sS, sq)
                ph.op("vector", lambda h: h.tensor_copy(aibuf[:], tbuf[:]), sem=sV)
                ph.drain("vector")
                ph.op("vector", lambda h: h.tensor_copy(fbuf[:], aibuf[:]), sem=sV)
                ph.drain("vector")
                ph.op("vector", lambda h: h.scalar_tensor_tensor(
                    tbuf[:], tbuf[:], -0.5, fbuf[:], ALU.add, ALU.is_ge), sem=sV)
                ph.drain("vector")
                if prev_tl[0] is not None:
                    ph.wait("vector", sT, prev_tl[0])
                vq = ph.op("vector", lambda h: h.scalar_tensor_tensor(
                    xqs[:], fbuf[:], -256.0, tbuf[:], ALU.add, ALU.add), sem=sV)
                v_q.append(vq)
                prev_vq[0] = vq
                ph.wait("tensor", sV, vq)
                for kb in range(4):
                    if ev_last[0] is not None:
                        ph.wait("tensor", sV, ev_last[0])
                    for k in range(4):
                        kt = kb * 4 + k
                        t_last = ph.op("tensor", lambda h, k=k, kt=kt: h.transpose(
                            pX[:, ts(k, 128)], xqs[:, ts(kt, 128)], eyeb[:]), sem=sT)
                    prev_tl[0] = t_last
                    ph.wait("vector", sT, t_last)
                    dst = xqT.ap().rearrange("p (k t f) -> p k t f", k=KD, t=NT // 2)[:, ts(kb, 4), ii, :]
                    ev_last[0] = ph.op("vector", lambda h, d=dst: h.tensor_copy(
                        d, pX.ap().rearrange("p (k f) -> p k f", k=4)), sem=sV)
            for ii in range(NT // 2):
                i = th * (NT // 2) + ii
                xa = xr[:, ts(i, D)]
                ph.wait("vector", sS, s_q[ii])
                gb.append(ph.op("vector", lambda h, a=xa: h.tensor_add(
                    a, a, bias_rep[:]), sem=sV))
                ph.drain("vector")

            for oh in range(2):
                buf = ohstep[0] % 2
                ohstep[0] += 1
                wl = []
                for kt in range(KD):
                    pw = []
                    if th == 0 and oh == 0 and kt < len(tern_store):
                        pw.append((tern_store[kt][0], tern_store[kt][1]))
                    if mm_by_buf[buf] is not None and kt == 0:
                        pw.append((sT, mm_by_buf[buf]))
                    wl.append(dmaS.issue(
                        ph, weffh[buf].ap().rearrange("p (k f) -> p k f", k=KD)[:, kt, :],
                        wffn_tern[oh * 1024:(oh + 1) * 1024, ts(kt, 128)],
                        transpose=True, pre_wait=pw or None))
                for (s, v) in wl:
                    ph.wait("tensor", s, v)
                ph.wait("tensor", sV, ev_last[0])
                for ii in range(NT // 2):
                    i = th * (NT // 2) + ii
                    for nb in range(2):
                        pidx = (2 * ii + nb) % 2
                        if len(ep_hist) >= 2:
                            ph.wait("tensor", sV, ep_hist[-2])
                        for kt in range(KD):
                            lhs = xqT.ap().rearrange("p (k t f) -> p k t f", k=KD, t=NT // 2)[:, kt, ii, :]
                            mmv = ph.op("tensor", lambda h, lhs=lhs, kt=kt, nb=nb, pidx=pidx, buf=buf: h.matmul(
                                pM[pidx][:], lhs,
                                weffh[buf].ap().rearrange("p (k f) -> p k f", k=KD)[:, kt, ts(nb, 512)],
                                start=(kt == 0), stop=(kt == KD - 1)), sem=sT)
                        ph.wait("vector", sT, mmv)
                        col0 = i * D + oh * 1024 + nb * 512
                        dst = xr[:, col0: col0 + 512]
                        if write_out:
                            if st_hist[pidx] is not None:
                                ph.wait("vector", st_hist[pidx][0], st_hist[pidx][1])
                            ev = ph.op("vector", lambda h, d=dst, i=i, pidx=pidx: h.scalar_tensor_tensor(
                                fstb[pidx][:], pM[pidx][:], stat(Q_XS, i), d, ALU.mult, ALU.add), sem=sV)
                            st_hist[pidx] = dmaS.issue(
                                ph, x_out[ts(i, 128), oh * 1024 + nb * 512: oh * 1024 + (nb + 1) * 512],
                                fstb[pidx][:], pre_wait=[(sV, ev)])
                        else:
                            ev = ph.op("vector", lambda h, d=dst, i=i, pidx=pidx: h.scalar_tensor_tensor(
                                d, pM[pidx][:], stat(Q_XS, i), d, ALU.mult, ALU.add), sem=sV)
                        ep_hist.append(ev)
                mm_by_buf[buf] = sT.val
        ph.emit(blk); ps_.close()

    emit_ffn("p1", False, "p0_end")
    snap("p1_end")
    if DBG_PHASE == "p1":
        return dbg_finish()

    p2s = contextlib.ExitStack()
    sq2 = sbuf("sq2", [128, D], F32, p2s)
    xn2 = [sbuf(f"xn2_{i}", [128, D], BF16, p2s) for i in range(2)]
    p2 = Phase("p2")
    barrier(p2, "p1_end")
    for i in range(NT):
        xa = xr[:, ts(i, D)]
        p2.op("vector", lambda h, a=xa, i=i: h.tensor_reduce(stat(Q_SUMX, i), a, AX, ALU.add))
        p2.op("scalar", lambda h, a=xa, i=i: h.activation(sq2[:], a, ACT.Square, accum_out=stat(Q_SUMSQ, i)))
        p2.drain("scalar")
    p2.drain("vector")
    v_st2 = nop_inc(p2, "vector", sV)
    s_st2 = nop_inc(p2, "scalar", sS)
    p2.wait("vector", sS, s_st2)
    p2.op("vector", lambda h: h.tensor_scalar(stat(Q_M), stat(Q_SUMX), 1.0 / D, None, ALU.mult))
    p2.drain("vector")
    p2.op("vector", lambda h: h.tensor_mul(stat(Q_T0), stat(Q_M), stat(Q_M)))
    p2.op("vector", lambda h: h.tensor_scalar(stat(Q_T1), stat(Q_SUMSQ), 1.0 / D, None, ALU.mult))
    p2.drain("vector")
    p2.op("vector", lambda h: h.tensor_sub(stat(Q_V), stat(Q_T1), stat(Q_T0)))
    p2.drain("vector")
    rsqrt_chain(p2, stat(Q_V), stat(Q_SC), stat(Q_T1), stat(Q_T2), stat(Q_T3), eps_add=EPS_LN)
    p2.op("vector", lambda h: h.tensor_mul(stat(Q_KB), stat(Q_M), stat(Q_SC)))
    p2.drain("vector")
    p2.op("vector", lambda h: h.tensor_scalar(stat(Q_KB), stat(Q_KB), -1.0, None, ALU.mult))
    p2.drain("vector")
    v_m3 = nop_inc(p2, "vector", sV)
    p2.wait("scalar", sV, v_m3)
    ag_st = []
    for i in range(NT):
        xa = xr[:, ts(i, D)]
        if i >= 2:
            p2.wait("scalar", ag_st[i - 2][0], ag_st[i - 2][1])
        sc = p2.op("scalar", lambda h, a=xa, i=i: h.activation(
            xn2[i % 2][:], a, ACT.Identity, scale=stat(Q_SC, i), bias=stat(Q_KB, i)), sem=sS)
        k = dmaS.i % len(dmaS.sems)
        dmaS.i += 1
        sring = dmaS.sems[k]
        if dmaS.uses[k] > 0:
            p2.wait("sync", sring, 16 * dmaS.uses[k])
        dmaS.uses[k] += 1
        p2.wait("sync", sS, sc)
        op_pair(p2, "sync",
                lambda h, i=i, b=xn2[i % 2]: h.dma_start(out=ag_out[ts(i, 128), :], in_=b[:]),
                lambda h, i=i, b=xn2[i % 2]: h.dma_start(out=ag_out[T + i * 128: T + (i + 1) * 128, :], in_=b[:]),
                sem=sring, n=16)
        ag_st.append((sring, sring.val))
    emit_barrier(p2, 0, ag_st)
    p2.emit(blk); p2s.close()
    snap("p2_end")
    if DBG_PHASE == "p2":
        return dbg_finish()

    p3s = contextlib.ExitStack()
    xnT = sbuf("xnT", [128, KD * 16 * 128], BF16, p3s)
    xg = [sbuf(f"xg{i}", [128, D], BF16, p3s) for i in range(2)]
    wst = [sbuf(f"wst{i}", [128, KD * 128], BF16, p3s) for i in range(2)]
    wg2s = sbuf("wg2s", [128, HL], BF16, p3s)
    g1T = sbuf("g1T", [128, N], BF16, p3s)
    beta_f = sbuf("beta_f", [NHL, N], F32, p3s)
    stq = [sbuf(f"stq{i}", [128, 512], BF16, p3s) for i in range(2)]
    pP = [psum(f"pP{i}", [128, 512], F32, p3s) for i in range(2)]
    pXt = psum("pXt", [128, 512], BF16, p3s)
    pBt = psum("pBt", [128, 128], F32, p3s)

    p3 = Phase("p3")
    barrier(p3, "p2_end")

    ldg2 = dmaG.issue(p3, wg2s[:], wg2_t[:])

    def xnT_blk(kt, n0, n1):
        return xnT.ap().rearrange("p (k f) -> p k f", k=KD)[:, kt, n0:n1]

    wsrc = [("q", wq_t, 0), ("k", wk_t, 1), ("v", wv_t, 2)]
    ev3 = [None]
    xg_tr_done = {0: None, 1: None}
    mm3_done = {}
    wst_free = {0: None, 1: None}
    pP_free = {0: None, 1: None}
    stq_free = {0: None, 1: None}

    for sh in range(2):
        for mt in range(16):
            gmt = sh * 16 + mt
            pw = []
            if mt == 0 and sh == 1:
                v = mm3_done.get((sh - 1, "last"))
                if v:
                    pw.append((sT, v))
            if xg_tr_done[mt % 2] is not None:
                pw.append((sT, xg_tr_done[mt % 2]))
            ld = dmaS.issue(p3, xg[mt % 2][:], ag_out[ts(gmt, 128), :], pre_wait=pw or None)
            p3.wait("tensor", ld[0], ld[1])
            for kb in range(4):
                if ev3[0] is not None:
                    p3.wait("tensor", sV, ev3[0])
                for k in range(4):
                    kt = kb * 4 + k
                    tv = p3.op("tensor", lambda h, k=k, kt=kt, mt=mt: h.transpose(
                        pXt[:, ts(k, 128)], xg[mt % 2][:, ts(kt, 128)], eyeb[:]), sem=sT)
                p3.wait("vector", sT, tv)
                dst = xnT.ap().rearrange("p (k t f) -> p k t f", k=KD, t=16)[:, ts(kb, 4), mt, :]
                ev3[0] = p3.op("vector", lambda h, d=dst: h.tensor_copy(
                    d, pXt.ap().rearrange("p (k f) -> p k f", k=4)), sem=sV)
            xg_tr_done[mt % 2] = tv

        for wi, (wname, wdr, widx) in enumerate(wsrc):
            for mh in range(NHL):
                mb = wi * NHL + mh
                pwv = wst_free[mb % 2]
                ld = dmaG.issue(p3, wst[mb % 2].ap().rearrange("p (k f) -> p k f", k=KD),
                                wdr.rearrange("(k p) f -> p k f", p=128)[:, :, ts(mh, 128)],
                                pre_wait=([(sT, pwv)] if pwv else None))
                p3.wait("tensor", ld[0], ld[1])
                if mb == 0:
                    p3.wait("tensor", sV, ev3[0])
                for nb in range(4):
                    par = nb % 2
                    if pP_free[par] is not None:
                        p3.wait("tensor", pP_free[par][0], pP_free[par][1])
                    for kt in range(KD):
                        mmv = p3.op("tensor", lambda h, kt=kt, nb=nb, mb=mb, par=par: h.matmul(
                            pP[par][:], wst[mb % 2][:, ts(kt, 128)],
                            xnT_blk(kt, nb * 512, (nb + 1) * 512),
                            start=(kt == 0), stop=(kt == KD - 1)), sem=sT)
                    eng, sem = ("scalar", sS) if wname in ("q", "k") else ("vector", sV)
                    p3.wait(eng, sT, mmv)
                    if stq_free[par] is not None:
                        p3.wait(eng, stq_free[par][0], stq_free[par][1])
                    if wname in ("q", "k"):
                        evv = p3.op(eng, lambda h, par=par: h.activation(
                            stq[par][:], pP[par][:], ACT.Silu), sem=sem)
                    else:
                        evv = p3.op(eng, lambda h, par=par: h.tensor_copy(stq[par][:], pP[par][:]), sem=sem)
                    pP_free[par] = (sem, evv)
                    c0 = sh * 16 + nb * 4
                    stv = dmaS.issue(p3, qkv_scan[c0:c0 + 4, :, mh, widx, :].rearrange("c p f -> p c f"),
                                     stq[par].ap().rearrange("p (c f) -> p c f", c=4),
                                     pre_wait=[(sem, evv)])
                    stq_free[par] = stv
                mm3_done[(sh, mb)] = sT.val
                wst_free[mb % 2] = sT.val
        mm3_done[(sh, "last")] = sT.val

        ldb = dmaG.issue(p3, wst[0].ap().rearrange("p (k f) -> p k f", k=KD)[:, :, 0:NHL],
                         wbeta_t.rearrange("(k p) f -> p k f", p=128),
                         pre_wait=[(sT, wst_free[0])])
        p3.wait("tensor", ldb[0], ldb[1])
        for nb in range(4):
            par = nb % 2
            if pP_free[par] is not None:
                p3.wait("tensor", pP_free[par][0], pP_free[par][1])
            for kt in range(KD):
                mmv = p3.op("tensor", lambda h, kt=kt, nb=nb, par=par: h.matmul(
                    pP[par][0:NHL, :], wst[0][:, kt * 128: kt * 128 + NHL],
                    xnT_blk(kt, nb * 512, (nb + 1) * 512),
                    start=(kt == 0), stop=(kt == KD - 1)), sem=sT)
            p3.wait("scalar", sT, mmv)
            scv = p3.op("scalar", lambda h, nb=nb, sh=sh, par=par: h.activation(
                beta_f[:, sh * T + nb * 512: sh * T + (nb + 1) * 512],
                pP[par][0:NHL, :], ACT.Sigmoid), sem=sS)
            pP_free[par] = (sS, scv)
        wst_free[0] = sT.val

        ldg1 = dmaG.issue(p3, wst[1].ap().rearrange("p (k f) -> p k f", k=KD),
                          wg1_t.rearrange("(k p) f -> p k f", p=128),
                          pre_wait=[(sT, wst_free[1])])
        p3.wait("tensor", ldg1[0], ldg1[1])
        for nb in range(4):
            par = nb % 2
            if pP_free[par] is not None:
                p3.wait("tensor", pP_free[par][0], pP_free[par][1])
            for kt in range(KD):
                mmv = p3.op("tensor", lambda h, kt=kt, nb=nb, par=par: h.matmul(
                    pP[par][:], wst[1][:, ts(kt, 128)],
                    xnT_blk(kt, nb * 512, (nb + 1) * 512),
                    start=(kt == 0), stop=(kt == KD - 1)), sem=sT)
            p3.wait("vector", sT, mmv)
            gv = p3.op("vector", lambda h, nb=nb, sh=sh, par=par: h.tensor_copy(
                g1T[:, sh * T + nb * 512: sh * T + (nb + 1) * 512], pP[par][:]), sem=sV)
            pP_free[par] = (sV, gv)
        wst_free[1] = sT.val

    p3.drain("scalar")
    s_bt = nop_inc(p3, "scalar", sS)
    p3.wait("vector", sS, s_bt)
    p3.op("vector", lambda h: h.tensor_scalar(beta_f[:], beta_f[:], 2.0, None, ALU.mult))
    p3.drain("vector")
    v_bt = nop_inc(p3, "vector", sV)
    p3.wait("tensor", sV, v_bt)
    for c in range(NTF):
        tv = p3.op("tensor", lambda h, c=c: h.transpose(
            pBt[:, 0:NHL], beta_f[:, ts(c, 128)], eyef[0:NHL, 0:NHL]), sem=sT)
        p3.wait("vector", sT, tv)
        bv = p3.op("vector", lambda h, c=c: h.tensor_copy(
            beta_tm[:, c * NHL:(c + 1) * NHL], pBt[:, 0:NHL]), sem=sV)
        p3.wait("tensor", sV, bv)

    p3.wait("tensor", ldg2[0], ldg2[1])
    gate_ev = {0: None, 1: None}
    for c in range(NTF):
        for nb in range(2):
            par = (2 * c + nb) % 2
            if gate_ev[par] is not None:
                p3.wait("tensor", gate_ev[par][0], gate_ev[par][1])
            mmv = p3.op("tensor", lambda h, c=c, nb=nb, par=par: h.matmul(
                pP[par][:], g1T[:, ts(c, 128)], wg2s[:, ts(nb, 512)],
                start=True, stop=True), sem=sT)
            p3.wait("scalar", sT, mmv)
            if stq_free[par] is not None:
                p3.wait("scalar", stq_free[par][0], stq_free[par][1])
            scv = p3.op("scalar", lambda h, par=par: h.activation(
                stq[par][:], pP[par][:], ACT.Sigmoid), sem=sS)
            stv = dmaS.issue(p3, gate_dram[c, :, ts(nb, 512)], stq[par][:],
                             pre_wait=[(sS, scv)])
            stq_free[par] = stv
            gate_ev[par] = (sS, scv)
    p3.emit(blk); p3s.close()
    snap("p3_end")
    if DBG_PHASE == "p3":
        return dbg_finish()

    p4s = contextlib.ExitStack()
    woT = sbuf("woT", [128, KHL * D], BF16, p4s)
    wof = [sbuf(f"wof{i}", [128, D], F32, p4s) for i in range(2)]
    S32 = sbuf("S32", [128, NHL * 128], F32, p4s)
    S_bf = sbuf("S_bf", [128, NHL * 128], BF16, p4s)
    qkvb = [sbuf(f"qkvb{i}", [128, NHL * 3 * 128], BF16, p4s) for i in range(2)]
    gtb = [sbuf(f"gtb{i}", [128, HL], BF16, p4s) for i in range(2)]
    scr = sbuf("scr", [128, 42 * 128], BF16, p4s)
    scw = sbuf("scw", [128, NHL * 2 * 128], BF16, p4s)
    gsb = sbuf("gsb", [128, NHL * 128], BF16, p4s)
    scf = sbuf("scf", [128, 128], F32, p4s)
    scf2 = sbuf("scf2", [128, 128], F32, p4s)
    ogb = sbuf("ogb", [128, HL], F32, p4s)
    ogsq = sbuf("ogsq", [128, HL], F32, p4s)
    ogn_s = sbuf("ogn_s", [128, HL], BF16, p4s)
    ognT = [sbuf(f"ognT{i}", [128, NHL * 128], BF16, p4s) for i in range(2)]
    sc8 = sbuf("sc8", [128, 8 * 8], F32, p4s)
    fst4 = [sbuf(f"fst4{i}", [128, 512], F32, p4s) for i in range(2)]

    bankA = [psum(f"pbA{i}", [128, 512], F32, p4s) for i in range(2)]
    bankB = [psum(f"pbB{i}", [128, 512], F32, p4s) for i in range(2)]
    pT4l = [psum(f"pT4{i}", [128, 256], BF16, p4s) for i in range(2)]
    pCmb = [bankA[0][:, 0:384], bankA[1][:, 0:384]]
    pWo = [psum(f"pWo{i}", [128, 512], F32, p4s) for i in range(2)]

    SLOT_NAMES = ["Nd", "Dbn", "P0", "P1", "P2", "P3", "Bd0", "Bd1", "Bd2", "Bd3", "Bd4",
                  "Bo", "LT", "ta", "tb", "v0", "Uf", "Khat", "R", "TG2", "GNs"]

    def slot(par, name):
        i = par * len(SLOT_NAMES) + SLOT_NAMES.index(name)
        return scr[:, i * 128:(i + 1) * 128]

    def s8(q, j=None):
        if j is None:
            return sc8[:, q * 8:(q + 1) * 8]
        return sc8[:, q * 8 + j: q * 8 + j + 1]

    p4 = Phase("p4")
    barrier(p4, "p3_end")

    s_wo = []
    for r in range(KHL):
        ld = dmaS.issue(p4, wof[r % 2][:], wo_t[ts(r, 128), :],
                        pre_wait=([(sS, s_wo[r - 2])] if r >= 2 else None))
        p4.wait("scalar", ld[0], ld[1])
        s_wo.append(p4.op("scalar", lambda h, r=r: h.activation(
            woT[:, ts(r, D)], wof[r % 2][:], ACT.Copy, scale=gnorm_sb[:, r:r + 1]), sem=sS))
    p4.op("vector", lambda h: h.memset(S32[:], 0.0), sem=sV)
    v_z2 = p4.op("vector", lambda h: h.memset(S_bf[:], 0.0), sem=sV)

    rs_stores = []
    last_sbf_v = {bh: (sV, v_z2) for bh in range(NHL)}
    prev_pa_ev = None
    qkv_free = []
    gate_free = []
    wo_done = {0: None, 1: None}
    pWo_free = {0: None, 1: None}
    fst4_free = {0: None, 1: None}
    ognT_ev = [None]
    mix_guard = {0: (sV, v_z2), 1: (sV, v_z2)}
    pt4_guard = {0: None, 1: None}
    u_guard = {0: None, 1: None}

    def emit_og_tail(c):
        p4.drain("vector")
        p4.op("vector", lambda h: h.tensor_mul(ogsq[:], ogb[:], ogb[:]), sem=sV)
        p4.drain("vector")
        p4.op("vector", lambda h: h.tensor_reduce(
            s8(4).rearrange("p (a o) -> p a o", o=1),
            ogsq.ap().rearrange("p (a f) -> p a f", a=NHL), AX, ALU.add), sem=sV)
        p4.drain("vector")
        p4.op("vector", lambda h: h.tensor_scalar(s8(4), s8(4), 1.0 / HD, None, ALU.mult))
        p4.drain("vector")
        rsqrt_chain(p4, s8(4), s8(5), s8(4), s8(6), s8(7), eps_add=EPS_OG)
        iv = s8(5)
        inv_b = bass.AP(iv.tensor, iv.offset, [iv.ap[0], list(iv.ap[1])[:1] + [NHL] if False else [1, NHL], [0, 128]])
        v_ogn = p4.op("vector", lambda h, ib=inv_b: h.tensor_mul(
            ogn_s.ap().rearrange("p (a f) -> p a f", a=NHL),
            ogb.ap().rearrange("p (a f) -> p a f", a=NHL), ib), sem=sV)
        p4.wait("tensor", sV, v_ogn)
        for _par in (0, 1):
            if pt4_guard[_par] is not None:
                p4.wait("tensor", sV, pt4_guard[_par])
        if wo_done[c % 2] is not None:
            p4.wait("vector", sT, wo_done[c % 2])
        for hb2 in range(4):
            if ognT_ev[0] is not None:
                p4.wait("tensor", sV, ognT_ev[0])
            p4.op("tensor", lambda h, hb2=hb2: h.transpose(
                pT4l[0][:, 0:128], ogn_s[:, ts(2 * hb2, 128)], eyeb[:]), sem=sT)
            tv2 = p4.op("tensor", lambda h, hb2=hb2: h.transpose(
                pT4l[1][:, 0:128], ogn_s[:, ts(2 * hb2 + 1, 128)], eyeb[:]), sem=sT)
            p4.wait("vector", sT, tv2)
            p4.op("vector", lambda h, hb2=hb2, c=c: h.tensor_copy(
                ognT[c % 2][:, hb2 * 256: hb2 * 256 + 128], pT4l[0][:, 0:128]), sem=sV)
            ognT_ev[0] = p4.op("vector", lambda h, hb2=hb2, c=c: h.tensor_copy(
                ognT[c % 2][:, hb2 * 256 + 128:(hb2 + 1) * 256], pT4l[1][:, 0:128]), sem=sV)
        p4.wait("tensor", sV, ognT_ev[0])
        pt4_guard[0] = ognT_ev[0]
        pt4_guard[1] = ognT_ev[0]
        for nb in range(4):
            par = nb % 2
            if pWo_free[par] is not None:
                p4.wait("tensor", pWo_free[par][0], pWo_free[par][1])
            for kt in range(KHL):
                t_wo = p4.op("tensor", lambda h, nb=nb, kt=kt, c=c, par=par: h.matmul(
                    pWo[par][:], ognT[c % 2][:, ts(kt, 128)],
                    woT[:, kt * D + nb * 512: kt * D + (nb + 1) * 512],
                    start=(kt == 0), stop=(kt == KHL - 1)), sem=sT)
            eng, sem = ("scalar", sS) if par == 0 else ("vector", sV)
            p4.wait(eng, sT, t_wo)
            if fst4_free[par] is not None:
                p4.wait(eng, fst4_free[par][0], fst4_free[par][1])
            if eng == "scalar":
                e_wo = p4.op(eng, lambda h, par=par: h.copy(fst4[par][:], pWo[par][:]), sem=sem)
            else:
                e_wo = p4.op(eng, lambda h, par=par: h.tensor_copy(fst4[par][:], pWo[par][:]), sem=sem)
            pWo_free[par] = (sem, e_wo)
            k_ = dmaS.i % len(dmaS.sems)
            dmaS.i += 1
            sring = dmaS.sems[k_]
            if dmaS.uses[k_] > 0:
                p4.wait("sync", sring, 16 * dmaS.uses[k_])
            dmaS.uses[k_] += 1
            p4.wait("sync", sem, e_wo)
            op_pair(p4, "sync",
                    lambda h, c=c, nb=nb, par=par: h.dma_start(
                        out=rs_sh[0, ts(c, 128), ts(nb, 512)], in_=fst4[par][:]),
                    lambda h, c=c, nb=nb, par=par: h.dma_start(
                        out=rs_sh[1, ts(c, 128), ts(nb, 512)], in_=fst4[par][:]),
                    sem=sring, n=16)
            stv = (sring, sring.val)
            fst4_free[par] = stv
            rs_stores.append(stv)
        wo_done[c % 2] = sT.val

    pending_og = [None]

    for c in range(DBG_CHUNKS if DBG_CHUNKS else NTF):
        pwq = [(sT, qkv_free[c - 2])] if c >= 2 else None
        ldq = dmaG.issue(p4, qkvb[c % 2][:],
                         qkv_scan[c].rearrange("p b t f -> p (b t f)"), pre_wait=pwq)
        pwg = [(sV, gate_free[c - 2])] if c >= 2 else None
        ldg = dmaG.issue(p4, gtb[c % 2][:], gate_dram[c, :, :], pre_wait=pwg)

        def kqv(bh, w, c=c):
            return qkvb[c % 2][:, (bh * 3 + w) * 128: (bh * 3 + w + 1) * 128]

        pa_ev = []
        p4.wait("tensor", ldq[0], ldq[1])
        for bh in (range(NHL) if DBG_SCANPART >= 1 else []):
            par = bh % 2
            p4.wait("tensor", last_sbf_v[bh][0], last_sbf_v[bh][1])
            p4.wait("tensor", mix_guard[par][0], mix_guard[par][1])
            guard = pa_ev[-2] if len(pa_ev) >= 2 else (prev_pa_ev[bh - 2] if prev_pa_ev else None)
            if guard:
                p4.wait("tensor", sV, guard[0])
                p4.wait("tensor", sS, guard[1])
            KT = kqv(bh, 1)
            QT = kqv(bh, 0)
            p4.op("tensor", lambda h, KT=KT, par=par: h.matmul(
                pCmb[par][:, 0:128], KT, KT, start=True, stop=True), sem=sT)
            p4.op("tensor", lambda h, KT=KT, par=par, bh=bh: h.matmul(
                pCmb[par][:, 128:256], KT, S_bf[:, ts(bh, 128)], start=True, stop=True), sem=sT)
            tA = p4.op("tensor", lambda h, KT=KT, QT=QT, par=par: h.matmul(
                pCmb[par][:, 256:384], KT, QT, start=True, stop=True), sem=sT)
            p4.wait("vector", sT, tA)
            if DBG_PA >= 2:
                p4.op("vector", lambda h, bh=bh, par=par: h.tensor_copy(
                    gsb[:, ts(bh, 128)], pCmb[par][:, 0:128]), sem=sV)
                p4.op("vector", lambda h, par=par: h.tensor_mul(scf[:], pCmb[par][:, 0:128], eyef[:]), sem=sV)
                p4.drain("vector")
                vA = p4.op("vector", lambda h, bh=bh: h.tensor_reduce(
                    s8(0, bh), scf[:], AX, ALU.add), sem=sV)
                p4.drain("vector")
            else:
                vA = nop_inc(p4, "vector", sV)
            p4.wait("scalar", sT, tA)
            p4.wait("scalar", sV, vA)
            if DBG_PA >= 3:
                p4.op("scalar", lambda h, bh=bh, par=par: h.copy(
                    scw[:, ts(2 * bh, 128)], pCmb[par][:, 128:256]), sem=sS)
                sA = p4.op("scalar", lambda h, bh=bh, par=par: h.copy(
                    scw[:, ts(2 * bh + 1, 128)], pCmb[par][:, 256:384]), sem=sS)
            else:
                sA = nop_inc(p4, "scalar", sS)
            pa_ev.append((vA, sA))
        prev_pa_ev = pa_ev
        if pending_og[0] is not None:
            emit_og_tail(pending_og[0])
            pending_og[0] = None

        if DBG_SCANPART < 2:
            qkv_free.append(sT.val)
            gate_free.append(sV.val)
            continue
        p4.drain("vector")
        rsqrt_chain(p4, s8(0), s8(1), s8(4), s8(5), s8(6), eps_add=EPS_K)
        bsl = beta_tm[:, c * NHL:(c + 1) * NHL]
        p4.op("vector", lambda h, b=bsl: h.tensor_mul(s8(2), s8(1), b))
        p4.drain("vector")
        p4.op("vector", lambda h: h.tensor_scalar(s8(3), s8(2), -1.0, None, ALU.mult))
        p4.drain("vector")
        nop_inc(p4, "vector", sV)

        def pass_b_gen(bh):
            par = bh % 2
            n_ap = s8(1, bh)
            bn_ap = s8(2, bh)
            nbn_ap = s8(3, bh)
            bcol = beta_tm[:, c * NHL + bh: c * NHL + bh + 1]
            KT = kqv(bh, 1)
            QT = kqv(bh, 0)
            VT = kqv(bh, 2)
            scfl = scf if par == 0 else scf2
            mxGN = bankB[par][:, 256:384]
            mxB0 = bankB[par][:, 384:512]
            mxO = mxGN
            mxDS = mxB0
            sqP = bankB[par][:, 0:128]
            sqB = bankB[par][:, 128:256]
            apP = bankA[par][:, 384:512]
            t4V = pT4l[par][:, 0:128]
            t4K = pT4l[par][:, 128:256]

            p4.op("vector", lambda h, par=par, n=n_ap: h.tensor_scalar(
                slot(par, "Nd"), eyef[:], n, None, ALU.mult), sem=sV)
            v_nd = p4.op("vector", lambda h, par=par, b=bn_ap: h.tensor_scalar(
                slot(par, "Dbn"), eyef[:], b, None, ALU.mult), sem=sV)
            p4.wait("tensor", mix_guard[par][0], mix_guard[par][1])
            p4.wait("tensor", sV, v_nd)
            t_gn = p4.op("tensor", lambda h, bh=bh, par=par, o=mxGN: h.matmul(
                o, gsb[:, ts(bh, 128)], slot(par, "Nd"), start=True, stop=True), sem=sT)
            yield
            p4.wait("vector", sT, t_gn)
            v_gns = p4.op("vector", lambda h, par=par, i_=mxGN: h.tensor_copy(slot(par, "GNs"), i_), sem=sV)
            v_p0 = p4.op("vector", lambda h, par=par, nb=nbn_ap, i_=mxGN: h.scalar_tensor_tensor(
                slot(par, "P0"), i_, nb, m_sl[:], ALU.mult, ALU.mult), sem=sV)
            yield
            p4.wait("tensor", sV, v_p0)
            t_b0 = p4.op("tensor", lambda h, par=par, o=mxB0: h.matmul(
                o, slot(par, "GNs"), slot(par, "Dbn"), start=True, stop=True), sem=sT)
            yield
            p4.wait("vector", sT, t_b0)
            p4.op("vector", lambda h, par=par, i_=mxB0: h.scalar_tensor_tensor(
                slot(par, "Bd0"), i_, -1.0, m_su[:], ALU.mult, ALU.mult), sem=sV)
            v_b0 = p4.op("vector", lambda h, par=par, i_=mxB0: h.scalar_tensor_tensor(
                slot(par, "Bo"), i_, -1.0, m_bou[:], ALU.mult, ALU.mult), sem=sV)
            yield
            if pt4_guard[par] is not None:
                p4.wait("tensor", sV, pt4_guard[par])
            p4.op("tensor", lambda h, a=VT, o=t4V: h.transpose(o, a, eyeb[:]), sem=sT)
            t_tr2 = p4.op("tensor", lambda h, a=KT, o=t4K: h.transpose(o, a, eyeb[:]), sem=sT)
            yield
            p4.wait("vector", sT, t_tr2)
            p4.op("vector", lambda h, b=bcol, i_=t4V, o=scfl: h.tensor_scalar(
                o[:], i_, b, None, ALU.mult), sem=sV)
            v_kh = p4.op("vector", lambda h, par=par, n=n_ap, i_=t4K: h.tensor_scalar(
                slot(par, "Khat"), i_, n, None, ALU.mult), sem=sV)
            pt4_guard[par] = v_kh
            p4.drain("vector")
            v_r = p4.op("vector", lambda h, par=par, bh=bh, nb=nbn_ap, o=scfl: h.scalar_tensor_tensor(
                slot(par, "R"), scw[:, ts(2 * bh, 128)], nb, o[:], ALU.mult, ALU.add), sem=sV)
            v_tg2 = p4.op("vector", lambda h, par=par, bh=bh, n=n_ap: h.scalar_tensor_tensor(
                slot(par, "TG2"), scw[:, ts(2 * bh + 1, 128)], n, m_tui[:], ALU.mult, ALU.mult), sem=sV)
            yield
            p4.wait("tensor", sV, v_b0)
            s_sq = None
            for j in range(1, 5):
                if s_sq is not None:
                    p4.wait("tensor", sS, s_sq)
                if j <= 3:
                    p4.op("tensor", lambda h, par=par, j=j, o=sqP: h.matmul(
                        o, slot(par, f"Bd{j-1}"), slot(par, f"P{j-1}"),
                        start=True, stop=True), sem=sT)
                t_sq = p4.op("tensor", lambda h, par=par, j=j, o=sqB: h.matmul(
                    o, slot(par, f"P{j-1}"), slot(par, f"Bd{j-1}"),
                    start=True, stop=True), sem=sT)
                p4.wait("scalar", sT, t_sq)
                if j <= 3:
                    p4.op("scalar", lambda h, par=par, j=j, i_=sqP: h.copy(
                        slot(par, f"P{j}"), i_), sem=sS)
                s_sq = p4.op("scalar", lambda h, par=par, j=j, i_=sqB: h.copy(
                    slot(par, f"Bd{j}"), i_), sem=sS)
                yield
            p4.wait("vector", sS, s_sq)
            v_lt = p4.op("vector", lambda h, par=par: h.tensor_add(
                slot(par, "LT"), eyef[:], slot(par, "Bd4")), sem=sV)
            for j in range(3, -1, -1):
                p4.wait("tensor", sV, v_lt)
                t_lt = p4.op("tensor", lambda h, par=par, j=j, o=apP: h.matmul(
                    o, slot(par, f"P{j}"), slot(par, "LT"), start=True, stop=True), sem=sT)
                p4.wait("vector", sT, t_lt)
                v_lt = p4.op("vector", lambda h, par=par, i_=apP: h.scalar_tensor_tensor(
                    slot(par, "LT"), i_, 1.0, slot(par, "LT"), ALU.bypass, ALU.add), sem=sV)
                yield
            p4.wait("tensor", sV, v_lt)
            p4.wait("tensor", sV, v_r)
            t_v0 = p4.op("tensor", lambda h, par=par, o=apP: h.matmul(
                o, slot(par, "LT"), slot(par, "R"), start=True, stop=True), sem=sT)
            p4.wait("scalar", sT, t_v0)
            s_v0 = p4.op("scalar", lambda h, par=par, i_=apP: h.copy(slot(par, "v0"), i_), sem=sS)
            yield
            hcur = "v0"
            ev_h = (sS, s_v0)
            for stage in range(3):
                p4.wait("tensor", ev_h[0], ev_h[1])
                t_z = p4.op("tensor", lambda h, par=par, hcur=hcur, o=apP: h.matmul(
                    o, slot(par, "Bo"), slot(par, hcur), start=True, stop=True), sem=sT)
                p4.wait("scalar", sT, t_z)
                s_z = p4.op("scalar", lambda h, par=par, i_=apP: h.copy(slot(par, "tb"), i_), sem=sS)
                yield
                p4.wait("tensor", sS, s_z)
                t_w = p4.op("tensor", lambda h, par=par, o=apP: h.matmul(
                    o, slot(par, "LT"), slot(par, "tb"), start=True, stop=True), sem=sT)
                dstn = "ta" if stage < 2 else "Uf"
                if dstn == "Uf" and u_guard[par] is not None:
                    p4.wait("vector", sT, u_guard[par])
                p4.wait("vector", sT, t_w)
                v_h = p4.op("vector", lambda h, par=par, dstn=dstn, i_=apP: h.scalar_tensor_tensor(
                    slot(par, dstn), i_, 1.0, slot(par, "v0"), ALU.bypass, ALU.add), sem=sV)
                ev_h = (sV, v_h)
                hcur = dstn
                yield
            U = hcur
            p4.wait("tensor", ev_h[0], ev_h[1])
            p4.wait("tensor", sV, v_tg2)
            p4.op("tensor", lambda h, QT=QT, bh=bh, o=mxO: h.matmul(
                o, QT, S_bf[:, ts(bh, 128)], start=True, stop=False), sem=sT)
            t_o = p4.op("tensor", lambda h, par=par, U=U, o=mxO: h.matmul(
                o, slot(par, "TG2"), slot(par, U), start=False, stop=True), sem=sT)
            t_ds = p4.op("tensor", lambda h, par=par, U=U, bh=bh, o=mxDS: h.matmul(
                o, slot(par, "Khat"), slot(par, U), start=True, stop=True), sem=sT)
            u_guard[par] = t_ds
            p4.wait("vector", sT, t_o)
            p4.wait("vector", ldg[0], ldg[1])
            p4.op("vector", lambda h, bh=bh, c=c, i_=mxO: h.scalar_tensor_tensor(
                ogb[:, ts(bh, 128)], i_, 1.0, gtb[c % 2][:, ts(bh, 128)],
                ALU.bypass, ALU.mult), sem=sV)
            p4.wait("vector", sT, t_ds)
            v_s32 = p4.op("vector", lambda h, bh=bh, i_=mxDS: h.tensor_add(
                S32[:, ts(bh, 128)], S32[:, ts(bh, 128)], i_), sem=sV)
            p4.wait("scalar", sV, v_s32)
            last_sbf_v[bh] = (sS, p4.op("scalar", lambda h, bh=bh: h.copy(
                S_bf[:, ts(bh, 128)], S32[:, ts(bh, 128)]), sem=sS))
            mix_guard[par] = last_sbf_v[bh]

        if DBG_SCANPART >= 3:
            for bh0 in range(0, NHL, 2):
                gens = [pass_b_gen(bh0), pass_b_gen(bh0 + 1)]
                live = [True, True]
                while any(live):
                    for gi in range(2):
                        if live[gi]:
                            try:
                                next(gens[gi])
                            except StopIteration:
                                live[gi] = False
        qkv_free.append(sT.val)

        if DBG_SCANPART < 6:
            gate_free.append(sV.val)
            continue
        gate_free.append(sV.val)
        pending_og[0] = c

    if pending_og[0] is not None:
        emit_og_tail(pending_og[0])
        pending_og[0] = None
    if not DBG_CHUNKS:
        emit_barrier(p4, 1, [(s, 16 * u) for s, u in zip(dmaS.sems, dmaS.uses) if u])
    p4.emit(blk); p4s.close()
    snap("p4_end")

    if DBG_CHUNKS:
        pD = Phase("pD")
        for se, u in zip(dmaS.sems, dmaS.uses):
            if u:
                pD.wait("sync", se, 16 * u)
        for se, u in zip(dmaG.sems, dmaG.uses):
            if u:
                pD.wait("gpsimd", se, 16 * u)
        pD.emit(blk)
        st.close()
        return nc

    p5s = contextlib.ExitStack()
    rsb = [sbuf(f"rsb{i}", [128, D], F32, p5s) for i in range(4)]
    p5 = Phase("p5")
    barrier(p5, "p4_end")
    v_add = []
    for i in range(NT):
        for sl_ in range(2):
            bi = (2 * i + sl_) % 4
            k_ = dmaS.i % len(dmaS.sems)
            dmaS.i += 1
            sring = dmaS.sems[k_]
            if dmaS.uses[k_] > 0:
                p5.wait("sync", sring, 16 * dmaS.uses[k_])
            dmaS.uses[k_] += 1
            idx = 2 * i + sl_
            if idx >= 4:
                p5.wait("sync", sV, v_add[idx - 4])
            op_pair(p5, "sync",
                    lambda h, i=i, sl_=sl_, bi=bi: h.dma_start(
                        out=rsb[bi][:], in_=rs_sh[sl_, ts(i, 128), :]),
                    lambda h, i=i, sl_=sl_, bi=bi: h.dma_start(
                        out=rsb[bi][:], in_=rs_sh[sl_, T + i * 128: T + (i + 1) * 128, :]),
                    sem=sring, n=16)
            p5.wait("vector", sring, sring.val)
        xa = xr[:, ts(i, D)]
        v_add.append(p5.op("vector", lambda h, a=xa, bi0=(2 * i) % 4: h.tensor_add(
            a, a, rsb[bi0][:]), sem=sV))
        p5.drain("vector")
        v_add.append(p5.op("vector", lambda h, a=xa, bi1=(2 * i + 1) % 4: h.tensor_add(
            a, a, rsb[bi1][:]), sem=sV))
        p5.drain("vector")
    p5.emit(blk); p5s.close()
    snap("p5_end")

    emit_ffn("p6", True, "p5_end")
    snap("p6_end")

    pFin = Phase("pF")
    for s, u in zip(dmaS.sems, dmaS.uses):
        if u:
            pFin.wait("sync", s, 16 * u)
    for s, u in zip(dmaG.sems, dmaG.uses):
        if u:
            pFin.wait("gpsimd", s, 16 * u)
    pFin.emit(blk)

    st.close()
    return nc


_NC_CACHE = None


def kernel(**inputs):
    global _NC_CACHE
    x = np.ascontiguousarray(np.asarray(inputs["L0_attn_x"], dtype=np.float32))
    Wq = np.asarray(inputs["L0_attn_Wq"], dtype=np.float32)
    Wk = np.asarray(inputs["L0_attn_Wk"], dtype=np.float32)
    Wv = np.asarray(inputs["L0_attn_Wv"], dtype=np.float32)
    Wbeta = np.asarray(inputs["L0_attn_Wbeta"], dtype=np.float32)
    Wg1 = np.asarray(inputs["L0_attn_Wg1"], dtype=np.float32)
    Wg2 = np.asarray(inputs["L0_attn_Wg2"], dtype=np.float32)
    g_norm = np.asarray(inputs["L0_attn_g_norm"], dtype=np.float32)
    Wo = np.asarray(inputs["L0_attn_Wo"], dtype=np.float32)
    Wffn = np.asarray(inputs["L0_ffn_weight"], dtype=np.float32)
    fb = np.asarray(inputs["L0_ffn_bias"], dtype=np.float32)

    wq_T = np.ascontiguousarray(Wq.T)
    wk_T = np.ascontiguousarray(Wk.T)
    wv_T = np.ascontiguousarray(Wv.T)
    wb_T = np.ascontiguousarray(Wbeta.T)
    wg1_T = np.ascontiguousarray(Wg1.T)
    wg2_T = np.ascontiguousarray(Wg2.T)
    wo_T = np.ascontiguousarray(Wo.T)

    eye_bf = np.eye(128, dtype=ml_dtypes.bfloat16)
    eye_f = np.eye(128, dtype=np.float32)
    idx = np.arange(128)
    same_blk = (idx[:, None] // 32) == (idx[None, :] // 32)
    msl = ((idx[:, None] > idx[None, :]) & same_blk).astype(np.float32)
    msu = ((idx[:, None] < idx[None, :]) & same_blk).astype(np.float32)
    mbou = ((idx[:, None] < idx[None, :]) & ~same_blk).astype(np.float32)
    mtui = (idx[:, None] <= idx[None, :]).astype(np.float32)

    if _NC_CACHE is None:
        _NC_CACHE = build_program()
    nc = _NC_CACHE

    nonce = int(np.random.randint(1, 2**30))
    flagv = np.empty((2, 128), dtype=np.int32)
    flagv[0, :] = nonce + 1
    flagv[1, :] = nonce + 2

    in_maps = []
    for c in range(8):
        b, h = c // 2, c % 2
        hs = slice(h * HL, (h + 1) * HL)
        in_maps.append({
            "x_in": np.ascontiguousarray(x[b, h * T:(h + 1) * T, :]),
            "wq_t": np.ascontiguousarray(wq_T[:, hs]),
            "wk_t": np.ascontiguousarray(wk_T[:, hs]),
            "wv_t": np.ascontiguousarray(wv_T[:, hs]),
            "wbeta_t": np.ascontiguousarray(wb_T[:, h * NHL:(h + 1) * NHL]),
            "wg1_t": wg1_T,
            "wg2_t": np.ascontiguousarray(wg2_T[:, hs]),
            "wo_t": np.ascontiguousarray(wo_T[hs, :]),
            "gnorm_c": np.ascontiguousarray(g_norm[hs]),
            "wffn": Wffn,
            "fbias": fb,
            "eye_bfd": eye_bf,
            "eye_fd": eye_f,
            "msl_d": msl,
            "msu_d": msu,
            "mtui_d": mtui,
            "mbou_d": mbou,
            "flagv": flagv,
        })

    res = run_bass_kernel_spmd(nc, in_maps, core_ids=list(range(8)))
    out = np.empty((B, N, D), dtype=np.float32)
    for c in range(8):
        b, h = c // 2, c % 2
        out[b, h * T:(h + 1) * T, :] = res.results[c]["x_out"]
    return out


# version 14
# speedup vs baseline: 1.7851x; 1.0047x over previous
import contextlib
import os
import numpy as np
import ml_dtypes
import concourse.bass as bass
import concourse.mybir as mybir
from concourse.bass_utils import run_bass_kernel_spmd
from concourse.alu_op_type import AluOpType as RALU

F32 = mybir.dt.float32
BF16 = mybir.dt.bfloat16
AX = mybir.AxisListType.X
ALU = mybir.AluOpType
ACT = mybir.ActivationFunctionType
ts = bass.ts

B, N, D = 4, 4096, 2048
T = 2048
NT = T // 128
NTF = N // 128
KD = D // 128
NHL, HD = 8, 128
HL = NHL * HD
KHL = HL // 128

EPS_LN = 1e-5
EPS_RMS = 1e-8
EPS_G = 1e-8
EPS_X = 1e-8
EPS_K = 1e-6
EPS_OG = 1e-6
QB = 127.0
USE_DRAIN = os.environ.get("KDBG_DRAIN", "1") == "1"
DBG_CHUNKS = int(os.environ.get("KDBG_CHUNKS", "0") or 0)
DBG_PHASE = os.environ.get("KDBG_PHASE", "")
DBG_SCANPART = int(os.environ.get("KDBG_SCANPART", "9") or 9)
DBG_PA = int(os.environ.get("KDBG_PA", "9") or 9)

ENGINES = ("sync", "gpsimd", "vector", "scalar", "tensor")


class Sem:
    def __init__(self, h, name):
        self.h = h
        self.name = name
        self.val = 0

    def inc(self, n=1):
        self.val += n
        return self.val


class Phase:

    def __init__(self, name):
        self.name = name
        self.ops = {e: [] for e in ENGINES}

    def op(self, eng, fn, sem=None, n=1):
        if sem is None:
            self.ops[eng].append(fn)
            return None
        v = sem.inc(n)
        self.ops[eng].append(lambda h, fn=fn, sem=sem, n=n: fn(h).then_inc(sem.h, n))
        return v

    def wait(self, eng, sem, val):
        if val:
            self.ops[eng].append(lambda h, s=sem, v=val: h.wait_ge(s.h, v))

    def drain(self, eng):
        if eng == "vector" and not USE_DRAIN:
            return
        self.ops[eng].append(lambda h: h.drain())

    def emit(self, blk):
        for e in ENGINES:
            lst = self.ops[e]
            if not lst:
                continue

            def body(h, lst=lst):
                for f in lst:
                    f(h)
            getattr(blk, e)(body)


class Dma:

    def __init__(self, sems, queue_eng):
        self.sems = sems
        self.uses = [0] * len(sems)
        self.i = 0
        self.eng = queue_eng

    def issue(self, ph, out_ap, in_ap, transpose=False, pre_wait=None, slow_ok=False):
        k = self.i % len(self.sems)
        self.i += 1
        s = self.sems[k]
        if self.uses[k] > 0:
            ph.wait(self.eng, s, 16 * self.uses[k])
        self.uses[k] += 1
        if pre_wait:
            for (ws, wv) in pre_wait:
                ph.wait(self.eng, ws, wv)
        v = s.inc(16)
        if transpose:
            ph.ops[self.eng].append(
                lambda h, o=out_ap, i_=in_ap, s=s: h.dma_start_transpose(o, i_).then_inc(s.h, 16))
        else:
            ph.ops[self.eng].append(
                lambda h, o=out_ap, i_=in_ap, s=s, sl=slow_ok: h.dma_start(
                    out=o, in_=i_, allow_slow_non_contiguous=sl).then_inc(s.h, 16))
        return (s, v)


def build_program():
    nc = bass.Bass("TRN2", target_bir_lowering=False,
                   detect_race_conditions=(os.environ.get("KDBG_RACE", "0") == "1"))

    x_in = nc.dram_tensor("x_in", [T, D], F32, kind="ExternalInput").ap()
    wq_t = nc.dram_tensor("wq_t", [D, HL], F32, kind="ExternalInput").ap()
    wk_t = nc.dram_tensor("wk_t", [D, HL], F32, kind="ExternalInput").ap()
    wv_t = nc.dram_tensor("wv_t", [D, HL], F32, kind="ExternalInput").ap()
    wbeta_t = nc.dram_tensor("wbeta_t", [D, NHL], F32, kind="ExternalInput").ap()
    wg1_t = nc.dram_tensor("wg1_t", [D, 128], F32, kind="ExternalInput").ap()
    wg2_t = nc.dram_tensor("wg2_t", [128, HL], F32, kind="ExternalInput").ap()
    wo_t = nc.dram_tensor("wo_t", [HL, D], F32, kind="ExternalInput").ap()
    gnorm_c = nc.dram_tensor("gnorm_c", [HL], F32, kind="ExternalInput").ap()
    wffn = nc.dram_tensor("wffn", [D, D], F32, kind="ExternalInput").ap()
    fbias = nc.dram_tensor("fbias", [D], F32, kind="ExternalInput").ap()
    eye_bfd = nc.dram_tensor("eye_bfd", [128, 128], BF16, kind="ExternalInput").ap()
    eye_fd = nc.dram_tensor("eye_fd", [128, 128], F32, kind="ExternalInput").ap()
    msl_d = nc.dram_tensor("msl_d", [128, 128], F32, kind="ExternalInput").ap()
    msu_d = nc.dram_tensor("msu_d", [128, 128], F32, kind="ExternalInput").ap()
    mtui_d = nc.dram_tensor("mtui_d", [128, 128], F32, kind="ExternalInput").ap()
    mbou_d = nc.dram_tensor("mbou_d", [128, 128], F32, kind="ExternalInput").ap()

    x_out = nc.dram_tensor("x_out", [T, D], F32, kind="ExternalOutput").ap()

    wffn_tern = nc.dram_tensor("wffn_tern", [D, D], BF16, kind="Internal").ap()
    ag_out = nc.dram_tensor("ag_out", [N, D], BF16, kind="Internal", addr_space="Shared").ap()
    qkv_scan = nc.dram_tensor("qkv_scan", [NTF, 128, NHL, 3, 128], BF16, kind="Internal").ap()
    gate_dram = nc.dram_tensor("gate_dram", [NTF, 128, HL], BF16, kind="Internal").ap()
    rs_sh = nc.dram_tensor("rs_sh", [2, N, D], F32, kind="Internal", addr_space="Shared").ap()
    flg = nc.dram_tensor("flg", [2, 128], mybir.dt.int32, kind="Internal", addr_space="Shared").ap()
    flagv = nc.dram_tensor("flagv", [2, 128], mybir.dt.int32, kind="ExternalInput").ap()

    st = contextlib.ExitStack()

    def mksem(name):
        return Sem(st.enter_context(nc.semaphore(name)), name)

    sT = mksem("sT")
    sV = mksem("sV")
    sS = mksem("sS")
    sGc = mksem("sGc")
    sC = mksem("sC")
    dmaS = Dma([mksem(f"dS{i}") for i in range(8)], "sync")
    dmaG = Dma([mksem(f"dG{i}") for i in range(8)], "gpsimd")
    ENG_SEMS = [sT, sV, sS, sGc]

    def sbuf(name, shape, dt=F32, scope=None):
        return (scope or st).enter_context(nc.sbuf_tensor(name, shape, dt))

    def psum(name, shape, dt=F32, scope=None):
        return (scope or st).enter_context(nc.psum_tensor(name, shape, dt))

    xr = sbuf("xr", [128, NT * D], BF16)
    stats = sbuf("stats", [128, 16 * 14], F32)
    bias_rep = sbuf("bias_rep", [128, D], BF16)
    sgam = sbuf("sgam", [128, KD], F32)
    eyeb = sbuf("eyeb", [128, 128], BF16)
    eyef = sbuf("eyef", [128, 128], F32)
    m_sl = sbuf("m_sl", [128, 128], F32)
    m_su = sbuf("m_su", [128, 128], F32)
    m_tui = sbuf("m_tui", [128, 128], F32)
    m_bou = sbuf("m_bou", [128, 128], F32)
    onesr = sbuf("onesr", [1, 128], BF16)
    fbias_sb = sbuf("fbias_sb", [1, D], BF16)
    gnorm_sb = sbuf("gnorm_sb", [128, KHL], F32)
    beta_tm = sbuf("beta_tm", [128, NTF * NHL], F32)
    flv_sb = st.enter_context(nc.sbuf_tensor("flv_sb", [2, 128], mybir.dt.int32))
    fl_sb = st.enter_context(nc.sbuf_tensor("fl_sb", [1, 128], mybir.dt.int32))

    (Q_SUMX, Q_SUMSQ, Q_MX, Q_MN, Q_M, Q_V, Q_SC, Q_KQ, Q_KB, Q_XS,
     Q_T0, Q_T1, Q_T2, Q_T3) = range(14)

    def stat(q, i=None):
        if i is None:
            return stats[:, q * 16: q * 16 + 16]
        return stats[:, q * 16 + i: q * 16 + i + 1]

    phases = []
    snaps = {}

    def snap(tag):
        snaps[tag] = ([(s, s.val) for s in ENG_SEMS + [sC]],
                      [(s, 16 * u) for s, u in zip(dmaS.sems + dmaG.sems, dmaS.uses + dmaG.uses)])

    def barrier(ph, tag):
        engs, dmas = snaps[tag]
        for e in ENGINES:
            for s, v in engs + dmas:
                if v:
                    ph.wait(e, s, v)

    blk = st.enter_context(nc.Block())

    PREG = {}

    def parity_setup(h, eng):
        if eng in PREG:
            return PREG[eng]
        pid = h.to_reg(h.partition_id())
        par = h.alloc_register()
        h.reg_alu(par, pid, 1, RALU.bitwise_and)
        PREG[eng] = par
        return par

    def op_pair(ph, eng, fn_even, fn_odd, sem=None, n=1):
        v = sem.inc(n) if sem is not None else None

        def closure(h, fe=fn_even, fo=fn_odd, eng=eng, sem=sem, n=n):
            par = parity_setup(h, eng)
            with h.If_eq(par, 0):
                ins = fe(h)
                if sem is not None:
                    ins.then_inc(sem.h, n)
            with h.Else():
                ins = fo(h)
                if sem is not None:
                    ins.then_inc(sem.h, n)
        ph.ops[eng].append(closure)
        return v

    poll_sems = [mksem(f"poll{i}") for i in range(2)]

    def emit_barrier(ph, k, pre_dma_sems):
        for (sm, val) in pre_dma_sems:
            ph.wait("gpsimd", sm, val)
        fw = poll_sems[k]

        def closure_all(h, k=k, fw=fw):
            par = parity_setup(h, "gpsimd")
            with h.If_eq(par, 0):
                h.dma_start(out=flg[0:1, :], in_=flv_sb[k:k + 1, :]).then_inc(fw.h, 16)
            with h.Else():
                h.dma_start(out=flg[1:2, :], in_=flv_sb[k:k + 1, :]).then_inc(fw.h, 16)
            h.wait_ge(fw.h, 16)
            with h.register(f"cnt{k}") as cnt, h.register(f"cc{k}") as cc, \
                 h.register(f"vv{k}") as vv, h.register(f"mgk{k}") as mgk:
                h.load(mgk, flv_sb[k:k + 1, 0:1])
                h.reg_mov(cnt, 16)
                h.reg_mov(cc, 1)
                with h.While(cc):
                    h.reg_add(cnt, cnt, 16)
                    with h.If_eq(par, 0):
                        h.dma_start(out=fl_sb[:], in_=flg[1:2, :]).then_inc(fw.h, 16)
                    with h.Else():
                        h.dma_start(out=fl_sb[:], in_=flg[0:1, :]).then_inc(fw.h, 16)
                    h.wait_ge(fw.h, cnt)
                    h.load(vv, fl_sb[0:1, 0:1])
                    h.reg_alu(cc, vv, mgk, RALU.subtract)
        ph.ops["gpsimd"].append(closure_all)
        return nop_inc(ph, "gpsimd", sGc)

    def nop_inc(ph, eng, sem):
        return ph.op(eng, lambda h: h.nop(), sem=sem)

    def rsqrt_chain(ph, src_ap, out_ap, t1, t2, t3, eps_add=None):
        if eps_add is not None:
            ph.op("vector", lambda h, a=src_ap, o=t1, e=eps_add: h.tensor_scalar(o, a, e, None, ALU.add))
            ph.drain("vector")
            base = t1
        else:
            base = src_ap
        vv = nop_inc(ph, "vector", sV)
        ph.wait("scalar", sV, vv)
        sv = ph.op("scalar", lambda h, a=base, o=t2: h.activation(o, a, ACT.Sqrt), sem=sS)
        ph.wait("vector", sS, sv)
        ph.op("vector", lambda h, o=t3, a=t2: h.reciprocal(o, a))
        ph.drain("vector")
        ph.op("vector", lambda h, o=t3, a=base, b=t3: h.tensor_mul(o, a, b))
        ph.drain("vector")
        ph.op("vector", lambda h, o=t3, a=t3, b=t2: h.tensor_add(o, a, b))
        ph.drain("vector")
        ph.op("vector", lambda h, o=t3, a=t3: h.tensor_scalar(o, a, 0.5, None, ALU.mult))
        ph.drain("vector")
        ph.op("vector", lambda h, o=out_ap, a=t3: h.reciprocal(o, a))
        ph.drain("vector")


    def dbg_finish():
        pD = Phase("pDbg")
        for se, u in zip(dmaS.sems, dmaS.uses):
            if u:
                pD.wait("sync", se, 16 * u)
        for se, u in zip(dmaG.sems, dmaG.uses):
            if u:
                pD.wait("gpsimd", se, 16 * u)
        pD.emit(blk)
        st.close()
        return nc

    p0s = contextlib.ExitStack()
    wkA = sbuf("wkA", [128, D], F32, p0s)
    wkB = sbuf("wkB", [128, D], F32, p0s)
    fA = sbuf("fA", [128, D], F32, p0s)
    fB = sbuf("fB", [128, D], F32, p0s)
    stg0 = sbuf("stg0", [128, D], BF16, p0s)
    stg1 = sbuf("stg1", [128, D], BF16, p0s)
    pbc = psum("pbc", [128, 512], F32, p0s)

    p0 = Phase("p0")
    cvals = []
    for sb_t, dr in ((eyeb, eye_bfd), (eyef, eye_fd), (m_sl, msl_d), (m_su, msu_d), (m_tui, mtui_d), (m_bou, mbou_d)):
        cvals.append(dmaS.issue(p0, sb_t[:], dr[:]))
    cvals.append(dmaS.issue(p0, gnorm_sb[:, 0:KHL],
                            gnorm_c.rearrange("(a p) -> p a", p=128), slow_ok=True))
    dmaG.issue(p0, xr.ap().rearrange("p (t d) -> p t d", t=NT),
               x_in.rearrange("(t p) d -> p t d", p=128))
    fbl = dmaG.issue(p0, fbias_sb[:], fbias[None, :])
    dmaS.issue(p0, flv_sb[:], flagv[:])

    v_ones = p0.op("vector", lambda h: h.memset(onesr[:], 1.0), sem=sV)

    p0.wait("tensor", fbl[0], fbl[1])
    p0.wait("tensor", sV, v_ones)
    for nb in range(4):
        if nb >= 1:
            p0.wait("tensor", sV, sV.val)
        tb = p0.op("tensor", lambda h, nb=nb: h.matmul(
            pbc[:, 0:512], onesr[:], fbias_sb[:, ts(nb, 512)], start=True, stop=True), sem=sT)
        p0.wait("vector", sT, tb)
        p0.op("vector", lambda h, nb=nb: h.tensor_scalar(
            bias_rep[:, ts(nb, 512)], pbc[:, 0:512], 0.5, None, ALU.mult), sem=sV)

    for (s, v) in cvals:
        p0.wait("vector", s, v)
    v_tern = []
    s_cast = []
    tern_store = []
    for i in range(KD):
        src = wkA if i % 2 == 0 else wkB
        dst = stg0 if i % 2 == 0 else stg1
        ld = dmaG.issue(p0, src[:], wffn[ts(i, 128), :],
                        pre_wait=([(sV, v_tern[i - 2])] if i >= 2 else None))
        p0.wait("vector", ld[0], ld[1])
        p0.op("vector", lambda h, s_=src, i=i: h.tensor_reduce(
            stat(Q_T0, i), s_[:], AX, ALU.add, apply_absolute_value=True))
        p0.drain("vector")
        p0.op("vector", lambda h, i=i: h.tensor_scalar(
            stat(Q_T1, i), stat(Q_T0, i), 1.0 / D, EPS_G, ALU.mult, ALU.add))
        p0.drain("vector")
        p0.op("vector", lambda h, i=i: h.reciprocal(stat(Q_T2, i), stat(Q_T1, i)))
        p0.op("vector", lambda h, i=i: h.tensor_scalar(
            sgam[:, i:i + 1], stat(Q_T0, i), 0.5 / D, None, ALU.mult))
        p0.drain("vector")
        if i >= 1:
            p0.wait("vector", sS, s_cast[i - 1])
        p0.op("vector", lambda h, s_=src, i=i: h.tensor_scalar(
            fA[:], s_[:], stat(Q_T2, i), 0.5, ALU.mult, ALU.is_gt))
        p0.op("vector", lambda h, s_=src, i=i: h.tensor_scalar(
            fB[:], s_[:], stat(Q_T2, i), -0.5, ALU.mult, ALU.is_lt))
        p0.drain("vector")
        v_tern.append(p0.op("vector", lambda h: h.tensor_sub(fA[:], fA[:], fB[:]), sem=sV))
        p0.wait("scalar", sV, v_tern[i])
        if i >= 2:
            p0.wait("scalar", tern_store[i - 2][0], tern_store[i - 2][1])
        s_cast.append(p0.op("scalar", lambda h, d=dst, i=i: h.activation(
            d[:], fA[:], ACT.Copy, scale=sgam[:, i:i + 1]), sem=sS))
        tern_store.append(dmaS.issue(p0, wffn_tern[ts(i, 128), :], dst[:],
                                     pre_wait=[(sS, s_cast[i])]))

    p0.emit(blk); p0s.close()
    snap("p0_end")
    if DBG_PHASE == "p0":
        return dbg_finish()

    def emit_ffn(tag, write_out, after_tag):
        ps_ = contextlib.ExitStack()
        tbuf = sbuf(f"tbuf_{tag}", [128, D], F32, ps_)
        fbuf = sbuf(f"fbuf_{tag}", [128, D], F32, ps_)
        aibuf = ps_.enter_context(nc.sbuf_tensor(f"ai_{tag}", [128, D], mybir.dt.int32))
        xqs = sbuf(f"xqs_{tag}", [128, D], BF16, ps_)
        xqT = sbuf(f"xqT_{tag}", [128, (NT // 2) * D], BF16, ps_)
        weffh = [sbuf(f"weffh{i}_{tag}", [128, KD * 1024], BF16, ps_) for i in range(2)]
        fstb = [sbuf(f"fst{i}_{tag}", [128, 512], F32, ps_) for i in range(2)]
        pM = [psum(f"pM{i}_{tag}", [128, 512], F32, ps_) for i in range(2)]
        pX = psum(f"pX_{tag}", [128, 512], BF16, ps_)

        ph = Phase(tag)
        barrier(ph, after_tag)

        for i in range(NT):
            xa = xr[:, ts(i, D)]
            ph.op("vector", lambda h, a=xa, i=i: h.tensor_reduce(stat(Q_SUMX, i), a, AX, ALU.add))
            ph.op("vector", lambda h, a=xa, i=i: h.tensor_reduce(stat(Q_MX, i), a, AX, ALU.max))
            ph.op("vector", lambda h, a=xa, i=i: h.tensor_reduce(
                stat(Q_MN, i), a, AX, ALU.min) if hasattr(h, "tensor_reduce") else None)
            ph.op("scalar", lambda h, a=xa, i=i: h.activation(
                fbuf[:], a, ACT.Square, accum_out=stat(Q_SUMSQ, i)))
            ph.drain("scalar")
        ph.drain("vector")
        v_st = nop_inc(ph, "vector", sV)
        s_st = nop_inc(ph, "scalar", sS)

        ph.wait("vector", sS, s_st)
        ph.op("vector", lambda h: h.tensor_scalar(stat(Q_M), stat(Q_SUMX), 1.0 / D, None, ALU.mult))
        ph.drain("vector")
        ph.op("vector", lambda h: h.tensor_mul(stat(Q_T0), stat(Q_M), stat(Q_M)))
        ph.op("vector", lambda h: h.tensor_scalar(stat(Q_T1), stat(Q_SUMSQ), 1.0 / D, None, ALU.mult))
        ph.drain("vector")
        ph.op("vector", lambda h: h.tensor_sub(stat(Q_V), stat(Q_T1), stat(Q_T0)))
        ph.drain("vector")
        ph.op("vector", lambda h: h.tensor_scalar(stat(Q_T0), stat(Q_V), 1.0 + EPS_RMS,
                                                  EPS_RMS * EPS_LN, ALU.mult, ALU.add))
        ph.drain("vector")
        rsqrt_chain(ph, stat(Q_T0), stat(Q_SC), stat(Q_T1), stat(Q_T2), stat(Q_T3))
        ph.op("vector", lambda h: h.tensor_sub(stat(Q_T0), stat(Q_MX), stat(Q_M)))
        ph.op("vector", lambda h: h.tensor_sub(stat(Q_T1), stat(Q_M), stat(Q_MN)))
        ph.drain("vector")
        ph.op("vector", lambda h: h.tensor_max(stat(Q_T0), stat(Q_T0), stat(Q_T1)))
        ph.drain("vector")
        ph.op("vector", lambda h: h.tensor_mul(stat(Q_T0), stat(Q_T0), stat(Q_SC)))
        ph.drain("vector")
        ph.op("vector", lambda h: h.tensor_scalar(stat(Q_XS), stat(Q_T0), 1.0 / QB, None, ALU.mult))
        ph.drain("vector")
        ph.op("vector", lambda h: h.tensor_scalar(stat(Q_T1), stat(Q_XS), EPS_X, None, ALU.add))
        ph.drain("vector")
        ph.op("vector", lambda h: h.reciprocal(stat(Q_T2), stat(Q_T1)))
        ph.drain("vector")
        ph.op("vector", lambda h: h.tensor_mul(stat(Q_KQ), stat(Q_SC), stat(Q_T2)))
        ph.drain("vector")
        ph.op("vector", lambda h: h.tensor_mul(stat(Q_KB), stat(Q_M), stat(Q_KQ)))
        ph.drain("vector")
        ph.op("vector", lambda h: h.tensor_scalar(stat(Q_KB), stat(Q_KB), -1.0, 256.0, ALU.mult, ALU.add))
        ph.drain("vector")
        v_m2 = nop_inc(ph, "vector", sV)
        ph.wait("scalar", sV, v_m2)
        ph.wait("gpsimd", sV, v_m2)

        ev_last = [None]
        mm_by_buf = {0: None, 1: None}
        ohstep = [0]
        ep_hist = []
        st_hist = {0: None, 1: None}
        prev_vq = [None]
        prev_tl = [None]
        for th in range(2):
            gb = []
            s_q = []
            v_q = []
            for ii in range(NT // 2):
                i = th * (NT // 2) + ii
                xa = xr[:, ts(i, D)]
                if prev_vq[0] is not None:
                    ph.wait("scalar", sV, prev_vq[0])
                sq = ph.op("scalar", lambda h, a=xa, i=i: h.activation(
                    tbuf[:], a, ACT.Identity, scale=stat(Q_KQ, i), bias=stat(Q_KB, i)), sem=sS)
                s_q.append(sq)
                ph.wait("vector", sS, sq)
                ph.op("vector", lambda h: h.tensor_copy(aibuf[:], tbuf[:]), sem=sV)
                ph.drain("vector")
                ph.op("vector", lambda h: h.tensor_copy(fbuf[:], aibuf[:]), sem=sV)
                ph.drain("vector")
                ph.op("vector", lambda h: h.scalar_tensor_tensor(
                    tbuf[:], tbuf[:], -0.5, fbuf[:], ALU.add, ALU.is_ge), sem=sV)
                ph.drain("vector")
                if prev_tl[0] is not None:
                    ph.wait("vector", sT, prev_tl[0])
                vq = ph.op("vector", lambda h: h.scalar_tensor_tensor(
                    xqs[:], fbuf[:], -256.0, tbuf[:], ALU.add, ALU.add), sem=sV)
                v_q.append(vq)
                prev_vq[0] = vq
                ph.wait("tensor", sV, vq)
                for kb in range(4):
                    if ev_last[0] is not None:
                        ph.wait("tensor", sV, ev_last[0])
                    for k in range(4):
                        kt = kb * 4 + k
                        t_last = ph.op("tensor", lambda h, k=k, kt=kt: h.transpose(
                            pX[:, ts(k, 128)], xqs[:, ts(kt, 128)], eyeb[:]), sem=sT)
                    prev_tl[0] = t_last
                    ph.wait("vector", sT, t_last)
                    dst = xqT.ap().rearrange("p (k t f) -> p k t f", k=KD, t=NT // 2)[:, ts(kb, 4), ii, :]
                    ev_last[0] = ph.op("vector", lambda h, d=dst: h.tensor_copy(
                        d, pX.ap().rearrange("p (k f) -> p k f", k=4)), sem=sV)
            for ii in range(NT // 2):
                i = th * (NT // 2) + ii
                xa = xr[:, ts(i, D)]
                ph.wait("vector", sS, s_q[ii])
                gb.append(ph.op("vector", lambda h, a=xa: h.tensor_add(
                    a, a, bias_rep[:]), sem=sV))
                ph.drain("vector")

            for oh in range(2):
                buf = ohstep[0] % 2
                ohstep[0] += 1
                wl = []
                for kt in range(KD):
                    pw = []
                    if th == 0 and oh == 0 and kt < len(tern_store):
                        pw.append((tern_store[kt][0], tern_store[kt][1]))
                    if mm_by_buf[buf] is not None and kt == 0:
                        pw.append((sT, mm_by_buf[buf]))
                    wl.append(dmaS.issue(
                        ph, weffh[buf].ap().rearrange("p (k f) -> p k f", k=KD)[:, kt, :],
                        wffn_tern[oh * 1024:(oh + 1) * 1024, ts(kt, 128)],
                        transpose=True, pre_wait=pw or None))
                for (s, v) in wl:
                    ph.wait("tensor", s, v)
                ph.wait("tensor", sV, ev_last[0])
                for ii in range(NT // 2):
                    i = th * (NT // 2) + ii
                    for nb in range(2):
                        pidx = (2 * ii + nb) % 2
                        if len(ep_hist) >= 2:
                            ph.wait("tensor", sV, ep_hist[-2])
                        for kt in range(KD):
                            lhs = xqT.ap().rearrange("p (k t f) -> p k t f", k=KD, t=NT // 2)[:, kt, ii, :]
                            mmv = ph.op("tensor", lambda h, lhs=lhs, kt=kt, nb=nb, pidx=pidx, buf=buf: h.matmul(
                                pM[pidx][:], lhs,
                                weffh[buf].ap().rearrange("p (k f) -> p k f", k=KD)[:, kt, ts(nb, 512)],
                                start=(kt == 0), stop=(kt == KD - 1)), sem=sT)
                        ph.wait("vector", sT, mmv)
                        col0 = i * D + oh * 1024 + nb * 512
                        dst = xr[:, col0: col0 + 512]
                        if write_out:
                            if st_hist[pidx] is not None:
                                ph.wait("vector", st_hist[pidx][0], st_hist[pidx][1])
                            ev = ph.op("vector", lambda h, d=dst, i=i, pidx=pidx: h.scalar_tensor_tensor(
                                fstb[pidx][:], pM[pidx][:], stat(Q_XS, i), d, ALU.mult, ALU.add), sem=sV)
                            st_hist[pidx] = dmaS.issue(
                                ph, x_out[ts(i, 128), oh * 1024 + nb * 512: oh * 1024 + (nb + 1) * 512],
                                fstb[pidx][:], pre_wait=[(sV, ev)])
                        else:
                            ev = ph.op("vector", lambda h, d=dst, i=i, pidx=pidx: h.scalar_tensor_tensor(
                                d, pM[pidx][:], stat(Q_XS, i), d, ALU.mult, ALU.add), sem=sV)
                        ep_hist.append(ev)
                mm_by_buf[buf] = sT.val
        ph.emit(blk); ps_.close()

    emit_ffn("p1", False, "p0_end")
    snap("p1_end")
    if DBG_PHASE == "p1":
        return dbg_finish()

    p2s = contextlib.ExitStack()
    sq2 = sbuf("sq2", [128, D], F32, p2s)
    xn2 = [sbuf(f"xn2_{i}", [128, D], BF16, p2s) for i in range(2)]
    p2 = Phase("p2")
    barrier(p2, "p1_end")
    for i in range(NT):
        xa = xr[:, ts(i, D)]
        p2.op("vector", lambda h, a=xa, i=i: h.tensor_reduce(stat(Q_SUMX, i), a, AX, ALU.add))
        p2.op("scalar", lambda h, a=xa, i=i: h.activation(sq2[:], a, ACT.Square, accum_out=stat(Q_SUMSQ, i)))
        p2.drain("scalar")
    p2.drain("vector")
    v_st2 = nop_inc(p2, "vector", sV)
    s_st2 = nop_inc(p2, "scalar", sS)
    p2.wait("vector", sS, s_st2)
    p2.op("vector", lambda h: h.tensor_scalar(stat(Q_M), stat(Q_SUMX), 1.0 / D, None, ALU.mult))
    p2.drain("vector")
    p2.op("vector", lambda h: h.tensor_mul(stat(Q_T0), stat(Q_M), stat(Q_M)))
    p2.op("vector", lambda h: h.tensor_scalar(stat(Q_T1), stat(Q_SUMSQ), 1.0 / D, None, ALU.mult))
    p2.drain("vector")
    p2.op("vector", lambda h: h.tensor_sub(stat(Q_V), stat(Q_T1), stat(Q_T0)))
    p2.drain("vector")
    rsqrt_chain(p2, stat(Q_V), stat(Q_SC), stat(Q_T1), stat(Q_T2), stat(Q_T3), eps_add=EPS_LN)
    p2.op("vector", lambda h: h.tensor_mul(stat(Q_KB), stat(Q_M), stat(Q_SC)))
    p2.drain("vector")
    p2.op("vector", lambda h: h.tensor_scalar(stat(Q_KB), stat(Q_KB), -1.0, None, ALU.mult))
    p2.drain("vector")
    v_m3 = nop_inc(p2, "vector", sV)
    p2.wait("scalar", sV, v_m3)
    ag_st = []
    for i in range(NT):
        xa = xr[:, ts(i, D)]
        if i >= 2:
            p2.wait("scalar", ag_st[i - 2][0], ag_st[i - 2][1])
        sc = p2.op("scalar", lambda h, a=xa, i=i: h.activation(
            xn2[i % 2][:], a, ACT.Identity, scale=stat(Q_SC, i), bias=stat(Q_KB, i)), sem=sS)
        k = dmaS.i % len(dmaS.sems)
        dmaS.i += 1
        sring = dmaS.sems[k]
        if dmaS.uses[k] > 0:
            p2.wait("sync", sring, 16 * dmaS.uses[k])
        dmaS.uses[k] += 1
        p2.wait("sync", sS, sc)
        op_pair(p2, "sync",
                lambda h, i=i, b=xn2[i % 2]: h.dma_start(out=ag_out[ts(i, 128), :], in_=b[:]),
                lambda h, i=i, b=xn2[i % 2]: h.dma_start(out=ag_out[T + i * 128: T + (i + 1) * 128, :], in_=b[:]),
                sem=sring, n=16)
        ag_st.append((sring, sring.val))
    emit_barrier(p2, 0, ag_st)
    p2.emit(blk); p2s.close()
    snap("p2_end")
    if DBG_PHASE == "p2":
        return dbg_finish()

    p3s = contextlib.ExitStack()
    xnT = sbuf("xnT", [128, KD * 16 * 128], BF16, p3s)
    xg = [sbuf(f"xg{i}", [128, D], BF16, p3s) for i in range(2)]
    wst = [sbuf(f"wst{i}", [128, KD * 128], BF16, p3s) for i in range(2)]
    wg2s = sbuf("wg2s", [128, HL], BF16, p3s)
    g1T = sbuf("g1T", [128, N], BF16, p3s)
    beta_f = sbuf("beta_f", [NHL, N], F32, p3s)
    stq = [sbuf(f"stq{i}", [128, 512], BF16, p3s) for i in range(2)]
    pP = [psum(f"pP{i}", [128, 512], F32, p3s) for i in range(2)]
    pXt = psum("pXt", [128, 512], BF16, p3s)
    pBt = psum("pBt", [128, 128], F32, p3s)

    p3 = Phase("p3")
    barrier(p3, "p2_end")

    ldg2 = dmaG.issue(p3, wg2s[:], wg2_t[:])

    def xnT_blk(kt, n0, n1):
        return xnT.ap().rearrange("p (k f) -> p k f", k=KD)[:, kt, n0:n1]

    wsrc = [("q", wq_t, 0), ("k", wk_t, 1), ("v", wv_t, 2)]
    ev3 = [None]
    xg_tr_done = {0: None, 1: None}
    mm3_done = {}
    wst_free = {0: None, 1: None}
    pP_free = {0: None, 1: None}
    stq_free = {0: None, 1: None}

    for sh in range(2):
        for mt in range(16):
            gmt = sh * 16 + mt
            pw = []
            if mt == 0 and sh == 1:
                v = mm3_done.get((sh - 1, "last"))
                if v:
                    pw.append((sT, v))
            if xg_tr_done[mt % 2] is not None:
                pw.append((sT, xg_tr_done[mt % 2]))
            ld = dmaS.issue(p3, xg[mt % 2][:], ag_out[ts(gmt, 128), :], pre_wait=pw or None)
            p3.wait("tensor", ld[0], ld[1])
            for kb in range(4):
                if ev3[0] is not None:
                    p3.wait("tensor", sV, ev3[0])
                for k in range(4):
                    kt = kb * 4 + k
                    tv = p3.op("tensor", lambda h, k=k, kt=kt, mt=mt: h.transpose(
                        pXt[:, ts(k, 128)], xg[mt % 2][:, ts(kt, 128)], eyeb[:]), sem=sT)
                p3.wait("vector", sT, tv)
                dst = xnT.ap().rearrange("p (k t f) -> p k t f", k=KD, t=16)[:, ts(kb, 4), mt, :]
                ev3[0] = p3.op("vector", lambda h, d=dst: h.tensor_copy(
                    d, pXt.ap().rearrange("p (k f) -> p k f", k=4)), sem=sV)
            xg_tr_done[mt % 2] = tv

        for wi, (wname, wdr, widx) in enumerate(wsrc):
            for mh in range(NHL):
                mb = wi * NHL + mh
                pwv = wst_free[mb % 2]
                ld = dmaG.issue(p3, wst[mb % 2].ap().rearrange("p (k f) -> p k f", k=KD),
                                wdr.rearrange("(k p) f -> p k f", p=128)[:, :, ts(mh, 128)],
                                pre_wait=([(sT, pwv)] if pwv else None))
                p3.wait("tensor", ld[0], ld[1])
                if mb == 0:
                    p3.wait("tensor", sV, ev3[0])
                for nb in range(4):
                    par = nb % 2
                    if pP_free[par] is not None:
                        p3.wait("tensor", pP_free[par][0], pP_free[par][1])
                    for kt in range(KD):
                        mmv = p3.op("tensor", lambda h, kt=kt, nb=nb, mb=mb, par=par: h.matmul(
                            pP[par][:], wst[mb % 2][:, ts(kt, 128)],
                            xnT_blk(kt, nb * 512, (nb + 1) * 512),
                            start=(kt == 0), stop=(kt == KD - 1)), sem=sT)
                    eng, sem = ("scalar", sS) if wname in ("q", "k") else ("vector", sV)
                    p3.wait(eng, sT, mmv)
                    if stq_free[par] is not None:
                        p3.wait(eng, stq_free[par][0], stq_free[par][1])
                    if wname in ("q", "k"):
                        evv = p3.op(eng, lambda h, par=par: h.activation(
                            stq[par][:], pP[par][:], ACT.Silu), sem=sem)
                    else:
                        evv = p3.op(eng, lambda h, par=par: h.tensor_copy(stq[par][:], pP[par][:]), sem=sem)
                    pP_free[par] = (sem, evv)
                    c0 = sh * 16 + nb * 4
                    stv = dmaS.issue(p3, qkv_scan[c0:c0 + 4, :, mh, widx, :].rearrange("c p f -> p c f"),
                                     stq[par].ap().rearrange("p (c f) -> p c f", c=4),
                                     pre_wait=[(sem, evv)])
                    stq_free[par] = stv
                mm3_done[(sh, mb)] = sT.val
                wst_free[mb % 2] = sT.val
        mm3_done[(sh, "last")] = sT.val

        ldb = dmaG.issue(p3, wst[0].ap().rearrange("p (k f) -> p k f", k=KD)[:, :, 0:NHL],
                         wbeta_t.rearrange("(k p) f -> p k f", p=128),
                         pre_wait=[(sT, wst_free[0])])
        p3.wait("tensor", ldb[0], ldb[1])
        for nb in range(4):
            par = nb % 2
            if pP_free[par] is not None:
                p3.wait("tensor", pP_free[par][0], pP_free[par][1])
            for kt in range(KD):
                mmv = p3.op("tensor", lambda h, kt=kt, nb=nb, par=par: h.matmul(
                    pP[par][0:NHL, :], wst[0][:, kt * 128: kt * 128 + NHL],
                    xnT_blk(kt, nb * 512, (nb + 1) * 512),
                    start=(kt == 0), stop=(kt == KD - 1)), sem=sT)
            p3.wait("scalar", sT, mmv)
            scv = p3.op("scalar", lambda h, nb=nb, sh=sh, par=par: h.activation(
                beta_f[:, sh * T + nb * 512: sh * T + (nb + 1) * 512],
                pP[par][0:NHL, :], ACT.Sigmoid), sem=sS)
            pP_free[par] = (sS, scv)
        wst_free[0] = sT.val

        ldg1 = dmaG.issue(p3, wst[1].ap().rearrange("p (k f) -> p k f", k=KD),
                          wg1_t.rearrange("(k p) f -> p k f", p=128),
                          pre_wait=[(sT, wst_free[1])])
        p3.wait("tensor", ldg1[0], ldg1[1])
        for nb in range(4):
            par = nb % 2
            if pP_free[par] is not None:
                p3.wait("tensor", pP_free[par][0], pP_free[par][1])
            for kt in range(KD):
                mmv = p3.op("tensor", lambda h, kt=kt, nb=nb, par=par: h.matmul(
                    pP[par][:], wst[1][:, ts(kt, 128)],
                    xnT_blk(kt, nb * 512, (nb + 1) * 512),
                    start=(kt == 0), stop=(kt == KD - 1)), sem=sT)
            p3.wait("vector", sT, mmv)
            gv = p3.op("vector", lambda h, nb=nb, sh=sh, par=par: h.tensor_copy(
                g1T[:, sh * T + nb * 512: sh * T + (nb + 1) * 512], pP[par][:]), sem=sV)
            pP_free[par] = (sV, gv)
        wst_free[1] = sT.val

    p3.drain("scalar")
    s_bt = nop_inc(p3, "scalar", sS)
    p3.wait("vector", sS, s_bt)
    p3.op("vector", lambda h: h.tensor_scalar(beta_f[:], beta_f[:], 2.0, None, ALU.mult))
    p3.drain("vector")
    v_bt = nop_inc(p3, "vector", sV)
    p3.wait("tensor", sV, v_bt)
    for c in range(NTF):
        tv = p3.op("tensor", lambda h, c=c: h.transpose(
            pBt[:, 0:NHL], beta_f[:, ts(c, 128)], eyef[0:NHL, 0:NHL]), sem=sT)
        p3.wait("vector", sT, tv)
        bv = p3.op("vector", lambda h, c=c: h.tensor_copy(
            beta_tm[:, c * NHL:(c + 1) * NHL], pBt[:, 0:NHL]), sem=sV)
        p3.wait("tensor", sV, bv)

    p3.wait("tensor", ldg2[0], ldg2[1])
    gate_ev = {0: None, 1: None}
    for c in range(NTF):
        for nb in range(2):
            par = (2 * c + nb) % 2
            if gate_ev[par] is not None:
                p3.wait("tensor", gate_ev[par][0], gate_ev[par][1])
            mmv = p3.op("tensor", lambda h, c=c, nb=nb, par=par: h.matmul(
                pP[par][:], g1T[:, ts(c, 128)], wg2s[:, ts(nb, 512)],
                start=True, stop=True), sem=sT)
            p3.wait("scalar", sT, mmv)
            if stq_free[par] is not None:
                p3.wait("scalar", stq_free[par][0], stq_free[par][1])
            scv = p3.op("scalar", lambda h, par=par: h.activation(
                stq[par][:], pP[par][:], ACT.Sigmoid), sem=sS)
            stv = dmaS.issue(p3, gate_dram[c, :, ts(nb, 512)], stq[par][:],
                             pre_wait=[(sS, scv)])
            stq_free[par] = stv
            gate_ev[par] = (sS, scv)
    p3.emit(blk); p3s.close()
    snap("p3_end")
    if DBG_PHASE == "p3":
        return dbg_finish()

    p4s = contextlib.ExitStack()
    woT = sbuf("woT", [128, KHL * D], BF16, p4s)
    wof = [sbuf(f"wof{i}", [128, D], F32, p4s) for i in range(2)]
    S32 = sbuf("S32", [128, NHL * 128], F32, p4s)
    S_bf = sbuf("S_bf", [128, NHL * 128], BF16, p4s)
    qkvb = [sbuf(f"qkvb{i}", [128, NHL * 3 * 128], BF16, p4s) for i in range(2)]
    gtb = [sbuf(f"gtb{i}", [128, HL], BF16, p4s) for i in range(2)]
    scr = sbuf("scr", [128, 42 * 128], BF16, p4s)
    scw = sbuf("scw", [128, NHL * 2 * 128], BF16, p4s)
    gsb = sbuf("gsb", [128, NHL * 128], BF16, p4s)
    scf = sbuf("scf", [128, 128], F32, p4s)
    scf2 = sbuf("scf2", [128, 128], F32, p4s)
    ogb = sbuf("ogb", [128, HL], F32, p4s)
    ogsq = sbuf("ogsq", [128, HL], F32, p4s)
    ogn_s = sbuf("ogn_s", [128, HL], BF16, p4s)
    ognT = [sbuf(f"ognT{i}", [128, NHL * 128], BF16, p4s) for i in range(2)]
    sc8 = sbuf("sc8", [128, 8 * 8], F32, p4s)
    fst4 = [sbuf(f"fst4{i}", [128, 512], F32, p4s) for i in range(2)]

    bankA = [psum(f"pbA{i}", [128, 512], F32, p4s) for i in range(2)]
    bankB = [psum(f"pbB{i}", [128, 512], F32, p4s) for i in range(2)]
    pT4l = [psum(f"pT4{i}", [128, 256], BF16, p4s) for i in range(2)]
    pCmb = [bankA[0][:, 0:384], bankA[1][:, 0:384]]
    pWo = [psum(f"pWo{i}", [128, 512], F32, p4s) for i in range(2)]

    SLOT_NAMES = ["Nd", "Dbn", "P0", "P1", "P2", "P3", "Bd0", "Bd1", "Bd2", "Bd3", "Bd4",
                  "Bo", "LT", "ta", "tb", "v0", "Uf", "Khat", "R", "TG2", "GNs"]

    def slot(par, name):
        i = par * len(SLOT_NAMES) + SLOT_NAMES.index(name)
        return scr[:, i * 128:(i + 1) * 128]

    def s8(q, j=None):
        if j is None:
            return sc8[:, q * 8:(q + 1) * 8]
        return sc8[:, q * 8 + j: q * 8 + j + 1]

    p4 = Phase("p4")
    barrier(p4, "p3_end")

    s_wo = []
    for r in range(KHL):
        ld = dmaS.issue(p4, wof[r % 2][:], wo_t[ts(r, 128), :],
                        pre_wait=([(sS, s_wo[r - 2])] if r >= 2 else None))
        p4.wait("scalar", ld[0], ld[1])
        s_wo.append(p4.op("scalar", lambda h, r=r: h.activation(
            woT[:, ts(r, D)], wof[r % 2][:], ACT.Copy, scale=gnorm_sb[:, r:r + 1]), sem=sS))
    p4.op("vector", lambda h: h.memset(S32[:], 0.0), sem=sV)
    v_z2 = p4.op("vector", lambda h: h.memset(S_bf[:], 0.0), sem=sV)

    rs_stores = []
    last_sbf_v = {bh: (sV, v_z2) for bh in range(NHL)}
    prev_pa_ev = None
    qkv_free = []
    gate_free = []
    wo_done = {0: None, 1: None}
    pWo_free = {0: None, 1: None}
    fst4_free = {0: None, 1: None}
    ognT_ev = [None]
    mix_guard = {0: (sV, v_z2), 1: (sV, v_z2)}
    pt4_guard = {0: None, 1: None}
    u_guard = {0: None, 1: None}

    og_tail_s = [None]

    def emit_og_tail(c, og_done):
        p4.wait("scalar", sV, og_done)
        for bh in range(NHL):
            s_sq = p4.op("scalar", lambda h, bh=bh: h.activation(
                ogsq[:, ts(bh, 128)], ogb[:, ts(bh, 128)], ACT.Square,
                accum_out=s8(4, bh)), sem=sS)
        og_tail_s[0] = s_sq
        p4.wait("vector", sS, s_sq)
        p4.op("vector", lambda h: h.tensor_scalar(s8(4), s8(4), 1.0 / HD, None, ALU.mult))
        p4.drain("vector")
        rsqrt_chain(p4, s8(4), s8(5), s8(4), s8(6), s8(7), eps_add=EPS_OG)
        iv = s8(5)
        inv_b = bass.AP(iv.tensor, iv.offset, [iv.ap[0], list(iv.ap[1])[:1] + [NHL] if False else [1, NHL], [0, 128]])
        v_ogn = p4.op("vector", lambda h, ib=inv_b: h.tensor_mul(
            ogn_s.ap().rearrange("p (a f) -> p a f", a=NHL),
            ogb.ap().rearrange("p (a f) -> p a f", a=NHL), ib), sem=sV)
        p4.wait("tensor", sV, v_ogn)
        for _par in (0, 1):
            if pt4_guard[_par] is not None:
                p4.wait("tensor", sV, pt4_guard[_par])
        if wo_done[c % 2] is not None:
            p4.wait("vector", sT, wo_done[c % 2])
        for hb2 in range(4):
            if ognT_ev[0] is not None:
                p4.wait("tensor", sV, ognT_ev[0])
            p4.op("tensor", lambda h, hb2=hb2: h.transpose(
                pT4l[0][:, 0:128], ogn_s[:, ts(2 * hb2, 128)], eyeb[:]), sem=sT)
            tv2 = p4.op("tensor", lambda h, hb2=hb2: h.transpose(
                pT4l[1][:, 0:128], ogn_s[:, ts(2 * hb2 + 1, 128)], eyeb[:]), sem=sT)
            p4.wait("vector", sT, tv2)
            p4.op("vector", lambda h, hb2=hb2, c=c: h.tensor_copy(
                ognT[c % 2][:, hb2 * 256: hb2 * 256 + 128], pT4l[0][:, 0:128]), sem=sV)
            ognT_ev[0] = p4.op("vector", lambda h, hb2=hb2, c=c: h.tensor_copy(
                ognT[c % 2][:, hb2 * 256 + 128:(hb2 + 1) * 256], pT4l[1][:, 0:128]), sem=sV)
        p4.wait("tensor", sV, ognT_ev[0])
        pt4_guard[0] = ognT_ev[0]
        pt4_guard[1] = ognT_ev[0]
        for nb in range(4):
            par = nb % 2
            if pWo_free[par] is not None:
                p4.wait("tensor", pWo_free[par][0], pWo_free[par][1])
            for kt in range(KHL):
                t_wo = p4.op("tensor", lambda h, nb=nb, kt=kt, c=c, par=par: h.matmul(
                    pWo[par][:], ognT[c % 2][:, ts(kt, 128)],
                    woT[:, kt * D + nb * 512: kt * D + (nb + 1) * 512],
                    start=(kt == 0), stop=(kt == KHL - 1)), sem=sT)
            eng, sem = ("scalar", sS) if par == 0 else ("vector", sV)
            p4.wait(eng, sT, t_wo)
            if fst4_free[par] is not None:
                p4.wait(eng, fst4_free[par][0], fst4_free[par][1])
            if eng == "scalar":
                e_wo = p4.op(eng, lambda h, par=par: h.copy(fst4[par][:], pWo[par][:]), sem=sem)
            else:
                e_wo = p4.op(eng, lambda h, par=par: h.tensor_copy(fst4[par][:], pWo[par][:]), sem=sem)
            pWo_free[par] = (sem, e_wo)
            k_ = dmaS.i % len(dmaS.sems)
            dmaS.i += 1
            sring = dmaS.sems[k_]
            if dmaS.uses[k_] > 0:
                p4.wait("sync", sring, 16 * dmaS.uses[k_])
            dmaS.uses[k_] += 1
            p4.wait("sync", sem, e_wo)
            op_pair(p4, "sync",
                    lambda h, c=c, nb=nb, par=par: h.dma_start(
                        out=rs_sh[0, ts(c, 128), ts(nb, 512)], in_=fst4[par][:]),
                    lambda h, c=c, nb=nb, par=par: h.dma_start(
                        out=rs_sh[1, ts(c, 128), ts(nb, 512)], in_=fst4[par][:]),
                    sem=sring, n=16)
            stv = (sring, sring.val)
            fst4_free[par] = stv
            rs_stores.append(stv)
        wo_done[c % 2] = sT.val

    pending_og = [None]

    for c in range(DBG_CHUNKS if DBG_CHUNKS else NTF):
        pwq = [(sT, qkv_free[c - 2])] if c >= 2 else None
        ldq = dmaG.issue(p4, qkvb[c % 2][:],
                         qkv_scan[c].rearrange("p b t f -> p (b t f)"), pre_wait=pwq)
        pwg = [(sV, gate_free[c - 2])] if c >= 2 else None
        ldg = dmaG.issue(p4, gtb[c % 2][:], gate_dram[c, :, :], pre_wait=pwg)

        def kqv(bh, w, c=c):
            return qkvb[c % 2][:, (bh * 3 + w) * 128: (bh * 3 + w + 1) * 128]

        pa_ev = []
        p4.wait("tensor", ldq[0], ldq[1])
        for bh in (range(NHL) if DBG_SCANPART >= 1 else []):
            par = bh % 2
            p4.wait("tensor", last_sbf_v[bh][0], last_sbf_v[bh][1])
            p4.wait("tensor", mix_guard[par][0], mix_guard[par][1])
            guard = pa_ev[-2] if len(pa_ev) >= 2 else (prev_pa_ev[bh - 2] if prev_pa_ev else None)
            if guard:
                p4.wait("tensor", sV, guard[0])
                p4.wait("tensor", sS, guard[1])
            KT = kqv(bh, 1)
            QT = kqv(bh, 0)
            p4.op("tensor", lambda h, KT=KT, par=par: h.matmul(
                pCmb[par][:, 0:128], KT, KT, start=True, stop=True), sem=sT)
            p4.op("tensor", lambda h, KT=KT, par=par, bh=bh: h.matmul(
                pCmb[par][:, 128:256], KT, S_bf[:, ts(bh, 128)], start=True, stop=True), sem=sT)
            tA = p4.op("tensor", lambda h, KT=KT, QT=QT, par=par: h.matmul(
                pCmb[par][:, 256:384], KT, QT, start=True, stop=True), sem=sT)
            p4.wait("vector", sT, tA)
            if DBG_PA >= 2:
                p4.op("vector", lambda h, bh=bh, par=par: h.tensor_copy(
                    gsb[:, ts(bh, 128)], pCmb[par][:, 0:128]), sem=sV)
                p4.op("vector", lambda h, par=par: h.tensor_mul(scf[:], pCmb[par][:, 0:128], eyef[:]), sem=sV)
                p4.drain("vector")
                vA = p4.op("vector", lambda h, bh=bh: h.tensor_reduce(
                    s8(0, bh), scf[:], AX, ALU.add), sem=sV)
                p4.drain("vector")
            else:
                vA = nop_inc(p4, "vector", sV)
            p4.wait("scalar", sT, tA)
            p4.wait("scalar", sV, vA)
            if DBG_PA >= 3:
                p4.op("scalar", lambda h, bh=bh, par=par: h.copy(
                    scw[:, ts(2 * bh, 128)], pCmb[par][:, 128:256]), sem=sS)
                sA = p4.op("scalar", lambda h, bh=bh, par=par: h.copy(
                    scw[:, ts(2 * bh + 1, 128)], pCmb[par][:, 256:384]), sem=sS)
            else:
                sA = nop_inc(p4, "scalar", sS)
            pa_ev.append((vA, sA))
        prev_pa_ev = pa_ev
        if pending_og[0] is not None:
            emit_og_tail(*pending_og[0])
            pending_og[0] = None

        if DBG_SCANPART < 2:
            qkv_free.append(sT.val)
            gate_free.append(sV.val)
            continue
        p4.drain("vector")
        rsqrt_chain(p4, s8(0), s8(1), s8(4), s8(5), s8(6), eps_add=EPS_K)
        bsl = beta_tm[:, c * NHL:(c + 1) * NHL]
        p4.op("vector", lambda h, b=bsl: h.tensor_mul(s8(2), s8(1), b))
        p4.drain("vector")
        p4.op("vector", lambda h: h.tensor_scalar(s8(3), s8(2), -1.0, None, ALU.mult))
        p4.drain("vector")
        nop_inc(p4, "vector", sV)

        def pass_b_gen(bh):
            par = bh % 2
            n_ap = s8(1, bh)
            bn_ap = s8(2, bh)
            nbn_ap = s8(3, bh)
            bcol = beta_tm[:, c * NHL + bh: c * NHL + bh + 1]
            KT = kqv(bh, 1)
            QT = kqv(bh, 0)
            VT = kqv(bh, 2)
            scfl = scf if par == 0 else scf2
            mxGN = bankB[par][:, 256:384]
            mxB0 = bankB[par][:, 384:512]
            mxO = mxGN
            mxDS = mxB0
            sqP = bankB[par][:, 0:128]
            sqB = bankB[par][:, 128:256]
            apP = bankA[par][:, 384:512]
            t4V = pT4l[par][:, 0:128]
            t4K = pT4l[par][:, 128:256]

            p4.op("vector", lambda h, par=par, n=n_ap: h.tensor_scalar(
                slot(par, "Nd"), eyef[:], n, None, ALU.mult), sem=sV)
            v_nd = p4.op("vector", lambda h, par=par, b=bn_ap: h.tensor_scalar(
                slot(par, "Dbn"), eyef[:], b, None, ALU.mult), sem=sV)
            p4.wait("tensor", mix_guard[par][0], mix_guard[par][1])
            p4.wait("tensor", sV, v_nd)
            t_gn = p4.op("tensor", lambda h, bh=bh, par=par, o=mxGN: h.matmul(
                o, gsb[:, ts(bh, 128)], slot(par, "Nd"), start=True, stop=True), sem=sT)
            yield
            p4.wait("vector", sT, t_gn)
            v_gns = p4.op("vector", lambda h, par=par, i_=mxGN: h.tensor_copy(slot(par, "GNs"), i_), sem=sV)
            v_p0 = p4.op("vector", lambda h, par=par, nb=nbn_ap, i_=mxGN: h.scalar_tensor_tensor(
                slot(par, "P0"), i_, nb, m_sl[:], ALU.mult, ALU.mult), sem=sV)
            yield
            p4.wait("tensor", sV, v_p0)
            t_b0 = p4.op("tensor", lambda h, par=par, o=mxB0: h.matmul(
                o, slot(par, "GNs"), slot(par, "Dbn"), start=True, stop=True), sem=sT)
            yield
            p4.wait("vector", sT, t_b0)
            p4.op("vector", lambda h, par=par, i_=mxB0: h.scalar_tensor_tensor(
                slot(par, "Bd0"), i_, -1.0, m_su[:], ALU.mult, ALU.mult), sem=sV)
            v_b0 = p4.op("vector", lambda h, par=par, i_=mxB0: h.scalar_tensor_tensor(
                slot(par, "Bo"), i_, -1.0, m_bou[:], ALU.mult, ALU.mult), sem=sV)
            yield
            if pt4_guard[par] is not None:
                p4.wait("tensor", sV, pt4_guard[par])
            p4.op("tensor", lambda h, a=VT, o=t4V: h.transpose(o, a, eyeb[:]), sem=sT)
            t_tr2 = p4.op("tensor", lambda h, a=KT, o=t4K: h.transpose(o, a, eyeb[:]), sem=sT)
            yield
            p4.wait("vector", sT, t_tr2)
            p4.op("vector", lambda h, b=bcol, i_=t4V, o=scfl: h.tensor_scalar(
                o[:], i_, b, None, ALU.mult), sem=sV)
            v_kh = p4.op("vector", lambda h, par=par, n=n_ap, i_=t4K: h.tensor_scalar(
                slot(par, "Khat"), i_, n, None, ALU.mult), sem=sV)
            pt4_guard[par] = v_kh
            p4.drain("vector")
            v_r = p4.op("vector", lambda h, par=par, bh=bh, nb=nbn_ap, o=scfl: h.scalar_tensor_tensor(
                slot(par, "R"), scw[:, ts(2 * bh, 128)], nb, o[:], ALU.mult, ALU.add), sem=sV)
            v_tg2 = p4.op("vector", lambda h, par=par, bh=bh, n=n_ap: h.scalar_tensor_tensor(
                slot(par, "TG2"), scw[:, ts(2 * bh + 1, 128)], n, m_tui[:], ALU.mult, ALU.mult), sem=sV)
            yield
            p4.wait("tensor", sV, v_b0)
            s_sq = None
            for j in range(1, 5):
                if s_sq is not None:
                    p4.wait("tensor", sS, s_sq)
                if j <= 3:
                    p4.op("tensor", lambda h, par=par, j=j, o=sqP: h.matmul(
                        o, slot(par, f"Bd{j-1}"), slot(par, f"P{j-1}"),
                        start=True, stop=True), sem=sT)
                t_sq = p4.op("tensor", lambda h, par=par, j=j, o=sqB: h.matmul(
                    o, slot(par, f"P{j-1}"), slot(par, f"Bd{j-1}"),
                    start=True, stop=True), sem=sT)
                p4.wait("scalar", sT, t_sq)
                if j <= 3:
                    p4.op("scalar", lambda h, par=par, j=j, i_=sqP: h.copy(
                        slot(par, f"P{j}"), i_), sem=sS)
                s_sq = p4.op("scalar", lambda h, par=par, j=j, i_=sqB: h.copy(
                    slot(par, f"Bd{j}"), i_), sem=sS)
                yield
            p4.wait("vector", sS, s_sq)
            v_lt = p4.op("vector", lambda h, par=par: h.tensor_add(
                slot(par, "LT"), eyef[:], slot(par, "Bd4")), sem=sV)
            for j in range(3, -1, -1):
                p4.wait("tensor", sV, v_lt)
                t_lt = p4.op("tensor", lambda h, par=par, j=j, o=apP: h.matmul(
                    o, slot(par, f"P{j}"), slot(par, "LT"), start=True, stop=True), sem=sT)
                p4.wait("vector", sT, t_lt)
                v_lt = p4.op("vector", lambda h, par=par, i_=apP: h.scalar_tensor_tensor(
                    slot(par, "LT"), i_, 1.0, slot(par, "LT"), ALU.bypass, ALU.add), sem=sV)
                yield
            p4.wait("tensor", sV, v_lt)
            p4.wait("tensor", sV, v_r)
            t_v0 = p4.op("tensor", lambda h, par=par, o=apP: h.matmul(
                o, slot(par, "LT"), slot(par, "R"), start=True, stop=True), sem=sT)
            p4.wait("scalar", sT, t_v0)
            s_v0 = p4.op("scalar", lambda h, par=par, i_=apP: h.copy(slot(par, "v0"), i_), sem=sS)
            yield
            hcur = "v0"
            ev_h = (sS, s_v0)
            for stage in range(3):
                p4.wait("tensor", ev_h[0], ev_h[1])
                t_z = p4.op("tensor", lambda h, par=par, hcur=hcur, o=apP: h.matmul(
                    o, slot(par, "Bo"), slot(par, hcur), start=True, stop=True), sem=sT)
                p4.wait("scalar", sT, t_z)
                s_z = p4.op("scalar", lambda h, par=par, i_=apP: h.copy(slot(par, "tb"), i_), sem=sS)
                yield
                p4.wait("tensor", sS, s_z)
                t_w = p4.op("tensor", lambda h, par=par, o=apP: h.matmul(
                    o, slot(par, "LT"), slot(par, "tb"), start=True, stop=True), sem=sT)
                dstn = "ta" if stage < 2 else "Uf"
                if dstn == "Uf" and u_guard[par] is not None:
                    p4.wait("vector", sT, u_guard[par])
                p4.wait("vector", sT, t_w)
                v_h = p4.op("vector", lambda h, par=par, dstn=dstn, i_=apP: h.scalar_tensor_tensor(
                    slot(par, dstn), i_, 1.0, slot(par, "v0"), ALU.bypass, ALU.add), sem=sV)
                ev_h = (sV, v_h)
                hcur = dstn
                yield
            U = hcur
            p4.wait("tensor", ev_h[0], ev_h[1])
            p4.wait("tensor", sV, v_tg2)
            p4.op("tensor", lambda h, QT=QT, bh=bh, o=mxO: h.matmul(
                o, QT, S_bf[:, ts(bh, 128)], start=True, stop=False), sem=sT)
            t_o = p4.op("tensor", lambda h, par=par, U=U, o=mxO: h.matmul(
                o, slot(par, "TG2"), slot(par, U), start=False, stop=True), sem=sT)
            t_ds = p4.op("tensor", lambda h, par=par, U=U, bh=bh, o=mxDS: h.matmul(
                o, slot(par, "Khat"), slot(par, U), start=True, stop=True), sem=sT)
            u_guard[par] = t_ds
            p4.wait("vector", sT, t_o)
            p4.wait("vector", ldg[0], ldg[1])
            if og_tail_s[0] is not None:
                p4.wait("vector", sS, og_tail_s[0])
            p4.op("vector", lambda h, bh=bh, c=c, i_=mxO: h.scalar_tensor_tensor(
                ogb[:, ts(bh, 128)], i_, 1.0, gtb[c % 2][:, ts(bh, 128)],
                ALU.bypass, ALU.mult), sem=sV)
            p4.wait("vector", sT, t_ds)
            v_s32 = p4.op("vector", lambda h, bh=bh, i_=mxDS: h.tensor_add(
                S32[:, ts(bh, 128)], S32[:, ts(bh, 128)], i_), sem=sV)
            p4.wait("scalar", sV, v_s32)
            last_sbf_v[bh] = (sS, p4.op("scalar", lambda h, bh=bh: h.copy(
                S_bf[:, ts(bh, 128)], S32[:, ts(bh, 128)]), sem=sS))
            mix_guard[par] = last_sbf_v[bh]

        if DBG_SCANPART >= 3:
            for bh0 in range(0, NHL, 2):
                gens = [pass_b_gen(bh0), pass_b_gen(bh0 + 1)]
                live = [True, True]
                while any(live):
                    for gi in range(2):
                        if live[gi]:
                            try:
                                next(gens[gi])
                            except StopIteration:
                                live[gi] = False
        qkv_free.append(sT.val)

        if DBG_SCANPART < 6:
            gate_free.append(sV.val)
            continue
        gate_free.append(sV.val)
        pending_og[0] = (c, sV.val)

    if pending_og[0] is not None:
        emit_og_tail(*pending_og[0])
        pending_og[0] = None
    if not DBG_CHUNKS:
        emit_barrier(p4, 1, [(s, 16 * u) for s, u in zip(dmaS.sems, dmaS.uses) if u])
    p4.emit(blk); p4s.close()
    snap("p4_end")

    if DBG_CHUNKS:
        pD = Phase("pD")
        for se, u in zip(dmaS.sems, dmaS.uses):
            if u:
                pD.wait("sync", se, 16 * u)
        for se, u in zip(dmaG.sems, dmaG.uses):
            if u:
                pD.wait("gpsimd", se, 16 * u)
        pD.emit(blk)
        st.close()
        return nc

    p5s = contextlib.ExitStack()
    rsb = [sbuf(f"rsb{i}", [128, D], F32, p5s) for i in range(4)]
    p5 = Phase("p5")
    barrier(p5, "p4_end")
    v_add = []
    for i in range(NT):
        for sl_ in range(2):
            bi = (2 * i + sl_) % 4
            k_ = dmaS.i % len(dmaS.sems)
            dmaS.i += 1
            sring = dmaS.sems[k_]
            if dmaS.uses[k_] > 0:
                p5.wait("sync", sring, 16 * dmaS.uses[k_])
            dmaS.uses[k_] += 1
            idx = 2 * i + sl_
            if idx >= 4:
                p5.wait("sync", sV, v_add[idx - 4])
            op_pair(p5, "sync",
                    lambda h, i=i, sl_=sl_, bi=bi: h.dma_start(
                        out=rsb[bi][:], in_=rs_sh[sl_, ts(i, 128), :]),
                    lambda h, i=i, sl_=sl_, bi=bi: h.dma_start(
                        out=rsb[bi][:], in_=rs_sh[sl_, T + i * 128: T + (i + 1) * 128, :]),
                    sem=sring, n=16)
            p5.wait("vector", sring, sring.val)
        xa = xr[:, ts(i, D)]
        v_add.append(p5.op("vector", lambda h, a=xa, bi0=(2 * i) % 4: h.tensor_add(
            a, a, rsb[bi0][:]), sem=sV))
        p5.drain("vector")
        v_add.append(p5.op("vector", lambda h, a=xa, bi1=(2 * i + 1) % 4: h.tensor_add(
            a, a, rsb[bi1][:]), sem=sV))
        p5.drain("vector")
    p5.emit(blk); p5s.close()
    snap("p5_end")

    emit_ffn("p6", True, "p5_end")
    snap("p6_end")

    pFin = Phase("pF")
    for s, u in zip(dmaS.sems, dmaS.uses):
        if u:
            pFin.wait("sync", s, 16 * u)
    for s, u in zip(dmaG.sems, dmaG.uses):
        if u:
            pFin.wait("gpsimd", s, 16 * u)
    pFin.emit(blk)

    st.close()
    return nc


_NC_CACHE = None


def kernel(**inputs):
    global _NC_CACHE
    x = np.ascontiguousarray(np.asarray(inputs["L0_attn_x"], dtype=np.float32))
    Wq = np.asarray(inputs["L0_attn_Wq"], dtype=np.float32)
    Wk = np.asarray(inputs["L0_attn_Wk"], dtype=np.float32)
    Wv = np.asarray(inputs["L0_attn_Wv"], dtype=np.float32)
    Wbeta = np.asarray(inputs["L0_attn_Wbeta"], dtype=np.float32)
    Wg1 = np.asarray(inputs["L0_attn_Wg1"], dtype=np.float32)
    Wg2 = np.asarray(inputs["L0_attn_Wg2"], dtype=np.float32)
    g_norm = np.asarray(inputs["L0_attn_g_norm"], dtype=np.float32)
    Wo = np.asarray(inputs["L0_attn_Wo"], dtype=np.float32)
    Wffn = np.asarray(inputs["L0_ffn_weight"], dtype=np.float32)
    fb = np.asarray(inputs["L0_ffn_bias"], dtype=np.float32)

    wq_T = np.ascontiguousarray(Wq.T)
    wk_T = np.ascontiguousarray(Wk.T)
    wv_T = np.ascontiguousarray(Wv.T)
    wb_T = np.ascontiguousarray(Wbeta.T)
    wg1_T = np.ascontiguousarray(Wg1.T)
    wg2_T = np.ascontiguousarray(Wg2.T)
    wo_T = np.ascontiguousarray(Wo.T)

    eye_bf = np.eye(128, dtype=ml_dtypes.bfloat16)
    eye_f = np.eye(128, dtype=np.float32)
    idx = np.arange(128)
    same_blk = (idx[:, None] // 32) == (idx[None, :] // 32)
    msl = ((idx[:, None] > idx[None, :]) & same_blk).astype(np.float32)
    msu = ((idx[:, None] < idx[None, :]) & same_blk).astype(np.float32)
    mbou = ((idx[:, None] < idx[None, :]) & ~same_blk).astype(np.float32)
    mtui = (idx[:, None] <= idx[None, :]).astype(np.float32)

    if _NC_CACHE is None:
        _NC_CACHE = build_program()
    nc = _NC_CACHE

    nonce = int(np.random.randint(1, 2**30))
    flagv = np.empty((2, 128), dtype=np.int32)
    flagv[0, :] = nonce + 1
    flagv[1, :] = nonce + 2

    in_maps = []
    for c in range(8):
        b, h = c // 2, c % 2
        hs = slice(h * HL, (h + 1) * HL)
        in_maps.append({
            "x_in": np.ascontiguousarray(x[b, h * T:(h + 1) * T, :]),
            "wq_t": np.ascontiguousarray(wq_T[:, hs]),
            "wk_t": np.ascontiguousarray(wk_T[:, hs]),
            "wv_t": np.ascontiguousarray(wv_T[:, hs]),
            "wbeta_t": np.ascontiguousarray(wb_T[:, h * NHL:(h + 1) * NHL]),
            "wg1_t": wg1_T,
            "wg2_t": np.ascontiguousarray(wg2_T[:, hs]),
            "wo_t": np.ascontiguousarray(wo_T[hs, :]),
            "gnorm_c": np.ascontiguousarray(g_norm[hs]),
            "wffn": Wffn,
            "fbias": fb,
            "eye_bfd": eye_bf,
            "eye_fd": eye_f,
            "msl_d": msl,
            "msu_d": msu,
            "mtui_d": mtui,
            "mbou_d": mbou,
            "flagv": flagv,
        })

    res = run_bass_kernel_spmd(nc, in_maps, core_ids=list(range(8)))
    out = np.empty((B, N, D), dtype=np.float32)
    for c in range(8):
        b, h = c // 2, c % 2
        out[b, h * T:(h + 1) * T, :] = res.results[c]["x_out"]
    return out
